# version 05
# baseline (speedup 1.0000x reference)
import math
import numpy as np
import ml_dtypes
import concourse.bass as bass
import concourse.mybir as mybir
from concourse.bass_utils import run_bass_kernel_spmd

F32 = mybir.dt.float32
BF16 = mybir.dt.bfloat16
I32 = mybir.dt.int32
U8 = mybir.dt.uint8
AF = mybir.ActivationFunctionType
ALU = mybir.AluOpType
AX = mybir.AxisListType

NCORES = 8
T = 2048
C = 1024
NH = 16
HS = 64
NT = T // 128
NC8 = C // 128
NB = 127
NEG = -1.0e30

COMPUTE_SEMS = dict(tensor="T", vector="V", scalar="A", gpsimd="G", sync="S")
DMA_SEMS = dict(sync="DS", gpsimd="DG", scalar="DA")
ALL_SEMS = ["T", "V", "A", "G", "S", "DS", "DG", "DA"]


class EM:
    def __init__(self, nc, sems):
        self.nc = nc
        self.sems = sems
        self.cnt = {k: 0 for k in ALL_SEMS}
        self.ops = {e: [] for e in COMPUTE_SEMS}
        self.wm = {e: {k: 0 for k in ALL_SEMS} for e in COMPUTE_SEMS}
        self.n_instr = 0

    def _emit(self, eng, fn, waits, sem_name, inc, multi):
        w = {}
        for t in waits:
            if t is None:
                continue
            s, c = t
            if c > w.get(s, 0):
                w[s] = c
        own = COMPUTE_SEMS[eng]
        wme = self.wm[eng]
        wl = []
        for s, c in sorted(w.items()):
            if s == own or (self.DEDUP and c <= wme[s]):
                continue
            wl.append((s, c))
            wme[s] = max(wme[s], c)
        sem = self.sems[sem_name]

        def run(engine, wl=wl, fn=fn, sem=sem, inc=inc, multi=multi):
            for s, c in wl:
                engine.wait_ge(self.sems[s], c)
            ins = fn(engine)
            if multi:
                engine.nop().then_inc(sem, inc)
            else:
                ins.then_inc(sem, inc)

        self.ops[eng].append(run)
        self.n_instr += 1 + len(wl) + (1 if multi else 0)
        self.cnt[sem_name] += inc
        return (sem_name, self.cnt[sem_name])

    FENCE = "drain"
    FENCE_ENGINES = ("vector", "scalar", "gpsimd")
    DEDUP = True

    def op(self, eng, fn, waits=(), multi=False):
        if self.FENCE == "drain" and eng in self.FENCE_ENGINES:
            fn2 = (lambda e, fn=fn: (fn(e), e.drain())[1])
            return self._emit(eng, fn2, waits, COMPUTE_SEMS[eng], 1, False)
        return self._emit(eng, fn, waits, COMPUTE_SEMS[eng], 1, multi)

    def dma(self, queue, fn, waits=()):
        sem = DMA_SEMS[queue]
        waits = list(waits) + [(sem, self.cnt[sem])]
        return self._emit(queue, fn, waits, sem, 16, False)

    def join(self, exclude=("DS", "DG", "DA")):
        return [(s, c) for s, c in self.cnt.items() if c > 0 and s not in exclude]

    def all_tok(self):
        return [(s, c) for s, c in self.cnt.items() if c > 0]

    def build(self, block):
        for name, fns in self.ops.items():
            if not fns:
                continue

            def body(eng, fns=fns):
                for f in fns:
                    f(eng)

            getattr(block, name)(body)


class Arena:
    def __init__(self, nc, guard_handle, size):
        self.nc = nc
        self.base = nc.lookup_mloc(guard_handle).addr
        self.size = size
        self.cursor = 0
        self.peak = 0
        self.marks = []
        self.n = 0

    def alloc(self, name, shape, dtype):
        per_part = int(np.prod(shape[1:])) * mybir.dt.size(dtype)
        at = self.cursor
        self.cursor = (at + per_part + 31) & ~31
        self.peak = max(self.peak, self.cursor)
        assert self.cursor <= self.size, f"SBUF arena overflow at {name}: {self.cursor}"
        h = self.nc.virtual_sbuf_tensor(f"{name}_{self.n}", list(shape), dtype)
        self.n += 1
        self.nc._tensor(h.name, h.shape, h.dtype, type="SB")
        mloc = self.nc.lookup_mloc(h)
        mloc.allocated = True
        mloc.addr = self.base + at
        return h

    def mark(self):
        self.marks.append(self.cursor)

    def release(self):
        self.cursor = self.marks.pop()

    def alias(self, name, shape, dtype, src):
        per_part = int(np.prod(shape[1:])) * mybir.dt.size(dtype)
        sm = self.nc.lookup_mloc(src)
        h = self.nc.virtual_sbuf_tensor(f"{name}_{self.n}", list(shape), dtype)
        self.n += 1
        self.nc._tensor(h.name, h.shape, h.dtype, type="SB")
        mloc = self.nc.lookup_mloc(h)
        mloc.allocated = True
        mloc.addr = sm.addr
        return h


ID_O = 0
ONES_O = 128
MASK_O = 256
MASKT_O = 384
PM_O = 512
F512_O = 513
LN127_O = 514
LN25_O = 515
COS_O = 544
SIN_O = COS_O + T
OH16_O = SIN_O + T
CW = OH16_O + T
E64_O = 0
ONESB_O = 64
IDB_O = 192
FOLD_O = 320
OH32_O = 384
CWB = OH32_O + 4096
CFS = 544
CBS = 384


def host_consts():
    cf = np.zeros((128, CW), dtype=np.float32)
    cf[:, ID_O:ID_O + 128] = np.eye(128, dtype=np.float32)
    cf[:, ONES_O:ONES_O + 128] = 1.0
    ii = np.arange(128)
    cf[:, MASK_O:MASK_O + 128] = np.where(ii[None, :] <= ii[:, None], 0.0, NEG)
    cf[:, MASKT_O:MASKT_O + 128] = np.where(ii[None, :] >= ii[:, None], 0.0, NEG)
    cf[0, PM_O] = 1.0
    cf[1, PM_O] = -1.0
    cf[:, F512_O] = 512.0
    cf[:, LN127_O] = np.float32(np.log(127.0))
    cf[:, LN25_O] = np.float32(np.log(2.5))
    dhe = 32
    invf = 1.0 / (10000.0 ** (np.arange(0, dhe, 2, dtype=np.float64) / dhe))
    fr = np.arange(T, dtype=np.float64)[None, :] * invf[:, None]
    cos32 = np.concatenate([np.cos(fr), np.cos(fr)], 0).astype(np.float32)
    sin32 = np.concatenate([np.sin(fr), np.sin(fr)], 0).astype(np.float32)
    cf[0:32, COS_O:COS_O + T] = 1.0
    cf[32:64, COS_O:COS_O + T] = cos32
    cf[96:128, COS_O:COS_O + T] = sin32
    for j in range(16):
        cf[j, OH16_O + 128 * j:OH16_O + 128 * j + 128] = 1.0
    cb = np.zeros((128, CWB), dtype=np.float32)
    cb[64:128, E64_O:E64_O + 64] = np.eye(64, dtype=np.float32)
    cb[:, IDB_O:IDB_O + 128] = np.eye(128, dtype=np.float32)
    cb[0:64, FOLD_O:FOLD_O + 64] = np.eye(64, dtype=np.float32)
    cb[64:128, FOLD_O:FOLD_O + 64] = np.eye(64, dtype=np.float32)
    for e in range(32):
        cb[e, OH32_O + 128 * e:OH32_O + 128 * e + 128] = 1.0
    cb[:, ONESB_O:ONESB_O + 128] = 1.0
    return cf, cb.astype(ml_dtypes.bfloat16)


def _pvec(v, p=128):
    v = np.asarray(v)
    return np.ascontiguousarray(v.reshape(-1, p).T)


def host_inputs(core, A):
    d = {
        "x": np.ascontiguousarray(A["L0_attn_x"][core].T),
        "cattn_w": A["L0_attn_c_attn_w"],
        "cattn_b": _pvec(A["L0_attn_c_attn_b"]),
        "cproj_w": A["L0_attn_c_proj_w"],
        "cproj_b": _pvec(A["L0_attn_c_proj_b"]),
        "conv_w": np.ascontiguousarray(
            A["L0_attn_conv_w"].transpose(1, 2, 0).reshape(64, 32 * 64)),
        "conv_b": A["L0_attn_conv_b"].reshape(64, 1),
        "compln_w": A["L0_attn_comp_ln_w"].reshape(64, 1),
        "compln_b": A["L0_attn_comp_ln_b"].reshape(64, 1),
        "lnloc_w": _pvec(A["L0_attn_ln_local_w"]),
        "lnloc_b": _pvec(A["L0_attn_ln_local_b"]),
        "gate_w1": A["L0_attn_gate_w1"],
        "gate_b1": _pvec(A["L0_attn_gate_b1"]),
        "gateln_w": _pvec(A["L0_attn_gate_ln_w"]),
        "gateln_b": _pvec(A["L0_attn_gate_ln_b"]),
        "gate_w2": np.ascontiguousarray(
            A["L0_attn_gate_w2"].reshape(8, 128, 2).transpose(1, 0, 2).reshape(128, 16)),
        "gate_b2": A["L0_attn_gate_b2"].reshape(2, 1),
        "ffn_w1t": np.ascontiguousarray(A["L0_ffn_w1"].T),
        "ffn_convw": np.ascontiguousarray(
            A["L0_ffn_conv_w"].reshape(4096, 3).reshape(32, 128, 3)
            .transpose(1, 0, 2).reshape(128, 96)),
        "ffn_convb": _pvec(A["L0_ffn_conv_b"]),
        "ffn_w2t": np.ascontiguousarray(A["L0_ffn_w2"].T),
        "wq_down": A["L1_attn_Wq_down"],
        "wk_down": A["L1_attn_Wk_down"],
        "wv_down": A["L1_attn_Wv_down"],
        "wq_upc": A["L1_attn_Wq_up_c"],
        "wq_upe": A["L1_attn_Wq_up_e"],
        "wk_upc": A["L1_attn_Wk_up_c"],
        "wk_upe": A["L1_attn_Wk_up_e"],
        "wv_up": A["L1_attn_Wv_up"],
        "wc": A["L1_attn_Wc"],
        "rw_t": np.ascontiguousarray(A["L1_ffn_router_weight"].T),
        "r_bias": A["L1_ffn_router_bias"].reshape(1, 32),
        "moe_gate": np.ascontiguousarray(A["L1_ffn_gate_w"].transpose(0, 2, 1)),
        "moe_up": np.ascontiguousarray(A["L1_ffn_up_w"].transpose(0, 2, 1)),
        "moe_down": np.ascontiguousarray(A["L1_ffn_down_w"].transpose(0, 2, 1)),
        "sh_gate": np.ascontiguousarray(A["L1_ffn_shared_gate_w"].T),
        "sh_up": np.ascontiguousarray(A["L1_ffn_shared_up_w"].T),
        "sh_down": np.ascontiguousarray(A["L1_ffn_shared_down_w"].T),
    }
    cf, cb = host_consts()
    d["constf"] = cf
    d["constb"] = cb
    return {k: np.ascontiguousarray(v) for k, v in d.items()}


INPUT_SPECS = {
    "x": ((C, T), F32), "cattn_w": ((C, 3 * C), F32), "cattn_b": ((128, 24), F32),
    "cproj_w": ((C, C), F32), "cproj_b": ((128, 8), F32),
    "conv_w": ((64, 2048), F32), "conv_b": ((64, 1), F32),
    "compln_w": ((64, 1), F32), "compln_b": ((64, 1), F32),
    "lnloc_w": ((128, 8), F32), "lnloc_b": ((128, 8), F32),
    "gate_w1": ((2 * C, C), F32), "gate_b1": ((128, 8), F32),
    "gateln_w": ((128, 8), F32), "gateln_b": ((128, 8), F32),
    "gate_w2": ((128, 16), F32), "gate_b2": ((2, 1), F32),
    "ffn_w1t": ((C, 4096), F32), "ffn_convw": ((128, 96), F32),
    "ffn_convb": ((128, 32), F32), "ffn_w2t": ((4096, C), F32),
    "wq_down": ((C, C), F32), "wk_down": ((C, C), F32), "wv_down": ((C, C), F32),
    "wq_upc": ((64, 32), F32), "wq_upe": ((64, 32), F32),
    "wk_upc": ((64, 32), F32), "wk_upe": ((64, 32), F32), "wv_up": ((64, 64), F32),
    "wc": ((C, C), F32),
    "rw_t": ((C, 32), F32), "r_bias": ((1, 32), F32),
    "moe_gate": ((32, C, 512), F32), "moe_up": ((32, C, 512), F32),
    "moe_down": ((32, 512, C), F32),
    "sh_gate": ((C, C), F32), "sh_up": ((C, C), F32), "sh_down": ((C, C), F32),
    "constf": ((128, CW), F32), "constb": ((128, CWB), BF16),
}


class P:
    pass


def mkprog(stage, dbg):
    import contextlib
    p = P()
    nc = bass.Bass("TRN2", target_bir_lowering=False, detect_race_conditions=False)
    p.nc = nc
    p.stage = stage
    dram = {}
    for name, (shape, dt) in INPUT_SPECS.items():
        dram[name] = nc.dram_tensor(name, list(shape), dt, kind="ExternalInput")
    dram["y"] = nc.dram_tensor("y", [C, T], F32, kind="ExternalOutput")
    if dbg is not None:
        dram["dbg"] = nc.dram_tensor("dbg", list(dbg), F32, kind="ExternalOutput")
    dram["spill_x"] = nc.dram_tensor("spill_x", [C, T], F32, kind="Internal")
    dram["spill_x2"] = nc.dram_tensor("spill_x2", [C, T], F32, kind="Internal")
    dram["t1_scr"] = nc.dram_tensor("t1_scr", [C, 4096], BF16, kind="Internal")
    dram["t2_scr"] = nc.dram_tensor("t2_scr", [4096, C], BF16, kind="Internal")
    dram["y_scr"] = nc.dram_tensor("y_scr", [C, T], BF16, kind="Internal")
    p.dram = dram
    p.stack = contextlib.ExitStack()
    ARENA = 211200
    guard = p.stack.enter_context(nc.sbuf_tensor("arena", [128, ARENA], U8))
    p.PA = p.stack.enter_context(nc.psum_tensor("PA", [128, 2048], F32))
    p.PB = p.stack.enter_context(nc.psum_tensor("PB", [128, 2048], F32))
    p.sems = {s: p.stack.enter_context(nc.semaphore(f"sem_{s}")) for s in ALL_SEMS}
    p.block = p.stack.enter_context(nc.Block())
    p.ar = Arena(nc, guard, ARENA)
    p.em = EM(nc, p.sems)
    em = p.em

    p.cf = p.ar.alloc("cf", [128, CFS], F32)
    p.cb = p.ar.alloc("cb", [128, CBS], BF16)
    c1 = em.dma("sync", lambda e: e.dma_start(out=p.cf[:], in_=dram["constf"][:, 0:CFS]))
    c2 = em.dma("sync", lambda e: e.dma_start(out=p.cb[:], in_=dram["constb"][:, 0:CBS]))
    p.ctok = [c1, c2]

    p.x_bf = p.ar.alloc("x_bf", [128, NC8, T], BF16)
    xt = em.dma("gpsimd", lambda e: e.dma_start(
        out=p.x_bf[:],
        in_=dram["x"][:, :].rearrange("(c p) t -> p c t", p=128)))
    p.xtok = [xt]
    p.pv = {}
    for nm in ("cattn_b", "cproj_b", "conv_b", "compln_w", "compln_b", "lnloc_w",
               "lnloc_b", "gate_b1", "gateln_w", "gateln_b", "gate_w2", "gate_b2",
               "ffn_convb", "ffn_convw", "r_bias"):
        shape, dt = INPUT_SPECS[nm]
        t = p.ar.alloc("pv_" + nm, list(shape), dt)
        tok = em.dma("sync", lambda e, t=t, nm=nm, shape=shape: e.dma_start(
            out=t[0:shape[0], :], in_=dram[nm][:, :]))
        p.pv[nm] = t
        p.ctok.append(tok)

    p.scr = p.ar.alloc("scr", [128, 2048], F32)
    p.scr2 = p.ar.alloc("scr2", [128, 1024], F32)
    p.scrb = p.ar.alloc("scrb", [128, 2048], BF16)
    p.stat = p.ar.alloc("stat", [128, 64], F32)
    p.msks = p.ar.alloc("msks", [128, 128], F32)
    p.strip0 = p.ar.alloc("strip0", [1, 512], F32)
    p.strip1 = p.ar.alloc("strip1", [1, 512], F32)
    p.strip2 = p.ar.alloc("strip2", [1, 512], F32)
    p.l2 = p.ar.alloc("l2", [2, 16], F32)
    p.vbias = [p.ar.alloc(f"vbias{i}", [1, 128], F32) for i in range(2)]
    p.bank_rd = {}
    p.x_hbm = dram["x"]
    return p


def ident(p):
    return p.cf[:, ID_O:ID_O + 128]


def ones(p, r, c):
    return p.cf[0:r, ONES_O:ONES_O + c]


def onesb(p, r, c):
    return p.cb[0:r, ONESB_O:ONESB_O + c]


def bank(p, i, rows=128, cols=512, coloff=0, rowoff=0):
    t = p.PA if i < 4 else p.PB
    return t[rowoff:rowoff + rows, 512 * (i % 4) + coloff:512 * (i % 4) + coloff + cols]


def use_bank(p, i, waits):
    t = p.bank_rd.get(i)
    return waits + ([t] if t else [])


def read_bank(p, i, tok):
    p.bank_rd[i] = tok


def rsqrt_strip(p, dst, src, epsb, deps, n):
    em = p.em
    t0 = em.op("vector", lambda e: e.tensor_scalar(dst, src, epsb, None,
                                                   op0=ALU.add), waits=deps)
    t1 = em.op("scalar", lambda e: e.activation(dst, dst, AF.Ln), waits=[t0])
    return em.op("scalar", lambda e: e.activation(dst, dst, AF.Exp, scale=-0.5), waits=[t1])


def ln_fm(p, xin, out_f32, out_b16, deps, gamma=None, beta=None, nfeat=1024,
          sb=(0, 1, 2, 3), bf_in=False, blocks=None):
    em = p.em
    outs = []
    inv = 1.0 / nfeat
    if blocks is None:
        blocks = [(slice(512 * tc, 512 * tc + 512),) * 3 for tc in range(4)]
    for sl, slo32, slo16 in blocks:
        stoks = []
        sqt = None
        mm_s = None
        for c in range(NC8):
            sqt = em.op("scalar", lambda e, c=c, sl=sl: e.activation(
                p.scr[:, 0:512] if c % 2 == 0 else p.scr[:, 512:1024],
                xin(c)[:, sl], AF.Square), waits=deps + ([mm_s] if mm_s else []))
            mm_s = em.op("tensor", lambda e, c=c: e.matmul(
                bank(p, sb[1], 1, 512), ones(p, 128, 1),
                p.scr[:, 0:512] if c % 2 == 0 else p.scr[:, 512:1024],
                start=(c == 0), stop=(c == NC8 - 1), skip_group_check=True),
                waits=use_bank(p, sb[1], [sqt]) if c == 0 else [sqt])
            o1 = onesb(p, 128, 1) if bf_in else ones(p, 128, 1)
            stoks.append(em.op("tensor", lambda e, c=c, o1=o1, sl=sl: e.matmul(
                bank(p, sb[0], 1, 512), o1, xin(c)[:, sl],
                start=(c == 0), stop=(c == NC8 - 1), skip_group_check=True),
                waits=use_bank(p, sb[0], deps) if c == 0 else deps))
        stoks.append(mm_s)
        sm = p.strip0[0:1, 0:512]
        sv = p.strip1[0:1, 0:512]
        t1 = em.op("vector", lambda e: e.tensor_scalar(
            sm, bank(p, sb[0], 1, 512), inv, None, op0=ALU.mult), waits=stoks)
        read_bank(p, sb[0], t1)
        t2 = em.op("vector", lambda e: e.tensor_tensor(
            p.strip2[0:1, 0:512], sm, sm, op=ALU.mult), waits=[t1])
        t3 = em.op("vector", lambda e: e.scalar_tensor_tensor(
            sv, bank(p, sb[1], 1, 512), inv, p.strip2[0:1, 0:512],
            op0=ALU.mult, op1=ALU.subtract), waits=[t2] + stoks)
        read_bank(p, sb[1], t3)
        t4 = rsqrt_strip(p, sv, sv, 1e-5, [t3], 512)
        bm = em.op("tensor", lambda e: e.matmul(
            bank(p, sb[2]), ones(p, 1, 128), sm, start=True, stop=True),
            waits=use_bank(p, sb[2], [t1]))
        br = em.op("tensor", lambda e: e.matmul(
            bank(p, sb[3]), ones(p, 1, 128), sv, start=True, stop=True),
            waits=use_bank(p, sb[3], [t4]))
        for c in range(NC8):
            ta = em.op("vector", lambda e, c=c, sl=sl: e.tensor_tensor(
                p.scr2[:, 0:512], xin(c)[:, sl], bank(p, sb[2]),
                op=ALU.subtract), waits=[bm] + deps)
            if gamma is None:
                dst32 = out_f32(c) if out_f32 else None
                if dst32 is not None:
                    tb = em.op("vector", lambda e, c=c, dst32=dst32, slo32=slo32: e.tensor_tensor(
                        dst32[:, slo32], p.scr2[:, 0:512], bank(p, sb[3]),
                        op=ALU.mult), waits=[ta, br])
                    if out_b16:
                        tb = em.op("vector", lambda e, c=c, dst32=dst32, slo32=slo32,
                                   slo16=slo16: e.tensor_copy(
                            out_b16(c)[:, slo16], dst32[:, slo32]), waits=[tb])
                else:
                    tb = em.op("vector", lambda e, c=c, slo16=slo16: e.tensor_tensor(
                        out_b16(c)[:, slo16], p.scr2[:, 0:512], bank(p, sb[3]),
                        op=ALU.mult), waits=[ta, br])
            else:
                tb = em.op("vector", lambda e, c=c: e.tensor_tensor(
                    p.scr2[:, 512:1024], p.scr2[:, 0:512], bank(p, sb[3]),
                    op=ALU.mult), waits=[ta, br])
                slo = slo16 if out_b16 else slo32
                tb = em.op("vector", lambda e, c=c, slo=slo: e.tensor_scalar(
                    (out_b16(c) if out_b16 else out_f32(c))[:, slo],
                    p.scr2[:, 512:1024], gamma[:, c:c + 1], beta[:, c:c + 1],
                    op0=ALU.mult, op1=ALU.add), waits=[tb])
            outs.append(tb)
        read_bank(p, sb[2], outs[-1])
        read_bank(p, sb[3], outs[-1])
    return outs


def tap(p, name, ap_flat, deps):
    if getattr(p, "tap", None) != name or p.tap_done:
        return deps
    p.tap_done = True
    t = p.em.dma("gpsimd", lambda e: e.dma_start(
        out=p.dram["dbg"][:, :], in_=ap_flat), waits=list(deps))
    return deps + [t]


def stream_w_bf16(p, dst, src_ap, deps=()):
    return p.em.dma("gpsimd", lambda e: e.dma_start(out=dst, in_=src_ap),
                    waits=list(deps))


def attn_head(p, h, Qaug, Kaug, Vtm, out_chunk_ap, wdeps, scale=0.125,
              comp=None):
    em = p.em
    par = h % 2
    rowoff = 64 * par
    vhl = p.stat[:, 32:34]
    mx = p.stat[:, 34:40]
    dn = p.stat[:, 40:46]
    m1 = p.stat[:, 46:47]
    bm = p.stat[:, 47:48]
    den = p.stat[:, 48:49]
    lv = p.stat[:, 49:50]
    ot = None
    ph1_done = []
    for Tt in range(NT):
        if comp is None:
            kl = 128 * (Tt + 1)
            nch = (kl + 511) // 512
            chs = [(i * 512, min(512, kl - i * 512)) for i in range(nch)]
        else:
            chs = [(0, NB)]
        mmt = []
        for ci, (k0, kn) in enumerate(chs):
            src = (lambda e, k0=k0, kn=kn, Tt=Tt, ci=ci: e.matmul(
                bank(p, ci, 128, kn), Qaug[0:64, 128 * Tt:128 * Tt + 128],
                (Kaug[0:64, k0:k0 + kn] if comp is None else
                 comp["kc"][0:64, 0:NB]), start=True, stop=True))
            mmt.append(em.op("tensor", src,
                             waits=use_bank(p, ci, wdeps)))
        mt = None
        dc0 = 0
        mw = 128
        if comp is None:
            k0l, knl = chs[-1]
            dc0 = knl - 128
            mt = em.op("vector", lambda e, dc0=dc0, nch=len(chs): e.tensor_tensor(
                p.msks[:, 0:128], bank(p, nch - 1, 128, 128, coloff=dc0),
                p.cf[:, MASK_O:MASK_O + 128], op=ALU.add), waits=[mmt[-1]])
        elif Tt == 0:
            mw = NB
            mt = em.op("vector", lambda e: e.tensor_tensor(
                p.msks[:, 0:NB], bank(p, 0, 128, NB),
                p.cf[:, MASK_O:MASK_O + NB], op=ALU.add), waits=[mmt[-1]])
        if getattr(p, "tap", None) == "h0scores" and h == 0 and comp is None and Tt == 0 and not p.tap_done:
            p.tap_done = True
            c1 = em.op("vector", lambda e: e.tensor_copy(
                p.scr2[:, 0:128], bank(p, 0, 128, 128)), waits=[mmt[0]])
            tdma = em.dma("gpsimd", lambda e: e.dma_start(
                out=p.dram["dbg"][:, 0:128], in_=p.scr2[:, 0:128]), waits=[c1])
            mmt[-1] = (mmt[-1][0], mmt[-1][1])
            p.tap_guard = [tdma]
            ph1_done.append(c1)
        nfull = len(chs) - (1 if mt is not None else 0)
        nidx = nfull + (1 if (mt is not None and dc0 > 0) else 0)
        n_all = (nidx + 1) if mt is not None else nfull
        lastci = len(chs) - 1
        rts = []
        for ci, (k0, kn) in enumerate(chs[:nfull]):
            rts.append(em.op("vector", lambda e, ci=ci, kn=kn: e.reduce_max(
                mx[:, ci:ci + 1], bank(p, ci, 128, kn), axis=AX.X),
                waits=[mmt[ci]]))
        if mt is not None and dc0 > 0:
            rts.append(em.op("vector", lambda e, nfull=nfull, lastci=lastci,
                             dc0=dc0: e.reduce_max(
                mx[:, nfull:nfull + 1], bank(p, lastci, 128, dc0), axis=AX.X),
                waits=[mmt[-1]]))
        if mt is not None:
            rts.append(em.op("vector", lambda e, nidx=nidx, mw=mw: e.reduce_max(
                mx[:, nidx:nidx + 1], p.msks[:, 0:mw], axis=AX.X), waits=[mt]))
        tm = em.op("vector", lambda e, n=n_all: e.reduce_max(
            m1, mx[:, 0:n], axis=AX.X), waits=rts)
        tb = em.op("vector", lambda e: e.tensor_scalar(
            bm, m1, -scale, None, op0=ALU.mult), waits=[tm])
        ets = []
        for ci, (k0, kn) in enumerate(chs[:nfull]):
            ets.append(em.op("scalar", lambda e, ci=ci, kn=kn: e.activation(
                p.scrb[:, 1024:1024 + kn], bank(p, ci, 128, kn), AF.Exp, bias=bm,
                scale=scale, accum_out=dn[:, ci:ci + 1]), waits=[tb, mmt[ci]]))
            read_bank(p, ci, ets[-1])
        if mt is not None and dc0 > 0:
            ets.append(em.op("scalar", lambda e, nfull=nfull, lastci=lastci,
                             dc0=dc0: e.activation(
                p.scrb[:, 1024:1024 + dc0], bank(p, lastci, 128, dc0), AF.Exp,
                bias=bm, scale=scale, accum_out=dn[:, nfull:nfull + 1]),
                waits=[tb, mmt[-1]]))
            read_bank(p, lastci, ets[-1])
        if mt is not None:
            ets.append(em.op("scalar", lambda e, nidx=nidx, mw=mw: e.activation(
                p.scrb[:, 1664:1664 + mw], p.msks[:, 0:mw], AF.Exp,
                bias=bm, scale=scale, accum_out=dn[:, nidx:nidx + 1]),
                waits=[tb, mt]))
            if dc0 == 0:
                read_bank(p, lastci, mt)
        td = em.op("vector", lambda e, n=n_all: e.reduce_sum(
            den, dn[:, 0:n], axis=AX.X), waits=ets)
        tl = em.op("scalar", lambda e: e.activation(lv, den, AF.Ln), waits=[td])
        tv = em.op("vector", lambda e: e.scalar_tensor_tensor(
            p.stat[:, 50:51], lv, -1.0 / scale, m1, op0=ALU.mult,
            op1=ALU.subtract), waits=[tl, tm])
        th = em.op("vector", lambda e: e.tensor_copy(
            p.scrb[:, 1536:1537], p.stat[:, 50:51]), waits=[tv])
        th2 = em.op("vector", lambda e: e.tensor_copy(
            vhl[:, 0:1], p.scrb[:, 1536:1537]), waits=[th])
        th3 = em.op("vector", lambda e: e.tensor_tensor(
            vhl[:, 1:2], p.stat[:, 50:51], vhl[:, 0:1], op=ALU.subtract),
            waits=[th2])
        if getattr(p, "tap", None) == "h0stats" and h == 0 and comp is None:
            W = 16
            em.op("vector", lambda e, Tt=Tt: e.tensor_copy(
                p.scr2[:, W * Tt:W * Tt + 4], mx[:, 0:4]), waits=rts + [tm])
            em.op("vector", lambda e, Tt=Tt: e.tensor_copy(
                p.scr2[:, W * Tt + 4:W * Tt + 8], dn[:, 0:4]), waits=ets)
            em.op("vector", lambda e, Tt=Tt: e.tensor_copy(
                p.scr2[:, W * Tt + 8:W * Tt + 9], m1), waits=[tm])
            em.op("vector", lambda e, Tt=Tt: e.tensor_copy(
                p.scr2[:, W * Tt + 9:W * Tt + 10], den), waits=[td])
            hst = em.op("vector", lambda e, Tt=Tt: e.tensor_copy(
                p.scr2[:, W * Tt + 10:W * Tt + 11], lv), waits=[tl])
            hst = em.op("vector", lambda e, Tt=Tt: e.tensor_copy(
                p.scr2[:, W * Tt + 11:W * Tt + 13], vhl), waits=[th3])
        bt = 7
        tt = em.op("tensor", lambda e, bt=bt: e.matmul(
            bank(p, bt, 2, 128, rowoff=64), vhl, ident(p),
            start=True, stop=True, tile_position=(0, 64)),
            waits=use_bank(p, bt, [th3]))
        te = em.op("vector", lambda e, Tt=Tt, bt=bt: e.tensor_copy(
            Qaug[64:66, 128 * Tt:128 * Tt + 128], bank(p, bt, 2, 128, rowoff=64)),
            waits=[tt] + wdeps)
        read_bank(p, bt, te)
        if getattr(p, "tap", None) == "h0stats" and h == 0 and comp is None:
            W = 16
            hst = em.op("vector", lambda e, Tt=Tt, bt=bt: e.tensor_copy(
                p.scr2[64:66, W * Tt + 13:W * Tt + 15],
                bank(p, bt, 2, 2, rowoff=64)), waits=[tt])
            read_bank(p, bt, hst)
            if Tt == NT - 1:
                p.tap_done = True
                p.tap_guard = [em.dma("gpsimd", lambda e: e.dma_start(
                    out=p.dram["dbg"][:, 0:256], in_=p.scr2[:, 0:256]),
                    waits=[hst])]
        ph1_done.append(te)
    nkt = NT if comp is None else 1
    prev_tvm = {}
    for qc in range(4):
        q0c = 512 * qc
        kts = [kt for kt in range(nkt)
               if comp is not None or 128 * kt < q0c + 512]
        for ki, kt in enumerate(kts):
            first = ki == 0
            last = ki == len(kts) - 1
            q0 = max(128 * kt, q0c) if comp is None else q0c
            qn = q0c + 512 - q0
            sb = 4 + (ki % 2)
            sco = 512 * (ki % 2)
            if comp is None:
                tmm = em.op("tensor", lambda e, kt=kt, q0=q0, qn=qn, sb=sb: e.matmul(
                    bank(p, sb, 128, qn), Kaug[0:66, 128 * kt:128 * kt + 128],
                    Qaug[0:66, q0:q0 + qn], start=True, stop=True),
                    waits=use_bank(p, sb, ph1_done))
                rows = 128
            else:
                tmm = em.op("tensor", lambda e, q0=q0, qn=qn, sb=sb: e.matmul(
                    bank(p, sb, NB, qn), comp["kc"][0:66, 0:NB],
                    Qaug[0:66, q0:q0 + qn], start=True, stop=True),
                    waits=use_bank(p, sb, ph1_done))
                rows = NB
            mt2 = None
            if comp is None and 128 * kt >= q0c:
                mt2 = em.op("vector", lambda e, sb=sb: e.tensor_tensor(
                    p.msks[:, 0:128], bank(p, sb, 128, 128),
                    p.cf[:, MASKT_O:MASKT_O + 128], op=ALU.add), waits=[tmm])
            elif comp is not None and qc == 0:
                mt2 = em.op("vector", lambda e, sb=sb: e.tensor_tensor(
                    p.msks[0:NB, 0:128], bank(p, sb, NB, 128),
                    p.cf[0:NB, MASKT_O:MASKT_O + 128], op=ALU.add), waits=[tmm])
            prev = [prev_tvm[ki % 2]] if (ki % 2) in prev_tvm else []
            if mt2 is None:
                tex = em.op("scalar", lambda e, qn=qn, sb=sb, rows=rows, sco=sco:
                            e.activation(p.scrb[0:rows, sco:sco + qn],
                                         bank(p, sb, rows, qn), AF.Exp,
                                         scale=scale),
                            waits=[tmm] + prev)
                read_bank(p, sb, tex)
                texs = [tex]
            else:
                texs = [em.op("scalar", lambda e, sb=sb, rows=rows, sco=sco:
                              e.activation(p.scrb[0:rows, sco:sco + 128],
                                           p.msks[0:rows, 0:128], AF.Exp,
                                           scale=scale),
                              waits=[mt2] + prev)]
                if qn > 128:
                    t2x = em.op("scalar", lambda e, qn=qn, sb=sb, rows=rows,
                                sco=sco: e.activation(
                        p.scrb[0:rows, sco + 128:sco + qn],
                        bank(p, sb, rows, qn - 128, coloff=128), AF.Exp,
                        scale=scale), waits=[tmm] + prev)
                    read_bank(p, sb, t2x)
                    texs.append(t2x)
                else:
                    read_bank(p, sb, mt2)
                tex = texs[-1]
            if (getattr(p, "tap", None) == "h0p2" and h == 0 and comp is None
                    and qc == 0 and ki == 0 and not p.tap_done):
                p.tap_done = True
                tpd = em.dma("gpsimd", lambda e: e.dma_start(
                    out=p.dram["dbg"][:, 0:512], in_=p.scrb[0:128, 0:512]),
                    waits=texs)
                p.tap_guard = [tpd]
                texs = [tpd]
            tvm = em.op("tensor", lambda e, kt=kt, q0=q0, qn=qn, first=first,
                        last=last, q0c=q0c, sco=sco: e.matmul(
                bank(p, 6, 64, 512, rowoff=rowoff)[:, q0 - q0c:q0 - q0c + qn],
                (Vtm[:, kt, 0:64] if comp is None else comp["vtm"][0:NB, 0:64]),
                p.scrb[0:(128 if comp is None else NB), sco:sco + qn],
                start=first, stop=last, skip_group_check=True,
                tile_position=(0, rowoff)),
                waits=use_bank(p, 6, texs) if first else texs)
            prev_tvm[ki % 2] = tvm
        ot = em.op("vector", lambda e, qc=qc: e.tensor_copy(
            out_chunk_ap[rowoff:rowoff + 64, 512 * qc:512 * qc + 512],
            bank(p, 6, 64, 512, rowoff=rowoff)), waits=[tvm])
        read_bank(p, 6, ot)
    if getattr(p, "tap", None) == f"head{h}" + ("c" if comp else "") and not p.tap_done:
        p.tap_done = True
        t1 = em.dma("gpsimd", lambda e: e.dma_start(
            out=p.dram["dbg"][0:66, 0:T], in_=Qaug[0:66, :]), waits=[ot])
        t2 = em.dma("gpsimd", lambda e: e.dma_start(
            out=p.dram["dbg"][0:66, T:2 * T], in_=Kaug[0:66, :]), waits=[ot])
        t3 = em.dma("gpsimd", lambda e: e.dma_start(
            out=p.dram["dbg"][:, 2 * T:2 * T + NT * 64],
            in_=Vtm[:].rearrange("p a b -> p (a b)")), waits=[ot])
        p.tap_guard = [t1, t2, t3]
    return ot, ph1_done


def emit_l0_attn(p, is_last=False):
    em = p.em
    ar = p.ar
    dram = p.dram
    deps0 = p.ctok + p.xtok

    ar.mark()
    local_all = ar.alloc("local_all", [128, NC8, T], BF16)
    comp_all = ar.alloc("comp_all", [128, NC8, T], BF16)
    ar.mark()
    kcomp = ar.alloc("kcomp", [128, NH, 128], BF16)
    vcomp = ar.alloc("vcomp", [128, NH, 64], BF16)
    wslot = [ar.alloc(f"wsl{i}", [128, 3, 8, 128], BF16) for i in range(2)]
    QS = [ar.alloc(f"QS{i}", [128, T], BF16) for i in range(4)]
    KS = [ar.alloc(f"KS{i}", [128, T], BF16) for i in range(4)]
    VD = [ar.alloc(f"VD{i}", [128, T], BF16) for i in range(4)]
    VT = [ar.alloc(f"VT{i}", [128, NT, 64], BF16) for i in range(4)]
    stage = ar.alloc("stage", [128, 512], BF16)
    sconv = ar.alloc("sconv", [64, 2048], BF16)
    cs = stream_w_bf16(p, sconv[0:64, :], dram["conv_w"][:, :])
    ko = em.op("vector", lambda e: e.memset(
        kcomp[64:66, :].rearrange("p h c -> p (h c)"), 1.0), waits=p.ctok)
    stz = em.op("vector", lambda e: e.memset(stage[0:64, :], 0.0), waits=[])
    ones_toks = [ko, stz]

    wtok = [None, None]
    last_prod = [[], []]

    def load_pair(pair):
        sl = wslot[pair % 2]
        toks = []
        dep = last_prod[pair % 2]
        for j, base in enumerate((0, C, 2 * C)):
            cols = slice(base + 128 * pair, base + 128 * pair + 128)
            toks.append(stream_w_bf16(
                p, sl[:, j, :, :],
                dram["cattn_w"][:, cols].rearrange("(c p) j -> p c j", p=128),
                deps=dep))
        wtok[pair % 2] = toks
        return toks

    deps0 = tap(p, "xbf", p.x_bf[:].rearrange("p c t -> p (c t)"), deps0)
    wt0 = load_pair(0)
    if getattr(p, "tap", None) == "wsl":
        p.tap_done = True
        tw = em.dma("gpsimd", lambda e: e.dma_start(
            out=p.dram["dbg"][:, 0:3072],
            in_=wslot[0][:].rearrange("p j c k -> p (j c k)")), waits=wt0)
        last_prod[0] = last_prod[0] + [tw]
    slot_free = [[], []]
    out_toks = []
    for pair in range(NH // 2):
        par2 = pair % 2
        wdep = wtok[par2]
        sl = wslot[par2]
        if pair + 1 < NH // 2:
            load_pair(pair + 1)
        qa, ka = QS[2 * par2], KS[2 * par2]
        qb, kb = QS[2 * par2 + 1], KS[2 * par2 + 1]
        vda, vdb = VD[2 * par2], VD[2 * par2 + 1]
        vta, vtb = VT[2 * par2], VT[2 * par2 + 1]
        sf = slot_free[par2]
        vbs = p.vbias[par2]
        tb0 = em.op("tensor", lambda e, pair=pair: e.transpose(
            bank(p, 3, 1, 128), p.pv["cattn_b"][:, 16 + pair:17 + pair],
            ident(p)), waits=use_bank(p, 3, p.ctok + last_prod[par2]))
        vb0 = em.op("vector", lambda e, vbs=vbs: e.tensor_copy(
            vbs[0:1, :], bank(p, 3, 1, 128)), waits=[tb0] + last_prod[par2])
        read_bank(p, 3, vb0)
        prod_toks = []
        for j, (evA, evB) in enumerate(((qa, qb), (ka, kb), (vda, vdb))):
            for tc in range(4):
                sltc = slice(512 * tc, 512 * tc + 512)
                mt = None
                for c in range(NC8):
                    mt = em.op("tensor", lambda e, j=j, c=c, sl=sl, sltc=sltc: e.matmul(
                        bank(p, 0), sl[:, j, c, :], p.x_bf[:, c, sltc],
                        start=(c == 0), stop=(c == NC8 - 1), skip_group_check=True),
                        waits=use_bank(p, 0, wdep + deps0) if c == 0 else [])
                bcol = {0: pair, 1: 8 + pair, 2: 16 + pair}[j]
                e1 = em.op("scalar", lambda e, evA=evA, tc=tc, bcol=bcol, sltc=sltc:
                           e.activation(evA[0:64, sltc], bank(p, 0, 64),
                                        AF.Identity,
                                        bias=p.pv["cattn_b"][0:64, bcol:bcol + 1]),
                           waits=[mt] + sf)
                e2 = em.op("scalar", lambda e, bcol=bcol: e.activation(
                    stage[64:128, :], bank(p, 0, 64, rowoff=64), AF.Identity,
                    bias=p.pv["cattn_b"][64:128, bcol:bcol + 1]), waits=[mt])
                read_bank(p, 0, e2)
                ms = em.op("tensor", lambda e: e.matmul(
                    bank(p, 1, 64), p.cb[:, E64_O:E64_O + 64], stage[:, :],
                    start=True, stop=True),
                    waits=use_bank(p, 1, [e2, stz]))
                e3 = em.op("vector", lambda e, evB=evB, sltc=sltc: e.tensor_copy(
                    evB[0:64, sltc], bank(p, 1, 64)), waits=[ms] + sf)
                read_bank(p, 1, e3)
                prod_toks += [e1, e3]
        prod_toks.append(em.op("vector", lambda e, ka=ka: e.memset(
            ka[64:66, :], 1.0), waits=sf))
        prod_toks.append(em.op("vector", lambda e, kb=kb: e.memset(
            kb[64:66, :], 1.0), waits=sf))
        for Tt in range(NT):
            mt = None
            for c in range(NC8):
                mt = em.op("tensor", lambda e, c=c, Tt=Tt, sl=sl: e.matmul(
                    bank(p, 2, 128, 128), p.x_bf[:, c, 128 * Tt:128 * Tt + 128],
                    sl[:, 2, c, :], start=(c == 0), stop=False,
                    skip_group_check=True),
                    waits=use_bank(p, 2, wdep + [vb0]) if c == 0 else [])
            mt = em.op("tensor", lambda e, vbs=vbs: e.matmul(
                bank(p, 2, 128, 128), ones(p, 1, 128), vbs[0:1, :],
                start=False, stop=True, skip_group_check=True), waits=[])
            ev = em.op("vector", lambda e, Tt=Tt, vta=vta: e.tensor_copy(
                vta[:, Tt, :], bank(p, 2, 128, 64)), waits=[mt] + sf)
            ev2 = em.op("vector", lambda e, Tt=Tt, vtb=vtb: e.tensor_copy(
                vtb[:, Tt, :], bank(p, 2, 128, 64, coloff=64)), waits=[mt] + sf)
            read_bank(p, 2, ev2)
            prod_toks += [ev, ev2]
        last_prod[par2] = prod_toks[-2:]
        nsf = []
        for hh in range(2):
            h = 2 * pair + hh
            Qh, Kh, Vdh, Vth = (qa, ka, vda, vta) if hh == 0 else (qb, kb, vdb, vtb)
            oc_ap = local_all[:, h // 2, :]
            ot, ph1 = attn_head(p, h, Qh, Kh, Vth, oc_ap,
                                prod_toks + ones_toks)
            out_toks.append(ot)
            for which, srct in (("k", Kh), ("v", Vdh)):
                mt = None
                for tau in range(32):
                    mt = em.op("tensor", lambda e, tau=tau, srct=srct: e.matmul(
                        bank(p, 0, 64, NB), sconv[0:64, 64 * tau:64 * tau + 64],
                        srct[0:64, tau:tau + 16 * (NB - 1) + 1:16],
                        start=(tau == 0), stop=(tau == 31), skip_group_check=True),
                        waits=use_bank(p, 0, prod_toks + [cs]) if tau == 0 else [])
                e0 = em.op("scalar", lambda e: e.activation(
                    p.scr[0:64, 0:NB], bank(p, 0, 64, NB), AF.Identity,
                    bias=p.pv["conv_b"][0:64, 0:1]), waits=[mt])
                read_bank(p, 0, e0)
                s1 = em.op("tensor", lambda e: e.matmul(
                    bank(p, 1, 1, NB), ones(p, 64, 1), p.scr[0:64, 0:NB],
                    start=True, stop=True), waits=use_bank(p, 1, [e0]))
                sq = em.op("scalar", lambda e: e.activation(
                    p.scr[0:64, 128:128 + NB], p.scr[0:64, 0:NB], AF.Square),
                    waits=[e0])
                s2 = em.op("tensor", lambda e: e.matmul(
                    bank(p, 2, 1, NB), ones(p, 64, 1), p.scr[0:64, 128:128 + NB],
                    start=True, stop=True), waits=use_bank(p, 2, [sq]))
                t1 = em.op("vector", lambda e: e.tensor_scalar(
                    p.strip0[0:1, 0:NB], bank(p, 1, 1, NB), 1.0 / 64, None,
                    op0=ALU.mult), waits=[s1])
                read_bank(p, 1, t1)
                t2 = em.op("vector", lambda e: e.tensor_tensor(
                    p.strip2[0:1, 0:NB], p.strip0[0:1, 0:NB], p.strip0[0:1, 0:NB],
                    op=ALU.mult), waits=[t1])
                t3 = em.op("vector", lambda e: e.scalar_tensor_tensor(
                    p.strip1[0:1, 0:NB], bank(p, 2, 1, NB), 1.0 / 64,
                    p.strip2[0:1, 0:NB], op0=ALU.mult, op1=ALU.subtract),
                    waits=[t2, s2])
                read_bank(p, 2, t3)
                t4 = rsqrt_strip(p, p.strip1[0:1, 0:NB], p.strip1[0:1, 0:NB],
                                 1e-5, [t3], NB)
                b1 = em.op("tensor", lambda e: e.matmul(
                    bank(p, 1, 64, NB), ones(p, 1, 64), p.strip0[0:1, 0:NB],
                    start=True, stop=True), waits=use_bank(p, 1, [t1]))
                b2 = em.op("tensor", lambda e: e.matmul(
                    bank(p, 2, 64, NB), ones(p, 1, 64), p.strip1[0:1, 0:NB],
                    start=True, stop=True), waits=use_bank(p, 2, [t4]))
                a1 = em.op("vector", lambda e: e.tensor_tensor(
                    p.scr[0:64, 256:256 + NB], p.scr[0:64, 0:NB],
                    bank(p, 1, 64, NB), op=ALU.subtract), waits=[b1, e0])
                read_bank(p, 1, a1)
                a2 = em.op("vector", lambda e: e.tensor_tensor(
                    p.scr[0:64, 256:256 + NB], p.scr[0:64, 256:256 + NB],
                    bank(p, 2, 64, NB), op=ALU.mult), waits=[a1, b2])
                read_bank(p, 2, a2)
                a3 = em.op("vector", lambda e: e.tensor_scalar(
                    p.scr[0:64, 256:256 + NB], p.scr[0:64, 256:256 + NB],
                    p.pv["compln_w"][0:64, 0:1], p.pv["compln_b"][0:64, 0:1],
                    op0=ALU.mult, op1=ALU.add), waits=[a2])
                if which == "k":
                    a4 = em.op("scalar", lambda e, h=h: e.activation(
                        kcomp[0:64, h, 0:NB], p.scr[0:64, 256:256 + NB],
                        AF.Gelu), waits=[a3])
                else:
                    a4 = em.op("scalar", lambda e: e.activation(
                        p.scr[0:64, 384:384 + NB], p.scr[0:64, 256:256 + NB],
                        AF.Gelu), waits=[a3])
                    tt = em.op("tensor", lambda e: e.transpose(
                        bank(p, 0, NB, 64), p.scr[0:64, 384:384 + NB],
                        ident(p)[0:64, 0:64]), waits=use_bank(p, 0, [a4]))
                    a4 = em.op("vector", lambda e, h=h: e.tensor_copy(
                        vcomp[0:NB, h, :], bank(p, 0, NB, 64)), waits=[tt])
                    read_bank(p, 0, a4)
                out_toks.append(a4)
            cdep = [out_toks[-1], out_toks[-2]] + ones_toks + getattr(p, "tap_guard", [])
            otc, _ = attn_head(p, h, Qh, Kh, Vth, comp_all[:, h // 2, :],
                               cdep, comp=dict(
                                   kc=kcomp[:, h, :], vtm=vcomp[:, h, :]))
            out_toks.append(otc)
            nsf += [ot, otc] + getattr(p, "tap_guard", [])
            p.tap_guard = []
        slot_free[par2] = nsf

    jn = em.join()
    jn = tap(p, "local", local_all[:].rearrange("p c t -> p (c t)"), jn)
    jn = tap(p, "comp", comp_all[:].rearrange("p c t -> p (c t)"), jn)
    lnl = ln_fm(p, lambda c: local_all[:, c, :], None,
                lambda c: local_all[:, c, :], jn,
                gamma=p.pv["lnloc_w"], beta=p.pv["lnloc_b"], bf_in=True)
    ar.release()
    ar.mark()
    x_fm = ar.alloc("x_fm", [128, NC8, T], F32)
    p.x_fm = x_fm
    gw1 = ar.alloc("gw1", [128, 16, 128], BF16)
    gh = p.stat[:, 0:8]
    ghf = p.stat[:, 8:16]
    gtoks = []
    for oc in range(NC8):
        wt = stream_w_bf16(p, gw1[:],
                           dram["gate_w1"][:, 128 * oc:128 * oc + 128]
                           .rearrange("(f pp) j -> pp f j", pp=128),
                           deps=(gtoks[-1:] if gtoks else lnl[-1:]))
        mt = None
        for fc in range(16):
            rhs = (local_all[:, fc, T - 1:T] if fc < NC8
                   else comp_all[:, fc - NC8, T - 1:T])
            mt = em.op("tensor", lambda e, fc=fc, rhs=rhs, wt=wt: e.matmul(
                bank(p, 3, 128, 1), gw1[:, fc, :], rhs,
                start=(fc == 0), stop=(fc == 15), skip_group_check=True),
                waits=use_bank(p, 3, [wt] + lnl) if fc == 0 else [wt])
        gt = em.op("scalar", lambda e, oc=oc: e.activation(
            gh[:, oc:oc + 1], bank(p, 3, 128, 1), AF.Identity,
            bias=p.pv["gate_b1"][:, oc:oc + 1]), waits=[mt])
        read_bank(p, 3, gt)
        gtoks.append(gt)
    s1 = em.op("tensor", lambda e: e.matmul(
        bank(p, 0, 1, 8), ones(p, 128, 1), gh, start=True, stop=True),
        waits=use_bank(p, 0, gtoks))
    sq = em.op("scalar", lambda e: e.activation(
        p.scr[:, 0:8], gh, AF.Square), waits=gtoks)
    s2 = em.op("tensor", lambda e: e.matmul(
        bank(p, 1, 1, 8), ones(p, 128, 1), p.scr[:, 0:8], start=True,
        stop=True), waits=use_bank(p, 1, [sq]))
    r1 = em.op("vector", lambda e: e.reduce_sum(
        p.strip0[0:1, 0:1], bank(p, 0, 1, 8), axis=AX.X), waits=[s1])
    read_bank(p, 0, r1)
    r2 = em.op("vector", lambda e: e.reduce_sum(
        p.strip1[0:1, 0:1], bank(p, 1, 1, 8), axis=AX.X), waits=[s2])
    read_bank(p, 1, r2)
    r3 = em.op("vector", lambda e: e.tensor_scalar(
        p.strip0[0:1, 0:1], p.strip0[0:1, 0:1], 1.0 / 1024, None, op0=ALU.mult),
        waits=[r1])
    r4 = em.op("vector", lambda e: e.tensor_tensor(
        p.strip2[0:1, 0:1], p.strip0[0:1, 0:1], p.strip0[0:1, 0:1], op=ALU.mult),
        waits=[r3])
    r5 = em.op("vector", lambda e: e.scalar_tensor_tensor(
        p.strip1[0:1, 0:1], p.strip1[0:1, 0:1], 1.0 / 1024, p.strip2[0:1, 0:1],
        op0=ALU.mult, op1=ALU.subtract), waits=[r2, r4])
    r6 = rsqrt_strip(p, p.strip1[0:1, 0:1], p.strip1[0:1, 0:1], 1e-5, [r5], 1)
    b1 = em.op("tensor", lambda e: e.matmul(
        bank(p, 0, 128, 1), ones(p, 1, 128), p.strip0[0:1, 0:1], start=True,
        stop=True), waits=use_bank(p, 0, [r3, r6]))
    b2 = em.op("tensor", lambda e: e.matmul(
        bank(p, 1, 128, 1), ones(p, 1, 128), p.strip1[0:1, 0:1], start=True,
        stop=True), waits=use_bank(p, 1, [r6]))
    e1 = em.op("vector", lambda e: e.tensor_copy(p.stat[:, 16:17],
                                                 bank(p, 0, 128, 1)), waits=[b1])
    read_bank(p, 0, e1)
    e2 = em.op("vector", lambda e: e.tensor_copy(p.stat[:, 17:18],
                                                 bank(p, 1, 128, 1)), waits=[b2])
    read_bank(p, 1, e2)
    a1 = em.op("vector", lambda e: e.tensor_scalar(
        ghf, gh, p.stat[:, 16:17], p.stat[:, 17:18], op0=ALU.subtract,
        op1=ALU.mult), waits=[e1, e2] + gtoks)
    a2 = em.op("vector", lambda e: e.tensor_tensor(
        ghf, ghf, p.pv["gateln_w"][:, 0:8], op=ALU.mult), waits=[a1])
    a3 = em.op("vector", lambda e: e.tensor_tensor(
        ghf, ghf, p.pv["gateln_b"][:, 0:8], op=ALU.add), waits=[a2])
    a4 = em.op("scalar", lambda e: e.activation(ghf, ghf, AF.Relu), waits=[a3])
    mt = None
    for oc in range(NC8):
        mt = em.op("tensor", lambda e, oc=oc: e.matmul(
            bank(p, 2, 2, 1), p.pv["gate_w2"][:, 2 * oc:2 * oc + 2],
            ghf[:, oc:oc + 1], start=(oc == 0), stop=(oc == 7),
            skip_group_check=True),
            waits=use_bank(p, 2, [a4] + p.ctok) if oc == 0 else [a4])
    l1 = em.op("scalar", lambda e: e.activation(
        p.l2[0:2, 0:1], bank(p, 2, 2, 1), AF.Identity,
        bias=p.pv["gate_b2"][0:2, 0:1]), waits=[mt])
    read_bank(p, 2, l1)
    dmm = em.op("tensor", lambda e: e.matmul(
        bank(p, 3, 1, 1), p.cf[0:2, PM_O:PM_O + 1], p.l2[0:2, 0:1],
        start=True, stop=True), waits=use_bank(p, 3, [l1]))
    g0 = em.op("scalar", lambda e: e.activation(
        p.strip0[0:1, 8:9], bank(p, 3, 1, 1), AF.Sigmoid), waits=[dmm])
    read_bank(p, 3, g0)
    g1 = em.op("vector", lambda e: e.tensor_scalar(
        p.strip0[0:1, 9:10], p.strip0[0:1, 8:9], -1.0, 1.0, op0=ALU.mult,
        op1=ALU.add), waits=[g0])
    bg = em.op("tensor", lambda e: e.matmul(
        bank(p, 0, 128, 2), ones(p, 1, 128), p.strip0[0:1, 8:10], start=True,
        stop=True), waits=use_bank(p, 0, [g1]))
    gv = em.op("vector", lambda e: e.tensor_copy(p.stat[:, 18:20],
                                                 bank(p, 0, 128, 2)), waits=[bg])
    read_bank(p, 0, gv)
    comb_toks = []
    for c in range(NC8):
        t1 = em.op("vector", lambda e, c=c: e.tensor_scalar(
            p.scr[:, 0:2048], local_all[:, c, :], p.stat[:, 18:19], None,
            op0=ALU.mult), waits=[gv] + lnl)
        t2 = em.op("vector", lambda e, c=c: e.scalar_tensor_tensor(
            comp_all[:, c, :], comp_all[:, c, :], p.stat[:, 19:20],
            p.scr[:, 0:2048], op0=ALU.mult, op1=ALU.add), waits=[t1])
        comb_toks.append(t2)
    comb_toks = tap(p, "combined", comp_all[:].rearrange("p c t -> p (c t)"),
                    comb_toks)
    xr = em.dma("sync", lambda e, xh=p.x_hbm: e.dma_start(
        out=x_fm[:],
        in_=xh[:, :].rearrange("(c p) t -> p c t", p=128)),
        waits=lnl)
    res_toks = []
    for oc in range(NC8):
        wt = stream_w_bf16(p, gw1[:, 0:8, :],
                           dram["cproj_w"][:, 128 * oc:128 * oc + 128]
                           .rearrange("(c pp) j -> pp c j", pp=128),
                           deps=(res_toks[-4:] if res_toks else gtoks[-1:]))
        for tc in range(4):
            sltc = slice(512 * tc, 512 * tc + 512)
            mt = None
            for c in range(NC8):
                mt = em.op("tensor", lambda e, c=c, oc=oc, sltc=sltc: e.matmul(
                    bank(p, 4 + oc % 2),
                    gw1[:, c, :],
                    comp_all[:, c, sltc],
                    start=(c == 0), stop=(c == NC8 - 1), skip_group_check=True),
                    waits=use_bank(p, 4 + oc % 2, [wt] + comb_toks) if c == 0 else [wt])
            ee1 = em.op("scalar", lambda e, oc=oc: e.activation(
                p.scr2[:, 0:512] if oc % 2 == 0 else p.scr2[:, 512:1024],
                bank(p, 4 + oc % 2), AF.Identity,
                bias=p.pv["cproj_b"][:, oc:oc + 1]), waits=[mt])
            read_bank(p, 4 + oc % 2, ee1)
            ee2 = em.op("vector", lambda e, oc=oc, sltc=sltc: e.tensor_tensor(
                x_fm[:, oc, sltc],
                p.scr2[:, 0:512] if oc % 2 == 0 else p.scr2[:, 512:1024],
                x_fm[:, oc, sltc], op=ALU.add), waits=[ee1, xr])
            res_toks.append(ee2)
    lnt = ln_fm(p, lambda c: x_fm[:, c, :], lambda c: x_fm[:, c, :],
                lambda c: p.x_bf[:, c, :], res_toks)
    if is_last:
        yt = em.dma("sync", lambda e: e.dma_start(
            out=dram["y"][:, :].rearrange("(c p) t -> p c t", p=128),
            in_=x_fm[:]), waits=lnt)
        out = [yt]
    else:
        st = em.dma("sync", lambda e: e.dma_start(
            out=dram["spill_x"][:, :].rearrange("(c p) t -> p c t", p=128),
            in_=x_fm[:]), waits=lnt)
        p.x_hbm = dram["spill_x"]
        out = [st] + lnt
    ar.release()
    ar.release()
    return out


def emit_l0_ffn(p, entry, is_last=False):
    em = p.em
    ar = p.ar
    dram = p.dram
    st = p.stat
    MX1 = 0
    SC1 = 16
    RS1 = 32
    TMP = 48
    M1B = 52
    M2B = 55
    AC1 = 58
    MX2 = 32

    ar.mark()
    hsil = ar.alloc("hsil", [128, 32, T], BF16)
    p._dbg_hsil = hsil
    s2bc = ar.alloc("s2bc", [128, T], BF16)
    ar.mark()
    h_full = ar.alloc("h_full", [128, 2050], F32)
    wsl = [ar.alloc(f"fw{i}", [128, 8, 128], BF16) for i in range(2)]
    xtmp = ar.alias("xtmp", [128, T], F32, h_full)
    scri = ar.alias("scri", [128, 1024], I32, p.scr2)

    def w_stats(src_tiles, acc_col, mean_col, thr_col, cnt, deps):
        toks = []
        z = em.op("vector", lambda e, acc_col=acc_col: e.memset(
            st[:, acc_col:acc_col + 1], 0.0), waits=deps)
        bufs = [xtmp, p.scr]
        prev = [z]
        ld_prev = {}
        for i, (ap, cols) in enumerate(src_tiles):
            buf = bufs[i % 2]
            ld = em.dma("sync", lambda e, ap=ap, buf=buf, cols=cols: e.dma_start(
                out=buf[:, 0:cols], in_=ap), waits=ld_prev.get(i % 2, []) + deps)
            rd = em.op("vector", lambda e, buf=buf, cols=cols: e.reduce_sum(
                st[:, TMP:TMP + 1], buf[:, 0:cols], axis=AX.X,
                apply_absolute_value=True), waits=[ld] + prev)
            ac = em.op("vector", lambda e, acc_col=acc_col: e.tensor_tensor(
                st[:, acc_col:acc_col + 1], st[:, acc_col:acc_col + 1],
                st[:, TMP:TMP + 1], op=ALU.add), waits=[rd])
            prev = [ac]
            ld_prev[i % 2] = [rd]
        pm = em.op("tensor", lambda e, acc_col=acc_col: e.matmul(
            bank(p, 3, 1, 1), st[:, acc_col:acc_col + 1], ones(p, 128, 1),
            start=True, stop=True), waits=use_bank(p, 3, prev + p.ctok))
        m1 = em.op("vector", lambda e, cnt=cnt: e.tensor_scalar(
            st[0:1, TMP:TMP + 1], bank(p, 3, 1, 1), 1.0 / cnt, None,
            op0=ALU.mult), waits=[pm])
        read_bank(p, 3, m1)
        m2 = em.op("vector", lambda e: e.tensor_scalar(
            st[0:1, TMP:TMP + 1], st[0:1, TMP:TMP + 1], 1e-5, None,
            op0=ALU.max), waits=[m1])
        bc = em.op("tensor", lambda e: e.matmul(
            bank(p, 3, 128, 1), ones(p, 1, 128), st[0:1, TMP:TMP + 1],
            start=True, stop=True), waits=use_bank(p, 3, [m2]))
        mb = em.op("vector", lambda e, mean_col=mean_col: e.tensor_copy(
            st[:, mean_col:mean_col + 1], bank(p, 3, 128, 1)), waits=[bc])
        read_bank(p, 3, mb)
        t1 = em.op("vector", lambda e, mean_col=mean_col, thr_col=thr_col:
                   e.tensor_scalar(st[:, thr_col:thr_col + 1],
                                   st[:, mean_col:mean_col + 1], 0.5, None,
                                   op0=ALU.mult), waits=[mb])
        t2 = em.op("vector", lambda e, mean_col=mean_col, thr_col=thr_col:
                   e.tensor_scalar(st[:, thr_col + 1:thr_col + 2],
                                   st[:, mean_col:mean_col + 1], -0.5, None,
                                   op0=ALU.mult), waits=[mb])
        return [t1, t2]

    def w_tern(src_tiles, dst_tiles, thr_col, deps, dma_guard=()):
        toks = []
        prev_q = deps
        prev_dma = list(dma_guard)
        for (ap, cols), dap in zip(src_tiles, dst_tiles):
            ld = em.dma("sync", lambda e, ap=ap, cols=cols: e.dma_start(
                out=xtmp[:, 0:cols], in_=ap), waits=prev_q + deps)
            c1 = em.op("vector", lambda e, cols=cols, thr_col=thr_col:
                       e.tensor_scalar(p.scr[:, 0:cols], xtmp[:, 0:cols],
                                       st[:, thr_col + 1:thr_col + 2], None,
                                       op0=ALU.is_lt), waits=[ld])
            q = em.op("vector", lambda e, cols=cols, thr_col=thr_col:
                      e.scalar_tensor_tensor(p.scrb[:, 0:cols], xtmp[:, 0:cols],
                                             st[:, thr_col:thr_col + 1],
                                             p.scr[:, 0:cols], op0=ALU.is_gt,
                                             op1=ALU.subtract),
                      waits=[c1] + prev_dma)
            wd = em.dma("gpsimd", lambda e, dap=dap, cols=cols: e.dma_start(
                out=dap, in_=p.scrb[:, 0:cols]), waits=[q])
            prev_q = [q]
            prev_dma = [wd]
            toks.append(wd)
        return toks

    w1_tiles = [(dram["ffn_w1t"][128 * c:128 * c + 128, 2048 * h:2048 * h + 2048],
                 2048)
                for c in range(8) for h in range(2)]
    w1_dst = [dram["t1_scr"][128 * c:128 * c + 128, 2048 * h:2048 * h + 2048]
              for c in range(8) for h in range(2)]
    w2_tiles = [(dram["ffn_w2t"][128 * k:128 * k + 128, 0:1024], 1024)
                for k in range(32)]
    w2_dst = [dram["t2_scr"][128 * k:128 * k + 128, 0:1024] for k in range(32)]

    s1t = w_stats(w1_tiles, AC1, M1B, M1B + 1, 4096 * 1024, entry_w(p, entry))
    tn1 = w_tern(w1_tiles, w1_dst, M1B + 1, s1t)
    s2t = w_stats(w2_tiles, AC1 + 1, M2B, M2B + 1, 4096 * 1024,
                  s1t + tn1[-1:])
    tn2 = w_tern(w2_tiles, w2_dst, M2B + 1, s2t, dma_guard=tn1[-1:])

    bufs = [xtmp, p.scr]
    ld_prev = {}
    mx_toks = []
    prev_acc = tn2[-1:]
    for c in range(NC8):
        buf = bufs[c % 2]
        ld = em.dma("sync", lambda e, c=c, buf=buf, xh=p.x_hbm: e.dma_start(
            out=buf[:, :], in_=xh[128 * c:128 * c + 128, :]),
            waits=ld_prev.get(c % 2, []) + entry + (tn2[-1:] if c < 2 else []))
        for i in range(NT):
            tr = em.op("tensor", lambda e, buf=buf, i=i: e.transpose(
                bank(p, 2, 128, 128), buf[:, 128 * i:128 * i + 128], ident(p)),
                waits=use_bank(p, 2, [ld] + p.ctok))
            if c == 0:
                rd = em.op("vector", lambda e, i=i: e.reduce_max(
                    st[:, MX1 + i:MX1 + i + 1], bank(p, 2, 128, 128), axis=AX.X,
                    apply_absolute_value=True), waits=[tr])
            else:
                rd = em.op("vector", lambda e: e.reduce_max(
                    st[:, TMP:TMP + 1], bank(p, 2, 128, 128), axis=AX.X,
                    apply_absolute_value=True), waits=[tr] + prev_acc)
                rd = em.op("vector", lambda e, i=i: e.tensor_tensor(
                    st[:, MX1 + i:MX1 + i + 1], st[:, MX1 + i:MX1 + i + 1],
                    st[:, TMP:TMP + 1], op=ALU.max), waits=[rd])
                prev_acc = [rd]
            read_bank(p, 2, rd)
            mx_toks.append(rd)
        ld_prev[c % 2] = mx_toks[-NT:]
    mc = em.op("vector", lambda e: e.tensor_scalar(
        st[:, MX1:MX1 + 16], st[:, MX1:MX1 + 16], 1e-5, None, op0=ALU.max),
        waits=mx_toks)
    sl1 = em.op("scalar", lambda e: e.activation(
        st[:, SC1:SC1 + 16], st[:, MX1:MX1 + 16], AF.Ln), waits=[mc])
    sl2 = em.op("scalar", lambda e: e.activation(
        st[:, SC1:SC1 + 16], st[:, SC1:SC1 + 16], AF.Exp, scale=-1.0,
        bias=p.cf[:, LN127_O:LN127_O + 1]), waits=[sl1] + p.ctok)
    rs1 = em.op("vector", lambda e: e.tensor_scalar(
        st[:, RS1:RS1 + 16], st[:, MX1:MX1 + 16], st[:, M1B:M1B + 1],
        1.0 / 127.0, op0=ALU.mult, op1=ALU.mult), waits=[mc] + s1t)
    sc_bc = []
    rs_toks = []
    for i in range(NT):
        d1 = em.op("vector", lambda e, i=i: e.tensor_scalar(
            p.msks[:, :], ident(p), st[:, SC1 + i:SC1 + i + 1], None,
            op0=ALU.mult), waits=[sl2] + (sc_bc[-1:] if sc_bc else []))
        m1 = em.op("tensor", lambda e, i=i: e.matmul(
            bank(p, i // 4, 128, 128, coloff=128 * (i % 4)), ones(p, 128, 128),
            p.msks[:, :], start=(i % 4 == 0), stop=(i % 4 == 3),
            skip_group_check=True),
            waits=use_bank(p, i // 4, [d1]) if i % 4 == 0 else [d1])
        sc_bc.append(m1)
        d2 = em.op("vector", lambda e, i=i: e.tensor_scalar(
            p.msks[:, :], ident(p), st[:, RS1 + i:RS1 + i + 1], None,
            op0=ALU.mult), waits=[rs1, m1])
        m2 = em.op("tensor", lambda e, i=i: e.matmul(
            bank(p, 4 + i // 4, 128, 128, coloff=128 * (i % 4)),
            ones(p, 128, 128), p.msks[:, :], start=(i % 4 == 0),
            stop=(i % 4 == 3), skip_group_check=True),
            waits=use_bank(p, 4 + i // 4, [d2]) if i % 4 == 0 else [d2])
        sc_bc.append(m2)
        rs_toks.append(m2)
    s2bt = []
    for b in range(4):
        t = em.op("vector", lambda e, b=b: e.tensor_copy(
            s2bc[:, 512 * b:512 * b + 512], bank(p, 4 + b, 128, 512)),
            waits=rs_toks)
        read_bank(p, 4 + b, t)
        s2bt.append(t)
    xq_toks = []
    ld_prev = []
    for c in range(NC8):
        ld = em.dma("sync", lambda e, c=c, xh=p.x_hbm: e.dma_start(
            out=xtmp[:, :], in_=xh[128 * c:128 * c + 128, :]),
            waits=ld_prev + entry)
        for hf in range(2):
            hsl = slice(1024 * hf, 1024 * hf + 1024)
            v = em.op("vector", lambda e, hsl=hsl: e.tensor_tensor(
                p.scr[:, 0:1024], xtmp[:, hsl], p.PA[:, hsl], op=ALU.mult),
                waits=[ld] + sc_bc + xq_toks[-1:])
            u = em.op("scalar", lambda e: e.activation(
                p.scr[:, 1024:2048], p.scr[:, 0:1024], AF.Identity,
                bias=p.cf[:, F512_O:F512_O + 1]), waits=[v])
            qi = em.op("vector", lambda e: e.tensor_copy(
                scri[:, 0:1024], p.scr[:, 1024:2048]), waits=[u])
            qf = em.op("scalar", lambda e: e.activation(
                p.scr[:, 0:1024], scri[:, 0:1024], AF.Identity), waits=[qi])
            cc = em.op("vector", lambda e: e.scalar_tensor_tensor(
                p.scrb[:, 0:1024], p.scr[:, 1024:2048], -0.5, p.scr[:, 0:1024],
                op0=ALU.add, op1=ALU.is_gt), waits=[qf])
            q = em.op("vector", lambda e, c=c, hsl=hsl: e.scalar_tensor_tensor(
                p.x_bf[:, c, hsl], p.scr[:, 0:1024], -512.0, p.scrb[:, 0:1024],
                op0=ALU.add, op1=ALU.add), waits=[cc])
            xq_toks.append(q)
        ld_prev = xq_toks[-2:]
    for b in range(4):
        read_bank(p, b, xq_toks[-1])

    ed0 = em.op("vector", lambda e: e.memset(h_full[:, 0:1], 0.0),
                waits=xq_toks[-1:])
    ed1 = em.op("vector", lambda e: e.memset(h_full[:, 2049:2050], 0.0),
                waits=xq_toks[-1:])
    edges = [ed0, ed1]
    wtok = [None, None]
    wsl_free = [[], []]
    prev_silu = None
    prev_conv = []
    mx2_first = True
    mx2_prev = []
    for j in range(32):
        wtok[j % 2] = em.dma("gpsimd", lambda e, j=j: e.dma_start(
            out=wsl[j % 2][:], in_=dram["t1_scr"][:, 128 * j:128 * j + 128]
            .rearrange("(c p) j2 -> p c j2", p=128)),
            waits=wsl_free[j % 2] + (tn1[-1:] if j < 2 else []))
        ev_toks = []
        mm_last = []
        for tc in range(4):
            sltc = slice(512 * tc, 512 * tc + 512)
            bk = tc % 2
            mt = None
            for c in range(NC8):
                mt = em.op("tensor", lambda e, j=j, c=c, sltc=sltc, bk=bk: e.matmul(
                    bank(p, bk, 128, 512), wsl[j % 2][:, c, :],
                    p.x_bf[:, c, sltc], start=(c == 0), stop=(c == NC8 - 1),
                    skip_group_check=True),
                    waits=use_bank(p, bk, [wtok[j % 2]] + xq_toks[-1:])
                    if c == 0 else [wtok[j % 2]])
            hv = em.op("vector", lambda e, tc=tc, sltc=sltc, bk=bk: e.tensor_tensor(
                h_full[:, 1 + 512 * tc:1 + 512 * tc + 512], bank(p, bk, 128, 512),
                s2bc[:, sltc], op=ALU.mult),
                waits=[mt] + s2bt + edges + prev_conv)
            read_bank(p, bk, hv)
            mm_last.append(mt)
            ev_toks.append(hv)
        wsl_free[j % 2] = [mm_last[-1]]
        w0 = p.pv["ffn_convw"][:, 3 * j:3 * j + 1]
        w1a = p.pv["ffn_convw"][:, 3 * j + 1:3 * j + 2]
        w2a = p.pv["ffn_convw"][:, 3 * j + 2:3 * j + 3]
        c1 = em.op("vector", lambda e, w0=w0: e.tensor_scalar(
            p.scr[:, 0:2048], h_full[:, 0:2048], w0, None, op0=ALU.mult),
            waits=ev_toks + ([prev_silu] if prev_silu else []) + p.ctok)
        c2 = em.op("vector", lambda e, w1a=w1a: e.scalar_tensor_tensor(
            p.scr[:, 0:2048], h_full[:, 1:2049], w1a, p.scr[:, 0:2048],
            op0=ALU.mult, op1=ALU.add), waits=[c1])
        c3 = em.op("vector", lambda e, w2a=w2a: e.scalar_tensor_tensor(
            p.scr[:, 0:2048], h_full[:, 2:2050], w2a, p.scr[:, 0:2048],
            op0=ALU.mult, op1=ALU.add), waits=[c2])
        sl_t = em.op("scalar", lambda e, j=j: e.activation(
            hsil[:, j, :], p.scr[:, 0:2048], AF.Silu,
            bias=p.pv["ffn_convb"][:, j:j + 1]), waits=[c3])
        sl_f = em.op("scalar", lambda e, j=j: e.activation(
            h_full[:, 1:2049], p.scr[:, 0:2048], AF.Silu,
            bias=p.pv["ffn_convb"][:, j:j + 1]), waits=[c3])
        prev_conv = [c3, sl_f]
        prev_silu = sl_t
        for i in range(NT):
            tr = em.op("tensor", lambda e, j=j, i=i: e.transpose(
                bank(p, 2, 128, 128), h_full[:, 1 + 128 * i:1 + 128 * i + 128],
                ident(p)), waits=use_bank(p, 2, [sl_f]))
            if mx2_first:
                rd = em.op("vector", lambda e, i=i: e.reduce_max(
                    st[:, MX2 + i:MX2 + i + 1], bank(p, 2, 128, 128),
                    axis=AX.X, apply_absolute_value=True), waits=[tr])
            else:
                rd = em.op("vector", lambda e: e.reduce_max(
                    st[:, TMP + 1:TMP + 2], bank(p, 2, 128, 128), axis=AX.X,
                    apply_absolute_value=True), waits=[tr] + mx2_prev)
                rd = em.op("vector", lambda e, i=i: e.tensor_tensor(
                    st[:, MX2 + i:MX2 + i + 1], st[:, MX2 + i:MX2 + i + 1],
                    st[:, TMP + 1:TMP + 2], op=ALU.max), waits=[rd])
                mx2_prev = [rd]
            read_bank(p, 2, rd)
        mx2_first = False
        mx2_toks = [rd]
    mc2 = em.op("vector", lambda e: e.tensor_scalar(
        st[:, MX2:MX2 + 16], st[:, MX2:MX2 + 16], 1e-5, None, op0=ALU.max),
        waits=mx2_toks + mx2_prev)
    q1 = em.op("scalar", lambda e: e.activation(
        st[:, SC1:SC1 + 16], st[:, MX2:MX2 + 16], AF.Ln), waits=[mc2])
    q2 = em.op("scalar", lambda e: e.activation(
        st[:, SC1:SC1 + 16], st[:, SC1:SC1 + 16], AF.Exp, scale=-1.0,
        bias=p.cf[:, LN127_O:LN127_O + 1]), waits=[q1])
    rs2 = em.op("vector", lambda e: e.tensor_scalar(
        st[:, MX1:MX1 + 16], st[:, MX2:MX2 + 16], st[:, M2B:M2B + 1],
        1.0 / 127.0, op0=ALU.mult, op1=ALU.mult), waits=[mc2] + s2t)
    sc_bc2 = []
    rs_toks2 = []
    for i in range(NT):
        d1 = em.op("vector", lambda e, i=i: e.tensor_scalar(
            p.msks[:, :], ident(p), st[:, SC1 + i:SC1 + i + 1], None,
            op0=ALU.mult), waits=[q2] + (sc_bc2[-1:] if sc_bc2 else []))
        m1 = em.op("tensor", lambda e, i=i: e.matmul(
            bank(p, i // 4, 128, 128, coloff=128 * (i % 4)), ones(p, 128, 128),
            p.msks[:, :], start=(i % 4 == 0), stop=(i % 4 == 3),
            skip_group_check=True),
            waits=use_bank(p, i // 4, [d1]) if i % 4 == 0 else [d1])
        sc_bc2.append(m1)
        d2 = em.op("vector", lambda e, i=i: e.tensor_scalar(
            p.msks[:, :], ident(p), st[:, MX1 + i:MX1 + i + 1], None,
            op0=ALU.mult), waits=[rs2, m1])
        m2 = em.op("tensor", lambda e, i=i: e.matmul(
            bank(p, 4 + i // 4, 128, 128, coloff=128 * (i % 4)),
            ones(p, 128, 128), p.msks[:, :], start=(i % 4 == 0),
            stop=(i % 4 == 3), skip_group_check=True),
            waits=use_bank(p, 4 + i // 4, [d2]) if i % 4 == 0 else [d2])
        sc_bc2.append(m2)
        rs_toks2.append(m2)
    s2bt2 = []
    for b in range(4):
        t = em.op("vector", lambda e, b=b: e.tensor_copy(
            s2bc[:, 512 * b:512 * b + 512], bank(p, 4 + b, 128, 512)),
            waits=rs_toks2 + s2bt)
        read_bank(p, 4 + b, t)
        s2bt2.append(t)
    hq_toks = []
    for kc in range(32):
        for hf in range(2):
            hsl = slice(1024 * hf, 1024 * hf + 1024)
            v = em.op("vector", lambda e, kc=kc, hsl=hsl: e.tensor_tensor(
                p.scr[:, 0:1024], hsil[:, kc, hsl], p.PA[:, hsl], op=ALU.mult),
                waits=sc_bc2 + hq_toks[-1:])
            u = em.op("scalar", lambda e: e.activation(
                p.scr[:, 1024:2048], p.scr[:, 0:1024], AF.Identity,
                bias=p.cf[:, F512_O:F512_O + 1]), waits=[v])
            qi = em.op("vector", lambda e: e.tensor_copy(
                scri[:, 0:1024], p.scr[:, 1024:2048]), waits=[u])
            qf = em.op("scalar", lambda e: e.activation(
                p.scr[:, 0:1024], scri[:, 0:1024], AF.Identity), waits=[qi])
            cc = em.op("vector", lambda e: e.scalar_tensor_tensor(
                p.scrb[:, 0:1024], p.scr[:, 1024:2048], -0.5, p.scr[:, 0:1024],
                op0=ALU.add, op1=ALU.is_gt), waits=[qf])
            q = em.op("vector", lambda e, kc=kc, hsl=hsl: e.scalar_tensor_tensor(
                hsil[:, kc, hsl], p.scr[:, 0:1024], -512.0, p.scrb[:, 0:1024],
                op0=ALU.add, op1=ALU.add), waits=[cc])
            hq_toks.append(q)
    for b in range(4):
        read_bank(p, b, hq_toks[-1])

    ar.release()
    hw = [ar.alloc(f"hw{i}", [128, 16, 128], BF16) for i in range(3)]
    hw_free = [[], [], []]
    wout_toks = {tc: [] for tc in range(4)}
    res_ld = []
    scrh_free = {0: hq_toks[-1:], 1: hq_toks[-1:]}
    scr2_free = {0: hq_toks[-1:], 1: hq_toks[-1:]}
    for oc in range(NC8):
        ht = []
        for hh in range(2):
            s = (2 * oc + hh) % 3
            t = em.dma("gpsimd", lambda e, oc=oc, hh=hh, s=s: e.dma_start(
                out=hw[s][:], in_=dram["t2_scr"]
                [2048 * hh:2048 * hh + 2048, 128 * oc:128 * oc + 128]
                .rearrange("(c p) o -> p c o", p=128)),
                waits=hw_free[s] + (tn2[-1:] + hq_toks[-1:] if oc == 0 else []))
            ht.append(t)
        mm_by_tc = {}
        for hh in range(2):
            s = (2 * oc + hh) % 3
            last_mm = None
            for tc in range(4):
                sltc = slice(512 * tc, 512 * tc + 512)
                bk = (4 * oc + tc) % 8
                for c16 in range(16):
                    kc = 16 * hh + c16
                    w0 = (hh == 0 and c16 == 0)
                    mm = em.op("tensor", lambda e, s=s, c16=c16, kc=kc, bk=bk,
                               sltc=sltc, w0=w0: e.matmul(
                        bank(p, bk, 128, 512), hw[s][:, c16, :],
                        hsil[:, kc, sltc], start=w0,
                        stop=(kc == 31 and c16 == 15), skip_group_check=True),
                        waits=(use_bank(p, bk, [ht[hh]] + hq_toks[-1:])
                               if w0 else [ht[hh]]))
                    last_mm = mm
                mm_by_tc[tc] = mm
            hw_free[s] = [last_mm]
        for tc in range(4):
            sltc = slice(512 * tc, 512 * tc + 512)
            bk = (4 * oc + tc) % 8
            sh = 512 * (tc % 2)
            rx = em.dma("sync", lambda e, oc=oc, sltc=sltc, tc=tc,
                        xh=p.x_hbm: e.dma_start(
                out=p.scr2[:, 512 * (tc % 2):512 * (tc % 2) + 512],
                in_=xh[128 * oc:128 * oc + 128, sltc]),
                waits=res_ld[-1:] + scr2_free.get(tc % 2, []))
            res_ld.append(rx)
            o1 = em.op("vector", lambda e, bk=bk, sltc=sltc, sh=sh: e.tensor_tensor(
                p.scr[:, sh:sh + 512], bank(p, bk, 128, 512), s2bc[:, sltc],
                op=ALU.mult),
                waits=[mm_by_tc[tc]] + s2bt2 + scrh_free.get(tc % 2, []))
            o2 = em.op("vector", lambda e, sh=sh, tc=tc: e.tensor_tensor(
                p.scr[:, sh:sh + 512], p.scr[:, sh:sh + 512],
                p.scr2[:, 512 * (tc % 2):512 * (tc % 2) + 512], op=ALU.add),
                waits=[o1, rx])
            read_bank(p, bk, o2)
            scr2_free[tc % 2] = [o2]
            wd = em.dma("scalar", lambda e, oc=oc, sltc=sltc, sh=sh: e.dma_start(
                out=dram["spill_x2"][128 * oc:128 * oc + 128, sltc],
                in_=p.scr[:, sh:sh + 512]), waits=[o2])
            scrh_free[tc % 2] = [wd]
            wout_toks[tc].append(wd)

    ar.release()
    return postln_stream(p, wout_toks, is_last, dram["spill_x"])


def postln_stream(p, wout_toks, is_last, new_xhbm):
    em = p.em
    ar = p.ar
    dram = p.dram
    ar.mark()
    stage16 = ar.alloc("stage16", [128, NC8, 512], F32)
    out_toks = []
    st_free = []
    for tc in range(4):
        sltc = slice(512 * tc, 512 * tc + 512)
        ld = em.dma("sync", lambda e, sltc=sltc: e.dma_start(
            out=stage16[:], in_=dram["spill_x2"][:, sltc]
            .rearrange("(c p) t -> p c t", p=128)),
            waits=wout_toks[tc] + st_free)
        lnt = ln_fm(p, lambda c: stage16[:, c, :],
                    lambda c: stage16[:, c, :],
                    lambda c: p.x_bf[:, c, :], [ld],
                    blocks=[(slice(0, 512), slice(0, 512), sltc)])
        dst = dram["y"] if is_last else new_xhbm
        wv = em.dma("sync", lambda e, dst=dst, sltc=sltc: e.dma_start(
            out=dst[:, sltc].rearrange("(c p) t -> p c t", p=128),
            in_=stage16[:]), waits=lnt)
        st_free = [wv] + lnt
        out_toks += [wv] + lnt
    ar.release()
    if not is_last:
        p.x_hbm = new_xhbm
    return out_toks


def entry_w(p, entry):
    return p.ctok


def emit_l1_attn(p, entry, is_last=False):
    em = p.em
    ar = p.ar
    dram = p.dram
    ar.mark()
    latq = ar.alloc("latq", [128, NC8, T], BF16)
    latk = ar.alloc("latk", [128, NC8, T], BF16)
    vtm_all = ar.alloc("vtm_all", [128, NH, NT, 64], BF16)
    Qaug = ar.alloc("l1Q", [128, T], BF16)
    Kaug = ar.alloc("l1K", [128, T], BF16)
    Ebuf = ar.alloc("l1E", [128, 512], BF16)
    ypair = ar.alloc("ypair", [128, T], BF16)
    ropec = ar.alloc("ropec", [128, T], F32)
    upk = ar.alloc("upk", [128, 320], BF16)
    smallf = ar.alloc("smallf", [128, 64], F32)
    wds = [ar.alloc(f"l1wd{i}", [128, 8, 128], BF16) for i in range(2)]

    rc = em.dma("sync", lambda e: e.dma_start(
        out=ropec[:], in_=dram["constf"][:, COS_O:COS_O + T]), waits=entry)
    ko = em.op("vector", lambda e: e.memset(Kaug[64:66, :], 1.0), waits=entry)

    upk_toks = []
    for off, nmc, nme in ((0, "wq_upc", "wq_upe"), (128, "wk_upc", "wk_upe")):
        l1 = em.dma("sync", lambda e, nmc=nmc: e.dma_start(
            out=smallf[0:64, 0:32], in_=dram[nmc][:, :]), waits=upk_toks[-1:])
        l2 = em.dma("sync", lambda e, nmc=nmc: e.dma_start(
            out=smallf[64:128, 0:32], in_=dram[nmc][:, :]))
        l3 = em.dma("sync", lambda e, nme=nme: e.dma_start(
            out=smallf[0:64, 32:64], in_=dram[nme][:, :]))
        l4 = em.dma("sync", lambda e, nme=nme: e.dma_start(
            out=smallf[64:128, 32:64], in_=dram[nme][:, :]))
        lds = [l1, l2, l3, l4]
        t1 = em.op("vector", lambda e, off=off: e.tensor_copy(
            upk[:, off:off + 64], smallf[:, 0:64]), waits=lds)
        t2 = em.op("vector", lambda e, off=off: e.memset(
            upk[:, off + 64:off + 96], 0.0), waits=entry)
        t3 = em.op("vector", lambda e, off=off: e.tensor_scalar(
            upk[:, off + 96:off + 128:2], smallf[:, 33:64:2], -1.0, None,
            op0=ALU.mult), waits=lds)
        t4 = em.op("vector", lambda e, off=off: e.tensor_copy(
            upk[:, off + 97:off + 128:2], smallf[:, 32:64:2]), waits=lds)
        upk_toks += [t1, t2, t3, t4]
    l5 = em.dma("sync", lambda e: e.dma_start(
        out=smallf[0:64, 0:64], in_=dram["wv_up"][:, :]), waits=upk_toks)
    l6 = em.dma("sync", lambda e: e.dma_start(
        out=smallf[64:128, 0:64], in_=dram["wv_up"][:, :]))
    t5 = em.op("vector", lambda e: e.tensor_copy(
        upk[:, 256:320], smallf[:, 0:64]), waits=[l5, l6])
    upk_toks.append(t5)

    def produce_lat(dst, wname, deps):
        toks = []
        wfree = [[], []]
        for oc in range(NC8):
            wt = em.dma("gpsimd", lambda e, oc=oc, wname=wname: e.dma_start(
                out=wds[oc % 2][:], in_=dram[wname][:, 128 * oc:128 * oc + 128]
                .rearrange("(c p) j -> p c j", p=128)),
                waits=wfree[oc % 2] + (deps if oc < 2 else []))
            for tc in range(4):
                sltc = slice(512 * tc, 512 * tc + 512)
                bk = tc % 2
                mt = None
                for c in range(NC8):
                    mt = em.op("tensor", lambda e, oc=oc, c=c, sltc=sltc, bk=bk:
                               e.matmul(bank(p, bk, 128, 512),
                                        wds[oc % 2][:, c, :], p.x_bf[:, c, sltc],
                                        start=(c == 0), stop=(c == NC8 - 1),
                                        skip_group_check=True),
                               waits=use_bank(p, bk, [wt] + deps)
                               if c == 0 else [wt])
                ev = em.op("vector", lambda e, oc=oc, sltc=sltc, bk=bk, dst=dst:
                           e.tensor_copy(dst[:, oc, sltc], bank(p, bk, 128, 512)),
                           waits=[mt])
                read_bank(p, bk, ev)
                toks.append(ev)
            wfree[oc % 2] = toks[-4:]
        return toks

    lv = produce_lat(latq, "wv_down", list(entry) + p.ctok)
    vc_toks = []
    for h in range(NH):
        base = 64 * (h % 2)
        for kt in range(NT):
            mm = em.op("tensor", lambda e, h=h, kt=kt, base=base: e.matmul(
                bank(p, 2, 128, 64), latq[base:base + 64, h // 2,
                                          128 * kt:128 * kt + 128],
                upk[base:base + 64, 256:320], start=True, stop=True),
                waits=use_bank(p, 2, lv + upk_toks))
            ev = em.op("vector", lambda e, h=h, kt=kt: e.tensor_copy(
                vtm_all[:, h, kt, :], bank(p, 2, 128, 64)), waits=[mm])
            read_bank(p, 2, ev)
            vc_toks.append(ev)
    lq = produce_lat(latq, "wq_down", vc_toks[-1:])
    lk = produce_lat(latk, "wk_down", vc_toks[-1:])
    prod_deps = lq + lk

    ypair_free = []
    youts = []
    prev_attn = []
    for h in range(NH):
        base = 64 * (h % 2)
        bld = []
        for dst, lat, off in ((Qaug, latq, 0), (Kaug, latk, 128)):
            for tc in range(4):
                sltc = slice(512 * tc, 512 * tc + 512)
                m1 = em.op("tensor", lambda e, h=h, base=base, off=off,
                           sltc=sltc, lat=lat: e.matmul(
                    bank(p, 0, 128, 512), upk[base:base + 64, off:off + 128],
                    lat[base:base + 64, h // 2, sltc], start=True, stop=True),
                    waits=use_bank(p, 0, prod_deps + upk_toks + prev_attn))
                te = em.op("vector", lambda e, sltc=sltc: e.tensor_tensor(
                    Ebuf[:, :], bank(p, 0, 128, 512), ropec[:, sltc],
                    op=ALU.mult), waits=[m1, rc])
                read_bank(p, 0, te)
                m2 = em.op("tensor", lambda e: e.matmul(
                    bank(p, 1, 64, 512), p.cb[:, FOLD_O:FOLD_O + 64],
                    Ebuf[:, :], start=True, stop=True),
                    waits=use_bank(p, 1, [te]))
                ev = em.op("vector", lambda e, dst=dst, sltc=sltc: e.tensor_copy(
                    dst[0:64, sltc], bank(p, 1, 64, 512)), waits=[m2])
                read_bank(p, 1, ev)
                bld.append(ev)
        ot, _ = attn_head(p, h, Qaug, Kaug, vtm_all[:, h], ypair,
                          bld + [ko] + ypair_free + vc_toks[-1:])
        prev_attn = [ot]
        if h % 2 == 1:
            yw = em.dma("gpsimd", lambda e, h=h: e.dma_start(
                out=dram["y_scr"][64 * (h - 1):64 * (h - 1) + 128, :],
                in_=ypair[:, :]), waits=[ot] + youts[-1:])
            youts.append(yw)
            ypair_free = [yw]
    ar.release()
    ar.mark()
    wc_all = ar.alloc("wc_all", [128, NC8, NC8, 128], BF16)
    ystage = [ar.alloc(f"yst{i}", [128, NC8, 512], BF16) for i in range(2)]
    wcl = em.dma("gpsimd", lambda e: e.dma_start(
        out=wc_all[:], in_=dram["wc"][:, :]
        .rearrange("(c p) (d j) -> p c d j", p=128, j=128)),
        waits=youts[-1:] + prev_attn)
    wout_toks = {tc: [] for tc in range(4)}
    yst_free = [[], []]
    res_ld = []
    scr2_free = {}
    scrh_free = {}
    for tc in range(4):
        sltc = slice(512 * tc, 512 * tc + 512)
        yl = em.dma("gpsimd", lambda e, tc=tc, sltc=sltc: e.dma_start(
            out=ystage[tc % 2][:], in_=dram["y_scr"][:, sltc]
            .rearrange("(c p) t -> p c t", p=128)),
            waits=yst_free[tc % 2] + youts[-1:])
        for oc in range(NC8):
            bk = oc
            mt = None
            for d in range(NC8):
                mt = em.op("tensor", lambda e, oc=oc, d=d, tc=tc, bk=bk: e.matmul(
                    bank(p, bk, 128, 512), wc_all[:, d, oc, :],
                    ystage[tc % 2][:, d, :], start=(d == 0), stop=(d == NC8 - 1),
                    skip_group_check=True),
                    waits=use_bank(p, bk, [yl, wcl]) if d == 0 else [yl])
            sh = 512 * (oc % 2)
            rx = em.dma("sync", lambda e, oc=oc, sltc=sltc, xh=p.x_hbm: e.dma_start(
                out=p.scr2[:, 512 * (oc % 2):512 * (oc % 2) + 512],
                in_=xh[128 * oc:128 * oc + 128, sltc]),
                waits=res_ld[-1:] + scr2_free.get(oc % 2, []))
            res_ld.append(rx)
            o1 = em.op("vector", lambda e, bk=bk, sh=sh, oc=oc: e.tensor_tensor(
                p.scr[:, sh:sh + 512], bank(p, bk, 128, 512),
                p.scr2[:, 512 * (oc % 2):512 * (oc % 2) + 512], op=ALU.add),
                waits=[mt, rx] + scrh_free.get(oc % 2, []))
            read_bank(p, bk, o1)
            scr2_free[oc % 2] = [o1]
            wd = em.dma("scalar", lambda e, oc=oc, sltc=sltc, sh=sh: e.dma_start(
                out=dram["spill_x2"][128 * oc:128 * oc + 128, sltc],
                in_=p.scr[:, sh:sh + 512]), waits=[o1])
            scrh_free[oc % 2] = [wd]
            wout_toks[tc].append(wd)
        yst_free[tc % 2] = wout_toks[tc][-NC8:]
    ar.release()
    return postln_stream(p, wout_toks, is_last, dram["spill_x"])


def emit_l1_ffn(p, entry, is_last=True):
    em = p.em
    ar = p.ar
    dram = p.dram
    BIG = 1.0e6
    ar.mark()
    yacc = ar.alloc("yacc", [128, NC8, T], F32)
    Wte = ar.alloc("Wte", [128, NT, 32], F32)

    ar.mark()
    xr = ar.alloc("xr", [128, T], F32)
    sfm = ar.alloc("sfm", [32, T], F32)
    sfcT = ar.alloc("sfcT", [128, NT, 32], F32)
    scT = ar.alloc("scT", [128, NT, 32], F32)
    tmpv = ar.alloc("tmpv", [128, NT, 32], F32)
    iot = ar.alloc("iot", [128, NT, 32], F32)
    eqb = ar.alloc("eqb", [128, NT, 32], F32)
    fms = [ar.alloc(f"fm{k}", [128, NT, 32], F32) for k in range(4)]
    g8 = ar.alloc("g8", [128, NT, 8], F32)
    g4 = [ar.alloc(f"g4_{i}", [128, NT, 4], F32) for i in range(5)]
    r16 = ar.alloc("r16", [128, NT, 8], F32)
    rwt = ar.alloc("rwt", [128, NC8, 32], F32)

    rwl = em.dma("sync", lambda e: e.dma_start(
        out=rwt[:], in_=dram["rw_t"][:, :].rearrange("(c p) e -> p c e", p=128)),
        waits=entry)
    lg_mm = {}
    ld_prev = []
    for c in range(NC8):
        ld = em.dma("sync", lambda e, c=c, xh=p.x_hbm: e.dma_start(
            out=xr[:, :], in_=xh[128 * c:128 * c + 128, :]),
            waits=ld_prev + entry)
        for tc in range(4):
            sltc = slice(512 * tc, 512 * tc + 512)
            lg_mm[tc] = em.op("tensor", lambda e, c=c, tc=tc, sltc=sltc: e.matmul(
                bank(p, tc, 32, 512), rwt[:, c, :], xr[:, sltc],
                start=(c == 0), stop=(c == NC8 - 1), skip_group_check=True),
                waits=use_bank(p, tc, [ld, rwl]) if c == 0 else [ld, rwl])
        ld_prev = [lg_mm[3]]
    sg_toks = []
    for tc in range(4):
        t = em.op("scalar", lambda e, tc=tc: e.activation(
            sfm[0:32, 512 * tc:512 * tc + 512], bank(p, tc, 32, 512),
            AF.Sigmoid), waits=[lg_mm[tc]])
        read_bank(p, tc, t)
        sg_toks.append(t)
    bb1 = em.op("tensor", lambda e: e.matmul(
        bank(p, 5, 128, 32), ones(p, 1, 128),
        p.pv["r_bias"][0:1, 0:32], start=True,
        stop=True), waits=use_bank(p, 5, p.ctok))
    tr_toks = []
    for i in range(NT):
        t1 = em.op("tensor", lambda e, i=i: e.transpose(
            bank(p, 4, 128, 32), sfm[0:32, 128 * i:128 * i + 128],
            ident(p)[0:32, 0:32]), waits=use_bank(p, 4, sg_toks + p.ctok))
        e1 = em.op("vector", lambda e, i=i: e.tensor_copy(
            scT[:, i, :], bank(p, 4, 128, 32)), waits=[t1])
        read_bank(p, 4, e1)
        e2 = em.op("vector", lambda e, i=i: e.tensor_tensor(
            sfcT[:, i, :], scT[:, i, :], bank(p, 5, 128, 32), op=ALU.add),
            waits=[e1, bb1])
        tr_toks.append(e2)
    read_bank(p, 5, tr_toks[-1])
    gm, cnt4, m24, gsum, rank4 = g4
    gt_toks = []
    for g in range(4):
        sl3 = (slice(None), slice(None), slice(8 * g, 8 * g + 8))
        m1 = em.op("vector", lambda e, g=g, sl3=sl3: e.reduce_max(
            gm[:, :, g:g + 1], sfcT[sl3], axis=AX.X), waits=tr_toks)
        eq = em.op("vector", lambda e, g=g, sl3=sl3: e.tensor_tensor(
            g8[:, :, :], sfcT[sl3],
            gm[:, :, g:g + 1].to_broadcast([128, NT, 8]), op=ALU.is_equal),
            waits=[m1] + gt_toks[-1:])
        ct = em.op("vector", lambda e, g=g: e.reduce_sum(
            cnt4[:, :, g:g + 1], g8[:, :, :], axis=AX.X), waits=[eq])
        mk = em.op("vector", lambda e, g=g, sl3=sl3: e.scalar_tensor_tensor(
            r16[:, :, :], g8[:, :, :], -BIG, sfcT[sl3], op0=ALU.mult,
            op1=ALU.add), waits=[eq])
        m2 = em.op("vector", lambda e, g=g: e.reduce_max(
            m24[:, :, g:g + 1], r16[:, :, :], axis=AX.X), waits=[mk])
        gt_toks += [ct, m2]
    s2 = em.op("vector", lambda e: e.tensor_scalar(
        cnt4[:, :, :], cnt4[:, :, :], 1.5, None, op0=ALU.is_gt), waits=gt_toks)
    d1 = em.op("vector", lambda e: e.tensor_tensor(
        rank4[:, :, :], gm[:, :, :], m24[:, :, :], op=ALU.subtract), waits=[s2])
    d2 = em.op("vector", lambda e: e.tensor_tensor(
        rank4[:, :, :], rank4[:, :, :], cnt4[:, :, :], op=ALU.mult), waits=[d1])
    d3 = em.op("vector", lambda e: e.tensor_tensor(
        gsum[:, :, :], gm[:, :, :], m24[:, :, :], op=ALU.add), waits=[d2])
    d4 = em.op("vector", lambda e: e.tensor_tensor(
        gsum[:, :, :], gsum[:, :, :], rank4[:, :, :], op=ALU.add), waits=[d3])
    z0 = em.op("vector", lambda e: e.memset(rank4[:, :, :], 0.0), waits=[d4])
    rk_toks = [z0]
    for g in range(4):
        for j in range(4):
            if j == g:
                continue
            op = ALU.is_ge if j < g else ALU.is_gt
            c1 = em.op("vector", lambda e, g=g, j=j, op=op: e.tensor_tensor(
                g8[:, :, 0:1], gsum[:, :, j:j + 1], gsum[:, :, g:g + 1], op=op),
                waits=rk_toks[-1:])
            c2 = em.op("vector", lambda e, g=g: e.tensor_tensor(
                rank4[:, :, g:g + 1], rank4[:, :, g:g + 1], g8[:, :, 0:1],
                op=ALU.add), waits=[c1])
            rk_toks.append(c2)
    sel = em.op("vector", lambda e: e.tensor_scalar(
        rank4[:, :, :], rank4[:, :, :], 1.5, None, op0=ALU.is_lt),
        waits=rk_toks)
    tv_toks = []
    for g in range(4):
        sl3 = (slice(None), slice(None), slice(8 * g, 8 * g + 8))
        t = em.op("vector", lambda e, g=g, sl3=sl3: e.tensor_tensor(
            tmpv[sl3], sfcT[sl3],
            rank4[:, :, g:g + 1].to_broadcast([128, NT, 8]), op=ALU.mult),
            waits=[sel])
        tv_toks.append(t)
    io_toks = []
    for ei in range(32):
        io_toks.append(em.op("vector", lambda e, ei=ei: e.memset(
            iot[:, :, ei:ei + 1], float(ei)), waits=entry))
    wtk = ar.alloc("wtk", [128, NT, 8], F32)
    prev = tv_toks + io_toks
    for k in range(4):
        mk = em.op("vector", lambda e: e.reduce_max(
            wtk[:, :, 4:5], tmpv[:, :, :], axis=AX.X), waits=prev)
        eq = em.op("vector", lambda e: e.tensor_tensor(
            eqb[:, :, :], tmpv[:, :, :],
            wtk[:, :, 4:5].to_broadcast([128, NT, 32]), op=ALU.is_equal),
            waits=[mk])
        mi = em.op("vector", lambda e, k=k: e.scalar_tensor_tensor(
            fms[k][:, :, :], eqb[:, :, :], -BIG, iot[:, :, :], op0=ALU.mult,
            op1=ALU.add), waits=[eq])
        rm = em.op("vector", lambda e, k=k: e.tensor_reduce(
            wtk[:, :, 5:6], fms[k][:, :, :], axis=AX.X, op=ALU.min), waits=[mi])
        ra = em.op("vector", lambda e: e.tensor_scalar(
            wtk[:, :, 5:6], wtk[:, :, 5:6], BIG, None, op0=ALU.add), waits=[rm])
        fm = em.op("vector", lambda e, k=k: e.tensor_tensor(
            fms[k][:, :, :], iot[:, :, :],
            wtk[:, :, 5:6].to_broadcast([128, NT, 32]), op=ALU.is_equal),
            waits=[ra])
        wv = em.op("vector", lambda e, k=k: e.tensor_tensor(
            eqb[:, :, :], fms[k][:, :, :], scT[:, :, :], op=ALU.mult),
            waits=[fm])
        ws = em.op("vector", lambda e, k=k: e.reduce_sum(
            wtk[:, :, k:k + 1], eqb[:, :, :], axis=AX.X), waits=[wv])
        rmv = em.op("vector", lambda e, k=k: e.scalar_tensor_tensor(
            tmpv[:, :, :], fms[k][:, :, :], -BIG, tmpv[:, :, :], op0=ALU.mult,
            op1=ALU.add), waits=[ws])
        prev = [rmv]
    n1 = em.op("vector", lambda e: e.tensor_tensor(
        wtk[:, :, 6:7], wtk[:, :, 0:1], wtk[:, :, 1:2], op=ALU.add), waits=prev)
    n2 = em.op("vector", lambda e: e.tensor_tensor(
        wtk[:, :, 7:8], wtk[:, :, 2:3], wtk[:, :, 3:4], op=ALU.add), waits=[n1])
    n3 = em.op("vector", lambda e: e.tensor_tensor(
        wtk[:, :, 6:7], wtk[:, :, 6:7], wtk[:, :, 7:8], op=ALU.add), waits=[n2])
    n4 = em.op("vector", lambda e: e.tensor_scalar(
        wtk[:, :, 6:7], wtk[:, :, 6:7], 1e-20, None, op0=ALU.add), waits=[n3])
    n5 = em.op("scalar", lambda e: e.activation(
        wtk[:, :, 7:8], wtk[:, :, 6:7], AF.Ln), waits=[n4])
    n6 = em.op("scalar", lambda e: e.activation(
        wtk[:, :, 7:8], wtk[:, :, 7:8], AF.Exp, scale=-1.0,
        bias=p.cf[:, LN25_O:LN25_O + 1]), waits=[n5])
    wt_toks = []
    for k in range(4):
        t1 = em.op("vector", lambda e, k=k: e.tensor_tensor(
            wtk[:, :, 4:5], wtk[:, :, k:k + 1], wtk[:, :, 7:8], op=ALU.mult),
            waits=[n6] + wt_toks[-1:])
        t2 = em.op("vector", lambda e, k=k: e.tensor_tensor(
            eqb[:, :, :], fms[k][:, :, :],
            wtk[:, :, 4:5].to_broadcast([128, NT, 32]), op=ALU.mult),
            waits=[t1])
        if k == 0:
            t3 = em.op("vector", lambda e: e.tensor_copy(
                Wte[:, :, :], eqb[:, :, :]), waits=[t2])
        else:
            t3 = em.op("vector", lambda e: e.tensor_tensor(
                Wte[:, :, :], Wte[:, :, :], eqb[:, :, :], op=ALU.add),
                waits=[t2])
        wt_toks.append(t3)
    route_done = wt_toks[-1:]
    ar.release()

    ar.mark()
    gsl = [ar.alloc(f"gsl{i}", [128, NC8, 512], BF16) for i in range(2)]
    usl = [ar.alloc(f"usl{i}", [128, NC8, 512], BF16) for i in range(2)]
    dsl = [ar.alloc(f"dsl{i}", [128, 4, 1024], BF16) for i in range(2)]
    hb = ar.alloc("hb", [128, 4, T], BF16)
    wbc = ar.alloc("wbc", [128, T], F32)
    stgu = ar.alloc("stgu", [128, 2, 512], F32)
    stgd = ar.alloc("stgd", [128, 1, 1024], F32)
    NE = 34
    slot_free = [route_done, route_done]
    hb_free = []
    wbc_free = []
    yacc_toks = []
    for e_ in range(NE):
        shared = e_ >= 32
        shi = e_ - 32
        s = e_ % 2
        if not shared:
            gsrc = dram["moe_gate"][e_].rearrange("(c p) i -> p c i", p=128)
            usrc = dram["moe_up"][e_].rearrange("(c p) i -> p c i", p=128)
            dsrc = dram["moe_down"][e_].rearrange("(ic p) o -> p ic o", p=128)
        else:
            gsrc = dram["sh_gate"][:, 512 * shi:512 * shi + 512] \
                .rearrange("(c p) i -> p c i", p=128)
            usrc = dram["sh_up"][:, 512 * shi:512 * shi + 512] \
                .rearrange("(c p) i -> p c i", p=128)
            dsrc = dram["sh_down"][512 * shi:512 * shi + 512, :] \
                .rearrange("(ic p) o -> p ic o", p=128)
        gt = em.dma("gpsimd", lambda e, s=s, gsrc=gsrc: e.dma_start(
            out=gsl[s][:], in_=gsrc), waits=slot_free[s])
        ucp = getattr(p, "_moe_ucp", [])
        dcp = getattr(p, "_moe_dcp", [])
        if not shared:
            upar = dram["moe_up"][e_]
            dnar = dram["moe_down"][e_]
        else:
            upar = dram["sh_up"][:, 512 * shi:512 * shi + 512]
            dnar = dram["sh_down"][512 * shi:512 * shi + 512, :]
        for qtr in range(4):
            uld = em.dma("scalar", lambda e, upar=upar, qtr=qtr: e.dma_start(
                out=stgu[:], in_=upar[256 * qtr:256 * qtr + 256, :]
                .rearrange("(c p) i -> p c i", p=128)), waits=ucp)
            ucp = [em.op("vector", lambda e, s=s, qtr=qtr: e.tensor_copy(
                usl[s][:, 2 * qtr:2 * qtr + 2, :], stgu[:]),
                waits=[uld] + (slot_free[s] if qtr == 0 else []))]
            dld = em.dma("sync", lambda e, dnar=dnar, qtr=qtr: e.dma_start(
                out=stgd[:], in_=dnar[128 * qtr:128 * qtr + 128, :]
                .rearrange("(c p) o -> p c o", p=128)), waits=dcp)
            dcp = [em.op("vector", lambda e, s=s, qtr=qtr: e.tensor_copy(
                dsl[s][:, qtr:qtr + 1, :], stgd[:]),
                waits=[dld] + (slot_free[s] if qtr == 0 else []))]
        p._moe_ucp = ucp
        p._moe_dcp = dcp
        ut = ucp[0]
        dt = dcp[0]
        wb_toks = []
        if not shared:
            bts = []
            for i in range(NT):
                bb = 4 + i // 4
                dg = em.op("vector", lambda e, i=i, e_=e_: e.tensor_scalar(
                    p.msks[:, :], ident(p), Wte[:, i, e_:e_ + 1],
                    None, op0=ALU.mult), waits=route_done + bts[-1:])
                bm = em.op("tensor", lambda e, i=i, bb=bb: e.matmul(
                    bank(p, bb, 128, 128, coloff=128 * (i % 4)),
                    ones(p, 128, 128), p.msks[:, :], start=(i % 4 == 0),
                    stop=(i % 4 == 3), skip_group_check=True),
                    waits=use_bank(p, bb, [dg]) if i % 4 == 0 else [dg])
                bts.append(bm)
            for b in range(4):
                t = em.op("vector", lambda e, b=b: e.tensor_copy(
                    wbc[:, 512 * b:512 * b + 512], bank(p, 4 + b, 128, 512)),
                    waits=bts + wbc_free)
                read_bank(p, 4 + b, t)
                wb_toks.append(t)
        hb_toks = []
        t1_prev = getattr(p, "_moe_t1_prev", {})
        for tc in range(4):
            sltc = slice(512 * tc, 512 * tc + 512)
            for ic in range(4):
                ba = ic % 2
                bu = 2 + ic % 2
                mg = None
                mu = None
                for c in range(NC8):
                    mg = em.op("tensor", lambda e, s=s, ic=ic, c=c, sltc=sltc,
                               ba=ba: e.matmul(
                        bank(p, ba, 128, 512),
                        gsl[s][:, c, 128 * ic:128 * ic + 128],
                        p.x_bf[:, c, sltc], start=(c == 0), stop=(c == NC8 - 1),
                        skip_group_check=True),
                        waits=use_bank(p, ba, [gt]) if c == 0 else [gt])
                for c in range(NC8):
                    mu = em.op("tensor", lambda e, s=s, ic=ic, c=c, sltc=sltc,
                               bu=bu: e.matmul(
                        bank(p, bu, 128, 512),
                        usl[s][:, c, 128 * ic:128 * ic + 128],
                        p.x_bf[:, c, sltc], start=(c == 0), stop=(c == NC8 - 1),
                        skip_group_check=True),
                        waits=use_bank(p, bu, [ut]) if c == 0 else [ut])
                sh0 = 512 * (ic % 2)
                sv = em.op("scalar", lambda e, ba=ba, sh0=sh0: e.activation(
                    p.scr[:, sh0:sh0 + 512], bank(p, ba, 128, 512), AF.Silu),
                    waits=[mg] + t1_prev.get(ic % 2, []))
                read_bank(p, ba, sv)
                t1 = em.op("vector", lambda e, bu=bu, sh0=sh0: e.tensor_tensor(
                    p.scr[:, 1024 + sh0:1024 + sh0 + 512],
                    p.scr[:, sh0:sh0 + 512], bank(p, bu, 128, 512),
                    op=ALU.mult), waits=[sv, mu])
                read_bank(p, bu, t1)
                t1_prev[ic % 2] = [t1]
                if shared:
                    t2 = em.op("vector", lambda e, ic=ic, sltc=sltc, sh0=sh0:
                               e.tensor_copy(
                        hb[:, ic, sltc], p.scr[:, 1024 + sh0:1024 + sh0 + 512]),
                        waits=[t1] + hb_free)
                else:
                    t2 = em.op("vector", lambda e, ic=ic, sltc=sltc, sh0=sh0:
                               e.tensor_tensor(
                        hb[:, ic, sltc], p.scr[:, 1024 + sh0:1024 + sh0 + 512],
                        wbc[:, sltc], op=ALU.mult),
                        waits=[t1] + wb_toks + hb_free)
                hb_toks.append(t2)
        dn_toks = []
        for tc in range(4):
            sltc = slice(512 * tc, 512 * tc + 512)
            for oc in range(NC8):
                bk = 4 + (4 * tc + oc) % 4
                md = None
                for ic in range(4):
                    md = em.op("tensor", lambda e, s=s, ic=ic, oc=oc, sltc=sltc,
                               bk=bk: e.matmul(
                        bank(p, bk, 128, 512),
                        dsl[s][:, ic, 128 * oc:128 * oc + 128],
                        hb[:, ic, sltc], start=(ic == 0), stop=(ic == 3),
                        skip_group_check=True),
                        waits=use_bank(p, bk, [dt] + hb_toks) if ic == 0
                        else [dt])
                if e_ == 0:
                    ya = em.op("vector", lambda e, oc=oc, sltc=sltc, bk=bk:
                               e.tensor_copy(
                        yacc[:, oc, sltc], bank(p, bk, 128, 512)), waits=[md])
                else:
                    ya = em.op("vector", lambda e, oc=oc, sltc=sltc, bk=bk:
                               e.tensor_tensor(
                        yacc[:, oc, sltc], yacc[:, oc, sltc],
                        bank(p, bk, 128, 512), op=ALU.add),
                        waits=[md] + yacc_toks[-1:])
                read_bank(p, bk, ya)
                dn_toks.append(ya)
        yacc_toks = dn_toks
        slot_free[s] = dn_toks[-4:]
        hb_free = dn_toks[-4:]
        wbc_free = hb_toks[-1:]
        p._moe_t1_prev = t1_prev
    wout_toks = {tc: [] for tc in range(4)}
    res_ld = []
    scr2_free = {}
    scrh_free = {}
    for tc in range(4):
        sltc = slice(512 * tc, 512 * tc + 512)
        for oc in range(NC8):
            sh0 = 512 * (oc % 2)
            rx = em.dma("sync", lambda e, oc=oc, sltc=sltc, xh=p.x_hbm:
                        e.dma_start(
                out=p.scr2[:, 512 * (oc % 2):512 * (oc % 2) + 512],
                in_=xh[128 * oc:128 * oc + 128, sltc]),
                waits=res_ld[-1:] + scr2_free.get(oc % 2, []))
            res_ld.append(rx)
            o1 = em.op("vector", lambda e, oc=oc, sltc=sltc, sh0=sh0:
                       e.tensor_tensor(
                p.scr[:, sh0:sh0 + 512], yacc[:, oc, sltc],
                p.scr2[:, 512 * (oc % 2):512 * (oc % 2) + 512], op=ALU.add),
                waits=[rx] + yacc_toks[-1:] + scrh_free.get(oc % 2, []))
            scr2_free[oc % 2] = [o1]
            wd = em.dma("scalar", lambda e, oc=oc, sltc=sltc, sh0=sh0:
                        e.dma_start(
                out=dram["spill_x2"][128 * oc:128 * oc + 128, sltc],
                in_=p.scr[:, sh0:sh0 + 512]), waits=[o1])
            scrh_free[oc % 2] = [wd]
            wout_toks[tc].append(wd)
    ar.release()
    return postln_stream(p, wout_toks, is_last, dram["spill_x"])


def build(stage="l0attn", dbg=None, tap=None):
    p = mkprog(stage, dbg)
    p.tap = tap
    p.tap_done = False
    p.tap_guard = []
    em = p.em
    if stage == "l0attn":
        emit_l0_attn(p, is_last=True)
    elif stage == "l0ffn":
        t = emit_l0_attn(p, is_last=False)
        emit_l0_ffn(p, t, is_last=True)
    elif stage == "l1attn":
        t = emit_l0_attn(p, is_last=False)
        t = emit_l0_ffn(p, t, is_last=False)
        emit_l1_attn(p, t, is_last=True)
    elif stage in ("l1ffn", "full"):
        t = emit_l0_attn(p, is_last=False)
        t = emit_l0_ffn(p, t, is_last=False)
        t = emit_l1_attn(p, t, is_last=False)
        emit_l1_ffn(p, t, is_last=True)
    else:
        raise NotImplementedError(stage)

    def fin(e):
        for s, c in em.all_tok():
            e.wait_ge(em.sems[s], c)
        e.nop()
    em.build(p.block)
    p.block.sync(lambda e: fin(e))
    p.stack.close()
    globals()["_last_p"] = p
    return p.nc


def kernel(**inputs):
    nc = build(stage="full")
    in_maps = [host_inputs(core, inputs) for core in range(NCORES)]
    res = run_bass_kernel_spmd(nc, in_maps, core_ids=list(range(NCORES)))
    out = np.stack([r["y"].T for r in res.results], axis=0)
    return np.ascontiguousarray(out)


# version 06
# speedup vs baseline: 1.0460x; 1.0460x over previous
import math
import numpy as np
import ml_dtypes
import concourse.bass as bass
import concourse.mybir as mybir
from concourse.bass_utils import run_bass_kernel_spmd

F32 = mybir.dt.float32
BF16 = mybir.dt.bfloat16
I32 = mybir.dt.int32
U8 = mybir.dt.uint8
AF = mybir.ActivationFunctionType
ALU = mybir.AluOpType
AX = mybir.AxisListType

NCORES = 8
T = 2048
C = 1024
NH = 16
HS = 64
NT = T // 128
NC8 = C // 128
NB = 127
NEG = -1.0e30

COMPUTE_SEMS = dict(tensor="T", vector="V", scalar="A", gpsimd="G", sync="S")
DMA_SEMS = dict(sync="DS", gpsimd="DG", scalar="DA")
ALL_SEMS = ["T", "V", "A", "G", "S", "DS", "DG", "DA"]


class EM:
    def __init__(self, nc, sems):
        self.nc = nc
        self.sems = sems
        self.cnt = {k: 0 for k in ALL_SEMS}
        self.ops = {e: [] for e in COMPUTE_SEMS}
        self.wm = {e: {k: 0 for k in ALL_SEMS} for e in COMPUTE_SEMS}
        self.n_instr = 0

    def _emit(self, eng, fn, waits, sem_name, inc, multi):
        w = {}
        for t in waits:
            if t is None:
                continue
            s, c = t
            if c > w.get(s, 0):
                w[s] = c
        own = COMPUTE_SEMS[eng]
        wme = self.wm[eng]
        wl = []
        for s, c in sorted(w.items()):
            if s == own or (self.DEDUP and c <= wme[s]):
                continue
            wl.append((s, c))
            wme[s] = max(wme[s], c)
        sem = self.sems[sem_name]

        def run(engine, wl=wl, fn=fn, sem=sem, inc=inc, multi=multi):
            for s, c in wl:
                engine.wait_ge(self.sems[s], c)
            ins = fn(engine)
            if multi:
                engine.nop().then_inc(sem, inc)
            else:
                ins.then_inc(sem, inc)

        self.ops[eng].append(run)
        self.n_instr += 1 + len(wl) + (1 if multi else 0)
        self.cnt[sem_name] += inc
        return (sem_name, self.cnt[sem_name])

    FENCE = "drain"
    FENCE_ENGINES = ("vector", "scalar", "gpsimd")
    DEDUP = True

    def op(self, eng, fn, waits=(), multi=False):
        if self.FENCE == "drain" and eng in self.FENCE_ENGINES:
            fn2 = (lambda e, fn=fn: (fn(e), e.drain())[1])
            return self._emit(eng, fn2, waits, COMPUTE_SEMS[eng], 1, False)
        return self._emit(eng, fn, waits, COMPUTE_SEMS[eng], 1, multi)

    def dma(self, queue, fn, waits=()):
        sem = DMA_SEMS[queue]
        waits = list(waits) + [(sem, self.cnt[sem])]
        return self._emit(queue, fn, waits, sem, 16, False)

    def join(self, exclude=("DS", "DG", "DA")):
        return [(s, c) for s, c in self.cnt.items() if c > 0 and s not in exclude]

    def all_tok(self):
        return [(s, c) for s, c in self.cnt.items() if c > 0]

    def build(self, block):
        for name, fns in self.ops.items():
            if not fns:
                continue

            def body(eng, fns=fns):
                for f in fns:
                    f(eng)

            getattr(block, name)(body)


class Arena:
    def __init__(self, nc, guard_handle, size):
        self.nc = nc
        self.base = nc.lookup_mloc(guard_handle).addr
        self.size = size
        self.cursor = 0
        self.peak = 0
        self.marks = []
        self.n = 0

    def alloc(self, name, shape, dtype):
        per_part = int(np.prod(shape[1:])) * mybir.dt.size(dtype)
        at = self.cursor
        self.cursor = (at + per_part + 31) & ~31
        self.peak = max(self.peak, self.cursor)
        assert self.cursor <= self.size, f"SBUF arena overflow at {name}: {self.cursor}"
        h = self.nc.virtual_sbuf_tensor(f"{name}_{self.n}", list(shape), dtype)
        self.n += 1
        self.nc._tensor(h.name, h.shape, h.dtype, type="SB")
        mloc = self.nc.lookup_mloc(h)
        mloc.allocated = True
        mloc.addr = self.base + at
        return h

    def mark(self):
        self.marks.append(self.cursor)

    def release(self):
        self.cursor = self.marks.pop()

    def alias(self, name, shape, dtype, src):
        per_part = int(np.prod(shape[1:])) * mybir.dt.size(dtype)
        sm = self.nc.lookup_mloc(src)
        h = self.nc.virtual_sbuf_tensor(f"{name}_{self.n}", list(shape), dtype)
        self.n += 1
        self.nc._tensor(h.name, h.shape, h.dtype, type="SB")
        mloc = self.nc.lookup_mloc(h)
        mloc.allocated = True
        mloc.addr = sm.addr
        return h


ID_O = 0
ONES_O = 128
MASK_O = 256
MASKT_O = 384
PM_O = 512
F512_O = 513
LN127_O = 514
LN25_O = 515
COS_O = 544
SIN_O = COS_O + T
OH16_O = SIN_O + T
CW = OH16_O + T
E64_O = 0
ONESB_O = 64
IDB_O = 192
FOLD_O = 320
OH32_O = 384
CWB = OH32_O + 4096
CFS = 544
CBS = 384


def host_consts():
    cf = np.zeros((128, CW), dtype=np.float32)
    cf[:, ID_O:ID_O + 128] = np.eye(128, dtype=np.float32)
    cf[:, ONES_O:ONES_O + 128] = 1.0
    ii = np.arange(128)
    cf[:, MASK_O:MASK_O + 128] = np.where(ii[None, :] <= ii[:, None], 0.0, NEG)
    cf[:, MASKT_O:MASKT_O + 128] = np.where(ii[None, :] >= ii[:, None], 0.0, NEG)
    cf[0, PM_O] = 1.0
    cf[1, PM_O] = -1.0
    cf[:, F512_O] = 512.0
    cf[:, LN127_O] = np.float32(np.log(127.0))
    cf[:, LN25_O] = np.float32(np.log(2.5))
    dhe = 32
    invf = 1.0 / (10000.0 ** (np.arange(0, dhe, 2, dtype=np.float64) / dhe))
    fr = np.arange(T, dtype=np.float64)[None, :] * invf[:, None]
    cos32 = np.concatenate([np.cos(fr), np.cos(fr)], 0).astype(np.float32)
    sin32 = np.concatenate([np.sin(fr), np.sin(fr)], 0).astype(np.float32)
    cf[0:32, COS_O:COS_O + T] = 1.0
    cf[32:64, COS_O:COS_O + T] = cos32
    cf[96:128, COS_O:COS_O + T] = sin32
    for j in range(16):
        cf[j, OH16_O + 128 * j:OH16_O + 128 * j + 128] = 1.0
    cb = np.zeros((128, CWB), dtype=np.float32)
    cb[64:128, E64_O:E64_O + 64] = np.eye(64, dtype=np.float32)
    cb[:, IDB_O:IDB_O + 128] = np.eye(128, dtype=np.float32)
    cb[0:64, FOLD_O:FOLD_O + 64] = np.eye(64, dtype=np.float32)
    cb[64:128, FOLD_O:FOLD_O + 64] = np.eye(64, dtype=np.float32)
    for e in range(32):
        cb[e, OH32_O + 128 * e:OH32_O + 128 * e + 128] = 1.0
    cb[:, ONESB_O:ONESB_O + 128] = 1.0
    return cf, cb.astype(ml_dtypes.bfloat16)


def _pvec(v, p=128):
    v = np.asarray(v)
    return np.ascontiguousarray(v.reshape(-1, p).T)


def host_inputs(core, A):
    d = {
        "x": np.ascontiguousarray(A["L0_attn_x"][core].T),
        "cattn_w": A["L0_attn_c_attn_w"],
        "cattn_b": _pvec(A["L0_attn_c_attn_b"]),
        "cproj_w": A["L0_attn_c_proj_w"],
        "cproj_b": _pvec(A["L0_attn_c_proj_b"]),
        "conv_w": np.ascontiguousarray(
            A["L0_attn_conv_w"].transpose(1, 2, 0).reshape(64, 32 * 64)),
        "conv_b": A["L0_attn_conv_b"].reshape(64, 1),
        "compln_w": A["L0_attn_comp_ln_w"].reshape(64, 1),
        "compln_b": A["L0_attn_comp_ln_b"].reshape(64, 1),
        "lnloc_w": _pvec(A["L0_attn_ln_local_w"]),
        "lnloc_b": _pvec(A["L0_attn_ln_local_b"]),
        "gate_w1": A["L0_attn_gate_w1"],
        "gate_b1": _pvec(A["L0_attn_gate_b1"]),
        "gateln_w": _pvec(A["L0_attn_gate_ln_w"]),
        "gateln_b": _pvec(A["L0_attn_gate_ln_b"]),
        "gate_w2": np.ascontiguousarray(
            A["L0_attn_gate_w2"].reshape(8, 128, 2).transpose(1, 0, 2).reshape(128, 16)),
        "gate_b2": A["L0_attn_gate_b2"].reshape(2, 1),
        "ffn_w1t": np.ascontiguousarray(A["L0_ffn_w1"].T),
        "ffn_convw": np.ascontiguousarray(
            A["L0_ffn_conv_w"].reshape(4096, 3).reshape(32, 128, 3)
            .transpose(1, 0, 2).reshape(128, 96)),
        "ffn_convb": _pvec(A["L0_ffn_conv_b"]),
        "ffn_w2t": np.ascontiguousarray(A["L0_ffn_w2"].T),
        "wq_down": A["L1_attn_Wq_down"],
        "wk_down": A["L1_attn_Wk_down"],
        "wv_down": A["L1_attn_Wv_down"],
        "wq_upc": A["L1_attn_Wq_up_c"],
        "wq_upe": A["L1_attn_Wq_up_e"],
        "wk_upc": A["L1_attn_Wk_up_c"],
        "wk_upe": A["L1_attn_Wk_up_e"],
        "wv_up": A["L1_attn_Wv_up"],
        "wc": A["L1_attn_Wc"],
        "rw_t": np.ascontiguousarray(A["L1_ffn_router_weight"].T),
        "r_bias": A["L1_ffn_router_bias"].reshape(1, 32),
        "moe_gate": np.ascontiguousarray(A["L1_ffn_gate_w"].transpose(0, 2, 1)),
        "moe_up": np.ascontiguousarray(A["L1_ffn_up_w"].transpose(0, 2, 1)),
        "moe_down": np.ascontiguousarray(A["L1_ffn_down_w"].transpose(0, 2, 1)),
        "sh_gate": np.ascontiguousarray(A["L1_ffn_shared_gate_w"].T),
        "sh_up": np.ascontiguousarray(A["L1_ffn_shared_up_w"].T),
        "sh_down": np.ascontiguousarray(A["L1_ffn_shared_down_w"].T),
    }
    cf, cb = host_consts()
    d["constf"] = cf
    d["constb"] = cb
    return {k: np.ascontiguousarray(v) for k, v in d.items()}


INPUT_SPECS = {
    "x": ((C, T), F32), "cattn_w": ((C, 3 * C), F32), "cattn_b": ((128, 24), F32),
    "cproj_w": ((C, C), F32), "cproj_b": ((128, 8), F32),
    "conv_w": ((64, 2048), F32), "conv_b": ((64, 1), F32),
    "compln_w": ((64, 1), F32), "compln_b": ((64, 1), F32),
    "lnloc_w": ((128, 8), F32), "lnloc_b": ((128, 8), F32),
    "gate_w1": ((2 * C, C), F32), "gate_b1": ((128, 8), F32),
    "gateln_w": ((128, 8), F32), "gateln_b": ((128, 8), F32),
    "gate_w2": ((128, 16), F32), "gate_b2": ((2, 1), F32),
    "ffn_w1t": ((C, 4096), F32), "ffn_convw": ((128, 96), F32),
    "ffn_convb": ((128, 32), F32), "ffn_w2t": ((4096, C), F32),
    "wq_down": ((C, C), F32), "wk_down": ((C, C), F32), "wv_down": ((C, C), F32),
    "wq_upc": ((64, 32), F32), "wq_upe": ((64, 32), F32),
    "wk_upc": ((64, 32), F32), "wk_upe": ((64, 32), F32), "wv_up": ((64, 64), F32),
    "wc": ((C, C), F32),
    "rw_t": ((C, 32), F32), "r_bias": ((1, 32), F32),
    "moe_gate": ((32, C, 512), F32), "moe_up": ((32, C, 512), F32),
    "moe_down": ((32, 512, C), F32),
    "sh_gate": ((C, C), F32), "sh_up": ((C, C), F32), "sh_down": ((C, C), F32),
    "constf": ((128, CW), F32), "constb": ((128, CWB), BF16),
}


class P:
    pass


def mkprog(stage, dbg):
    import contextlib
    p = P()
    nc = bass.Bass("TRN2", target_bir_lowering=False, detect_race_conditions=False)
    p.nc = nc
    p.stage = stage
    dram = {}
    for name, (shape, dt) in INPUT_SPECS.items():
        dram[name] = nc.dram_tensor(name, list(shape), dt, kind="ExternalInput")
    dram["y"] = nc.dram_tensor("y", [C, T], F32, kind="ExternalOutput")
    if dbg is not None:
        dram["dbg"] = nc.dram_tensor("dbg", list(dbg), F32, kind="ExternalOutput")
    dram["spill_x"] = nc.dram_tensor("spill_x", [C, T], F32, kind="Internal")
    dram["spill_x2"] = nc.dram_tensor("spill_x2", [C, T], F32, kind="Internal")
    dram["t1_scr"] = nc.dram_tensor("t1_scr", [C, 4096], BF16, kind="Internal")
    dram["t2_scr"] = nc.dram_tensor("t2_scr", [4096, C], BF16, kind="Internal")
    dram["y_scr"] = nc.dram_tensor("y_scr", [C, T], BF16, kind="Internal")
    p.dram = dram
    p.stack = contextlib.ExitStack()
    ARENA = 211200
    guard = p.stack.enter_context(nc.sbuf_tensor("arena", [128, ARENA], U8))
    p.PA = p.stack.enter_context(nc.psum_tensor("PA", [128, 2048], F32))
    p.PB = p.stack.enter_context(nc.psum_tensor("PB", [128, 2048], F32))
    p.sems = {s: p.stack.enter_context(nc.semaphore(f"sem_{s}")) for s in ALL_SEMS}
    p.block = p.stack.enter_context(nc.Block())
    p.ar = Arena(nc, guard, ARENA)
    p.em = EM(nc, p.sems)
    em = p.em

    p.cf = p.ar.alloc("cf", [128, CFS], F32)
    p.cb = p.ar.alloc("cb", [128, CBS], BF16)
    c1 = em.dma("sync", lambda e: e.dma_start(out=p.cf[:], in_=dram["constf"][:, 0:CFS]))
    c2 = em.dma("sync", lambda e: e.dma_start(out=p.cb[:], in_=dram["constb"][:, 0:CBS]))
    p.ctok = [c1, c2]

    p.x_bf = p.ar.alloc("x_bf", [128, NC8, T], BF16)
    xt = em.dma("gpsimd", lambda e: e.dma_start(
        out=p.x_bf[:],
        in_=dram["x"][:, :].rearrange("(c p) t -> p c t", p=128)))
    p.xtok = [xt]
    p.pv = {}
    for nm in ("cattn_b", "cproj_b", "conv_b", "compln_w", "compln_b", "lnloc_w",
               "lnloc_b", "gate_b1", "gateln_w", "gateln_b", "gate_w2", "gate_b2",
               "ffn_convb", "ffn_convw", "r_bias"):
        shape, dt = INPUT_SPECS[nm]
        t = p.ar.alloc("pv_" + nm, list(shape), dt)
        tok = em.dma("sync", lambda e, t=t, nm=nm, shape=shape: e.dma_start(
            out=t[0:shape[0], :], in_=dram[nm][:, :]))
        p.pv[nm] = t
        p.ctok.append(tok)

    p.scr = p.ar.alloc("scr", [128, 2048], F32)
    p.scr2 = p.ar.alloc("scr2", [128, 1024], F32)
    p.scrb = p.ar.alloc("scrb", [128, 2048], BF16)
    p.stat = p.ar.alloc("stat", [128, 64], F32)
    p.msks = p.ar.alloc("msks", [128, 128], F32)
    p.strip0 = p.ar.alloc("strip0", [1, 512], F32)
    p.strip1 = p.ar.alloc("strip1", [1, 512], F32)
    p.strip2 = p.ar.alloc("strip2", [1, 512], F32)
    p.l2 = p.ar.alloc("l2", [2, 16], F32)
    p.vbias = [p.ar.alloc(f"vbias{i}", [1, 128], F32) for i in range(2)]
    p.bank_rd = {}
    p.x_hbm = dram["x"]
    return p


def ident(p):
    return p.cf[:, ID_O:ID_O + 128]


def ones(p, r, c):
    return p.cf[0:r, ONES_O:ONES_O + c]


def onesb(p, r, c):
    return p.cb[0:r, ONESB_O:ONESB_O + c]


def bank(p, i, rows=128, cols=512, coloff=0, rowoff=0):
    t = p.PA if i < 4 else p.PB
    return t[rowoff:rowoff + rows, 512 * (i % 4) + coloff:512 * (i % 4) + coloff + cols]


def use_bank(p, i, waits):
    t = p.bank_rd.get(i)
    return waits + ([t] if t else [])


def read_bank(p, i, tok):
    p.bank_rd[i] = tok


def rsqrt_strip(p, dst, src, epsb, deps, n):
    em = p.em
    t0 = em.op("vector", lambda e: e.tensor_scalar(dst, src, epsb, None,
                                                   op0=ALU.add), waits=deps)
    t1 = em.op("scalar", lambda e: e.activation(dst, dst, AF.Ln), waits=[t0])
    return em.op("scalar", lambda e: e.activation(dst, dst, AF.Exp, scale=-0.5), waits=[t1])


def ln_fm(p, xin, out_f32, out_b16, deps, gamma=None, beta=None, nfeat=1024,
          sb=(0, 1, 2, 3), bf_in=False, blocks=None):
    em = p.em
    outs = []
    inv = 1.0 / nfeat
    if blocks is None:
        blocks = [(slice(512 * tc, 512 * tc + 512),) * 3 for tc in range(4)]
    for sl, slo32, slo16 in blocks:
        stoks = []
        sqt = None
        mm_s = None
        for c in range(NC8):
            sqt = em.op("scalar", lambda e, c=c, sl=sl: e.activation(
                p.scr[:, 0:512] if c % 2 == 0 else p.scr[:, 512:1024],
                xin(c)[:, sl], AF.Square), waits=deps + ([mm_s] if mm_s else []))
            mm_s = em.op("tensor", lambda e, c=c: e.matmul(
                bank(p, sb[1], 1, 512), ones(p, 128, 1),
                p.scr[:, 0:512] if c % 2 == 0 else p.scr[:, 512:1024],
                start=(c == 0), stop=(c == NC8 - 1), skip_group_check=True),
                waits=use_bank(p, sb[1], [sqt]) if c == 0 else [sqt])
            o1 = onesb(p, 128, 1) if bf_in else ones(p, 128, 1)
            stoks.append(em.op("tensor", lambda e, c=c, o1=o1, sl=sl: e.matmul(
                bank(p, sb[0], 1, 512), o1, xin(c)[:, sl],
                start=(c == 0), stop=(c == NC8 - 1), skip_group_check=True),
                waits=use_bank(p, sb[0], deps) if c == 0 else deps))
        stoks.append(mm_s)
        sm = p.strip0[0:1, 0:512]
        sv = p.strip1[0:1, 0:512]
        t1 = em.op("vector", lambda e: e.tensor_scalar(
            sm, bank(p, sb[0], 1, 512), inv, None, op0=ALU.mult), waits=stoks)
        read_bank(p, sb[0], t1)
        t2 = em.op("vector", lambda e: e.tensor_tensor(
            p.strip2[0:1, 0:512], sm, sm, op=ALU.mult), waits=[t1])
        t3 = em.op("vector", lambda e: e.scalar_tensor_tensor(
            sv, bank(p, sb[1], 1, 512), inv, p.strip2[0:1, 0:512],
            op0=ALU.mult, op1=ALU.subtract), waits=[t2] + stoks)
        read_bank(p, sb[1], t3)
        t4 = rsqrt_strip(p, sv, sv, 1e-5, [t3], 512)
        bm = em.op("tensor", lambda e: e.matmul(
            bank(p, sb[2]), ones(p, 1, 128), sm, start=True, stop=True),
            waits=use_bank(p, sb[2], [t1]))
        br = em.op("tensor", lambda e: e.matmul(
            bank(p, sb[3]), ones(p, 1, 128), sv, start=True, stop=True),
            waits=use_bank(p, sb[3], [t4]))
        for c in range(NC8):
            ta = em.op("vector", lambda e, c=c, sl=sl: e.tensor_tensor(
                p.scr2[:, 0:512], xin(c)[:, sl], bank(p, sb[2]),
                op=ALU.subtract), waits=[bm] + deps)
            if gamma is None:
                dst32 = out_f32(c) if out_f32 else None
                if dst32 is not None:
                    tb = em.op("vector", lambda e, c=c, dst32=dst32, slo32=slo32: e.tensor_tensor(
                        dst32[:, slo32], p.scr2[:, 0:512], bank(p, sb[3]),
                        op=ALU.mult), waits=[ta, br])
                    if out_b16:
                        tb = em.op("vector", lambda e, c=c, dst32=dst32, slo32=slo32,
                                   slo16=slo16: e.tensor_copy(
                            out_b16(c)[:, slo16], dst32[:, slo32]), waits=[tb])
                else:
                    tb = em.op("vector", lambda e, c=c, slo16=slo16: e.tensor_tensor(
                        out_b16(c)[:, slo16], p.scr2[:, 0:512], bank(p, sb[3]),
                        op=ALU.mult), waits=[ta, br])
            else:
                tb = em.op("vector", lambda e, c=c: e.tensor_tensor(
                    p.scr2[:, 512:1024], p.scr2[:, 0:512], bank(p, sb[3]),
                    op=ALU.mult), waits=[ta, br])
                slo = slo16 if out_b16 else slo32
                tb = em.op("vector", lambda e, c=c, slo=slo: e.tensor_scalar(
                    (out_b16(c) if out_b16 else out_f32(c))[:, slo],
                    p.scr2[:, 512:1024], gamma[:, c:c + 1], beta[:, c:c + 1],
                    op0=ALU.mult, op1=ALU.add), waits=[tb])
            outs.append(tb)
        read_bank(p, sb[2], outs[-1])
        read_bank(p, sb[3], outs[-1])
    return outs


def tap(p, name, ap_flat, deps):
    if getattr(p, "tap", None) != name or p.tap_done:
        return deps
    p.tap_done = True
    t = p.em.dma("gpsimd", lambda e: e.dma_start(
        out=p.dram["dbg"][:, :], in_=ap_flat), waits=list(deps))
    return deps + [t]


def stream_w_bf16(p, dst, src_ap, deps=()):
    return p.em.dma("gpsimd", lambda e: e.dma_start(out=dst, in_=src_ap),
                    waits=list(deps))


def attn_head(p, h, Qaug, Kaug, Vtm, out_chunk_ap, wdeps, scale=0.125,
              comp=None):
    em = p.em
    par = h % 2
    rowoff = 64 * par
    vhl = p.stat[:, 32:34]
    mx = p.stat[:, 34:40]
    dn = p.stat[:, 40:46]
    m1 = p.stat[:, 46:47]
    bm = p.stat[:, 47:48]
    den = p.stat[:, 48:49]
    lv = p.stat[:, 49:50]
    ot = None
    ph1_done = []
    for Tt in range(NT):
        if comp is None:
            kl = 128 * (Tt + 1)
            nch = (kl + 511) // 512
            chs = [(i * 512, min(512, kl - i * 512)) for i in range(nch)]
        else:
            chs = [(0, NB)]
        mmt = []
        for ci, (k0, kn) in enumerate(chs):
            src = (lambda e, k0=k0, kn=kn, Tt=Tt, ci=ci: e.matmul(
                bank(p, ci, 128, kn), Qaug[0:64, 128 * Tt:128 * Tt + 128],
                (Kaug[0:64, k0:k0 + kn] if comp is None else
                 comp["kc"][0:64, 0:NB]), start=True, stop=True))
            mmt.append(em.op("tensor", src,
                             waits=use_bank(p, ci, wdeps)))
        mt = None
        dc0 = 0
        mw = 128
        if comp is None:
            k0l, knl = chs[-1]
            dc0 = knl - 128
            mt = em.op("vector", lambda e, dc0=dc0, nch=len(chs): e.tensor_tensor(
                p.msks[:, 0:128], bank(p, nch - 1, 128, 128, coloff=dc0),
                p.cf[:, MASK_O:MASK_O + 128], op=ALU.add), waits=[mmt[-1]])
        elif Tt == 0:
            mw = NB
            mt = em.op("vector", lambda e: e.tensor_tensor(
                p.msks[:, 0:NB], bank(p, 0, 128, NB),
                p.cf[:, MASK_O:MASK_O + NB], op=ALU.add), waits=[mmt[-1]])
        if getattr(p, "tap", None) == "h0scores" and h == 0 and comp is None and Tt == 0 and not p.tap_done:
            p.tap_done = True
            c1 = em.op("vector", lambda e: e.tensor_copy(
                p.scr2[:, 0:128], bank(p, 0, 128, 128)), waits=[mmt[0]])
            tdma = em.dma("gpsimd", lambda e: e.dma_start(
                out=p.dram["dbg"][:, 0:128], in_=p.scr2[:, 0:128]), waits=[c1])
            mmt[-1] = (mmt[-1][0], mmt[-1][1])
            p.tap_guard = [tdma]
            ph1_done.append(c1)
        nfull = len(chs) - (1 if mt is not None else 0)
        nidx = nfull + (1 if (mt is not None and dc0 > 0) else 0)
        n_all = (nidx + 1) if mt is not None else nfull
        lastci = len(chs) - 1
        rts = []
        for ci, (k0, kn) in enumerate(chs[:nfull]):
            rts.append(em.op("vector", lambda e, ci=ci, kn=kn: e.reduce_max(
                mx[:, ci:ci + 1], bank(p, ci, 128, kn), axis=AX.X),
                waits=[mmt[ci]]))
        if mt is not None and dc0 > 0:
            rts.append(em.op("vector", lambda e, nfull=nfull, lastci=lastci,
                             dc0=dc0: e.reduce_max(
                mx[:, nfull:nfull + 1], bank(p, lastci, 128, dc0), axis=AX.X),
                waits=[mmt[-1]]))
        if mt is not None:
            rts.append(em.op("vector", lambda e, nidx=nidx, mw=mw: e.reduce_max(
                mx[:, nidx:nidx + 1], p.msks[:, 0:mw], axis=AX.X), waits=[mt]))
        tm = em.op("vector", lambda e, n=n_all: e.reduce_max(
            m1, mx[:, 0:n], axis=AX.X), waits=rts)
        tb = em.op("vector", lambda e: e.tensor_scalar(
            bm, m1, -scale, None, op0=ALU.mult), waits=[tm])
        ets = []
        for ci, (k0, kn) in enumerate(chs[:nfull]):
            ets.append(em.op("scalar", lambda e, ci=ci, kn=kn: e.activation(
                p.scrb[:, 1024:1024 + kn], bank(p, ci, 128, kn), AF.Exp, bias=bm,
                scale=scale, accum_out=dn[:, ci:ci + 1]), waits=[tb, mmt[ci]]))
            read_bank(p, ci, ets[-1])
        if mt is not None and dc0 > 0:
            ets.append(em.op("scalar", lambda e, nfull=nfull, lastci=lastci,
                             dc0=dc0: e.activation(
                p.scrb[:, 1024:1024 + dc0], bank(p, lastci, 128, dc0), AF.Exp,
                bias=bm, scale=scale, accum_out=dn[:, nfull:nfull + 1]),
                waits=[tb, mmt[-1]]))
            read_bank(p, lastci, ets[-1])
        if mt is not None:
            ets.append(em.op("scalar", lambda e, nidx=nidx, mw=mw: e.activation(
                p.scrb[:, 1664:1664 + mw], p.msks[:, 0:mw], AF.Exp,
                bias=bm, scale=scale, accum_out=dn[:, nidx:nidx + 1]),
                waits=[tb, mt]))
            if dc0 == 0:
                read_bank(p, lastci, mt)
        td = em.op("vector", lambda e, n=n_all: e.reduce_sum(
            den, dn[:, 0:n], axis=AX.X), waits=ets)
        tl = em.op("scalar", lambda e: e.activation(lv, den, AF.Ln), waits=[td])
        tv = em.op("vector", lambda e: e.scalar_tensor_tensor(
            p.stat[:, 50:51], lv, -1.0 / scale, m1, op0=ALU.mult,
            op1=ALU.subtract), waits=[tl, tm])
        th = em.op("vector", lambda e: e.tensor_copy(
            p.scrb[:, 1536:1537], p.stat[:, 50:51]), waits=[tv])
        th2 = em.op("vector", lambda e: e.tensor_copy(
            vhl[:, 0:1], p.scrb[:, 1536:1537]), waits=[th])
        th3 = em.op("vector", lambda e: e.tensor_tensor(
            vhl[:, 1:2], p.stat[:, 50:51], vhl[:, 0:1], op=ALU.subtract),
            waits=[th2])
        if getattr(p, "tap", None) == "h0stats" and h == 0 and comp is None:
            W = 16
            em.op("vector", lambda e, Tt=Tt: e.tensor_copy(
                p.scr2[:, W * Tt:W * Tt + 4], mx[:, 0:4]), waits=rts + [tm])
            em.op("vector", lambda e, Tt=Tt: e.tensor_copy(
                p.scr2[:, W * Tt + 4:W * Tt + 8], dn[:, 0:4]), waits=ets)
            em.op("vector", lambda e, Tt=Tt: e.tensor_copy(
                p.scr2[:, W * Tt + 8:W * Tt + 9], m1), waits=[tm])
            em.op("vector", lambda e, Tt=Tt: e.tensor_copy(
                p.scr2[:, W * Tt + 9:W * Tt + 10], den), waits=[td])
            hst = em.op("vector", lambda e, Tt=Tt: e.tensor_copy(
                p.scr2[:, W * Tt + 10:W * Tt + 11], lv), waits=[tl])
            hst = em.op("vector", lambda e, Tt=Tt: e.tensor_copy(
                p.scr2[:, W * Tt + 11:W * Tt + 13], vhl), waits=[th3])
        bt = 7
        tt = em.op("tensor", lambda e, bt=bt: e.matmul(
            bank(p, bt, 2, 128, rowoff=64), vhl, ident(p),
            start=True, stop=True, tile_position=(0, 64)),
            waits=use_bank(p, bt, [th3]))
        te = em.op("vector", lambda e, Tt=Tt, bt=bt: e.tensor_copy(
            Qaug[64:66, 128 * Tt:128 * Tt + 128], bank(p, bt, 2, 128, rowoff=64)),
            waits=[tt] + wdeps)
        read_bank(p, bt, te)
        if getattr(p, "tap", None) == "h0stats" and h == 0 and comp is None:
            W = 16
            hst = em.op("vector", lambda e, Tt=Tt, bt=bt: e.tensor_copy(
                p.scr2[64:66, W * Tt + 13:W * Tt + 15],
                bank(p, bt, 2, 2, rowoff=64)), waits=[tt])
            read_bank(p, bt, hst)
            if Tt == NT - 1:
                p.tap_done = True
                p.tap_guard = [em.dma("gpsimd", lambda e: e.dma_start(
                    out=p.dram["dbg"][:, 0:256], in_=p.scr2[:, 0:256]),
                    waits=[hst])]
        ph1_done.append(te)
    nkt = NT if comp is None else 1
    prev_tvm = {}
    for qc in range(4):
        q0c = 512 * qc
        kts = [kt for kt in range(nkt)
               if comp is not None or 128 * kt < q0c + 512]
        for ki, kt in enumerate(kts):
            first = ki == 0
            last = ki == len(kts) - 1
            q0 = max(128 * kt, q0c) if comp is None else q0c
            qn = q0c + 512 - q0
            sb = 4 + (ki % 2)
            sco = 512 * (ki % 2)
            if comp is None:
                tmm = em.op("tensor", lambda e, kt=kt, q0=q0, qn=qn, sb=sb: e.matmul(
                    bank(p, sb, 128, qn), Kaug[0:66, 128 * kt:128 * kt + 128],
                    Qaug[0:66, q0:q0 + qn], start=True, stop=True),
                    waits=use_bank(p, sb, ph1_done))
                rows = 128
            else:
                tmm = em.op("tensor", lambda e, q0=q0, qn=qn, sb=sb: e.matmul(
                    bank(p, sb, NB, qn), comp["kc"][0:66, 0:NB],
                    Qaug[0:66, q0:q0 + qn], start=True, stop=True),
                    waits=use_bank(p, sb, ph1_done))
                rows = NB
            mt2 = None
            if comp is None and 128 * kt >= q0c:
                mt2 = em.op("vector", lambda e, sb=sb: e.tensor_tensor(
                    p.msks[:, 0:128], bank(p, sb, 128, 128),
                    p.cf[:, MASKT_O:MASKT_O + 128], op=ALU.add), waits=[tmm])
            elif comp is not None and qc == 0:
                mt2 = em.op("vector", lambda e, sb=sb: e.tensor_tensor(
                    p.msks[0:NB, 0:128], bank(p, sb, NB, 128),
                    p.cf[0:NB, MASKT_O:MASKT_O + 128], op=ALU.add), waits=[tmm])
            prev = [prev_tvm[ki % 2]] if (ki % 2) in prev_tvm else []
            if mt2 is None:
                tex = em.op("scalar", lambda e, qn=qn, sb=sb, rows=rows, sco=sco:
                            e.activation(p.scrb[0:rows, sco:sco + qn],
                                         bank(p, sb, rows, qn), AF.Exp,
                                         scale=scale),
                            waits=[tmm] + prev)
                read_bank(p, sb, tex)
                texs = [tex]
            else:
                texs = [em.op("scalar", lambda e, sb=sb, rows=rows, sco=sco:
                              e.activation(p.scrb[0:rows, sco:sco + 128],
                                           p.msks[0:rows, 0:128], AF.Exp,
                                           scale=scale),
                              waits=[mt2] + prev)]
                if qn > 128:
                    t2x = em.op("scalar", lambda e, qn=qn, sb=sb, rows=rows,
                                sco=sco: e.activation(
                        p.scrb[0:rows, sco + 128:sco + qn],
                        bank(p, sb, rows, qn - 128, coloff=128), AF.Exp,
                        scale=scale), waits=[tmm] + prev)
                    read_bank(p, sb, t2x)
                    texs.append(t2x)
                else:
                    read_bank(p, sb, mt2)
                tex = texs[-1]
            if (getattr(p, "tap", None) == "h0p2" and h == 0 and comp is None
                    and qc == 0 and ki == 0 and not p.tap_done):
                p.tap_done = True
                tpd = em.dma("gpsimd", lambda e: e.dma_start(
                    out=p.dram["dbg"][:, 0:512], in_=p.scrb[0:128, 0:512]),
                    waits=texs)
                p.tap_guard = [tpd]
                texs = [tpd]
            tvm = em.op("tensor", lambda e, kt=kt, q0=q0, qn=qn, first=first,
                        last=last, q0c=q0c, sco=sco: e.matmul(
                bank(p, 6, 64, 512, rowoff=rowoff)[:, q0 - q0c:q0 - q0c + qn],
                (Vtm[:, kt, 0:64] if comp is None else comp["vtm"][0:NB, 0:64]),
                p.scrb[0:(128 if comp is None else NB), sco:sco + qn],
                start=first, stop=last, skip_group_check=True,
                tile_position=(0, rowoff)),
                waits=use_bank(p, 6, texs) if first else texs)
            prev_tvm[ki % 2] = tvm
        ot = em.op("vector", lambda e, qc=qc: e.tensor_copy(
            out_chunk_ap[rowoff:rowoff + 64, 512 * qc:512 * qc + 512],
            bank(p, 6, 64, 512, rowoff=rowoff)), waits=[tvm])
        read_bank(p, 6, ot)
    if getattr(p, "tap", None) == f"head{h}" + ("c" if comp else "") and not p.tap_done:
        p.tap_done = True
        t1 = em.dma("gpsimd", lambda e: e.dma_start(
            out=p.dram["dbg"][0:66, 0:T], in_=Qaug[0:66, :]), waits=[ot])
        t2 = em.dma("gpsimd", lambda e: e.dma_start(
            out=p.dram["dbg"][0:66, T:2 * T], in_=Kaug[0:66, :]), waits=[ot])
        t3 = em.dma("gpsimd", lambda e: e.dma_start(
            out=p.dram["dbg"][:, 2 * T:2 * T + NT * 64],
            in_=Vtm[:].rearrange("p a b -> p (a b)")), waits=[ot])
        p.tap_guard = [t1, t2, t3]
    return ot, ph1_done


def emit_l0_attn(p, is_last=False):
    em = p.em
    ar = p.ar
    dram = p.dram
    deps0 = p.ctok + p.xtok

    ar.mark()
    local_all = ar.alloc("local_all", [128, NC8, T], BF16)
    comp_all = ar.alloc("comp_all", [128, NC8, T], BF16)
    ar.mark()
    kcomp = ar.alloc("kcomp", [128, NH, 128], BF16)
    vcomp = ar.alloc("vcomp", [128, NH, 64], BF16)
    wslot = [ar.alloc(f"wsl{i}", [128, 3, 8, 128], BF16) for i in range(2)]
    QS = [ar.alloc(f"QS{i}", [128, T], BF16) for i in range(4)]
    KS = [ar.alloc(f"KS{i}", [128, T], BF16) for i in range(4)]
    VD = [ar.alloc(f"VD{i}", [128, T], BF16) for i in range(4)]
    VT = [ar.alloc(f"VT{i}", [128, NT, 64], BF16) for i in range(4)]
    stage = ar.alloc("stage", [128, 512], BF16)
    sconv = ar.alloc("sconv", [64, 2048], BF16)
    cs = stream_w_bf16(p, sconv[0:64, :], dram["conv_w"][:, :])
    ko = em.op("vector", lambda e: e.memset(
        kcomp[64:66, :].rearrange("p h c -> p (h c)"), 1.0), waits=p.ctok)
    stz = em.op("vector", lambda e: e.memset(stage[0:64, :], 0.0), waits=[])
    ones_toks = [ko, stz]

    wtok = [None, None]
    last_prod = [[], []]

    def load_pair(pair):
        sl = wslot[pair % 2]
        toks = []
        dep = last_prod[pair % 2]
        for j, base in enumerate((0, C, 2 * C)):
            cols = slice(base + 128 * pair, base + 128 * pair + 128)
            toks.append(stream_w_bf16(
                p, sl[:, j, :, :],
                dram["cattn_w"][:, cols].rearrange("(c p) j -> p c j", p=128),
                deps=dep))
        wtok[pair % 2] = toks
        return toks

    deps0 = tap(p, "xbf", p.x_bf[:].rearrange("p c t -> p (c t)"), deps0)
    wt0 = load_pair(0)
    if getattr(p, "tap", None) == "wsl":
        p.tap_done = True
        tw = em.dma("gpsimd", lambda e: e.dma_start(
            out=p.dram["dbg"][:, 0:3072],
            in_=wslot[0][:].rearrange("p j c k -> p (j c k)")), waits=wt0)
        last_prod[0] = last_prod[0] + [tw]
    slot_free = [[], []]
    out_toks = []
    for pair in range(NH // 2):
        par2 = pair % 2
        wdep = wtok[par2]
        sl = wslot[par2]
        if pair + 1 < NH // 2:
            load_pair(pair + 1)
        qa, ka = QS[2 * par2], KS[2 * par2]
        qb, kb = QS[2 * par2 + 1], KS[2 * par2 + 1]
        vda, vdb = VD[2 * par2], VD[2 * par2 + 1]
        vta, vtb = VT[2 * par2], VT[2 * par2 + 1]
        sf = slot_free[par2]
        vbs = p.vbias[par2]
        tb0 = em.op("tensor", lambda e, pair=pair: e.transpose(
            bank(p, 3, 1, 128), p.pv["cattn_b"][:, 16 + pair:17 + pair],
            ident(p)), waits=use_bank(p, 3, p.ctok + last_prod[par2]))
        vb0 = em.op("vector", lambda e, vbs=vbs: e.tensor_copy(
            vbs[0:1, :], bank(p, 3, 1, 128)), waits=[tb0] + last_prod[par2])
        read_bank(p, 3, vb0)
        prod_toks = []
        for j, (evA, evB) in enumerate(((qa, qb), (ka, kb), (vda, vdb))):
            for tc in range(4):
                sltc = slice(512 * tc, 512 * tc + 512)
                mt = None
                for c in range(NC8):
                    mt = em.op("tensor", lambda e, j=j, c=c, sl=sl, sltc=sltc: e.matmul(
                        bank(p, 0), sl[:, j, c, :], p.x_bf[:, c, sltc],
                        start=(c == 0), stop=(c == NC8 - 1), skip_group_check=True),
                        waits=use_bank(p, 0, wdep + deps0) if c == 0 else [])
                bcol = {0: pair, 1: 8 + pair, 2: 16 + pair}[j]
                e1 = em.op("scalar", lambda e, evA=evA, tc=tc, bcol=bcol, sltc=sltc:
                           e.activation(evA[0:64, sltc], bank(p, 0, 64),
                                        AF.Identity,
                                        bias=p.pv["cattn_b"][0:64, bcol:bcol + 1]),
                           waits=[mt] + sf)
                e2 = em.op("scalar", lambda e, bcol=bcol: e.activation(
                    stage[64:128, :], bank(p, 0, 64, rowoff=64), AF.Identity,
                    bias=p.pv["cattn_b"][64:128, bcol:bcol + 1]), waits=[mt])
                read_bank(p, 0, e2)
                ms = em.op("tensor", lambda e: e.matmul(
                    bank(p, 1, 64), p.cb[:, E64_O:E64_O + 64], stage[:, :],
                    start=True, stop=True),
                    waits=use_bank(p, 1, [e2, stz]))
                e3 = em.op("vector", lambda e, evB=evB, sltc=sltc: e.tensor_copy(
                    evB[0:64, sltc], bank(p, 1, 64)), waits=[ms] + sf)
                read_bank(p, 1, e3)
                prod_toks += [e1, e3]
        prod_toks.append(em.op("vector", lambda e, ka=ka: e.memset(
            ka[64:66, :], 1.0), waits=sf))
        prod_toks.append(em.op("vector", lambda e, kb=kb: e.memset(
            kb[64:66, :], 1.0), waits=sf))
        for Tt in range(NT):
            mt = None
            for c in range(NC8):
                mt = em.op("tensor", lambda e, c=c, Tt=Tt, sl=sl: e.matmul(
                    bank(p, 2, 128, 128), p.x_bf[:, c, 128 * Tt:128 * Tt + 128],
                    sl[:, 2, c, :], start=(c == 0), stop=False,
                    skip_group_check=True),
                    waits=use_bank(p, 2, wdep + [vb0]) if c == 0 else [])
            mt = em.op("tensor", lambda e, vbs=vbs: e.matmul(
                bank(p, 2, 128, 128), ones(p, 1, 128), vbs[0:1, :],
                start=False, stop=True, skip_group_check=True), waits=[])
            ev = em.op("vector", lambda e, Tt=Tt, vta=vta: e.tensor_copy(
                vta[:, Tt, :], bank(p, 2, 128, 64)), waits=[mt] + sf)
            ev2 = em.op("vector", lambda e, Tt=Tt, vtb=vtb: e.tensor_copy(
                vtb[:, Tt, :], bank(p, 2, 128, 64, coloff=64)), waits=[mt] + sf)
            read_bank(p, 2, ev2)
            prod_toks += [ev, ev2]
        last_prod[par2] = prod_toks[-2:]
        nsf = []
        for hh in range(2):
            h = 2 * pair + hh
            Qh, Kh, Vdh, Vth = (qa, ka, vda, vta) if hh == 0 else (qb, kb, vdb, vtb)
            oc_ap = local_all[:, h // 2, :]
            ot, ph1 = attn_head(p, h, Qh, Kh, Vth, oc_ap,
                                prod_toks + ones_toks)
            out_toks.append(ot)
            for which, srct in (("k", Kh), ("v", Vdh)):
                mt = None
                for tau in range(32):
                    mt = em.op("tensor", lambda e, tau=tau, srct=srct: e.matmul(
                        bank(p, 0, 64, NB), sconv[0:64, 64 * tau:64 * tau + 64],
                        srct[0:64, tau:tau + 16 * (NB - 1) + 1:16],
                        start=(tau == 0), stop=(tau == 31), skip_group_check=True),
                        waits=use_bank(p, 0, prod_toks + [cs]) if tau == 0 else [])
                e0 = em.op("scalar", lambda e: e.activation(
                    p.scr[0:64, 0:NB], bank(p, 0, 64, NB), AF.Identity,
                    bias=p.pv["conv_b"][0:64, 0:1]), waits=[mt])
                read_bank(p, 0, e0)
                s1 = em.op("tensor", lambda e: e.matmul(
                    bank(p, 1, 1, NB), ones(p, 64, 1), p.scr[0:64, 0:NB],
                    start=True, stop=True), waits=use_bank(p, 1, [e0]))
                sq = em.op("scalar", lambda e: e.activation(
                    p.scr[0:64, 128:128 + NB], p.scr[0:64, 0:NB], AF.Square),
                    waits=[e0])
                s2 = em.op("tensor", lambda e: e.matmul(
                    bank(p, 2, 1, NB), ones(p, 64, 1), p.scr[0:64, 128:128 + NB],
                    start=True, stop=True), waits=use_bank(p, 2, [sq]))
                t1 = em.op("vector", lambda e: e.tensor_scalar(
                    p.strip0[0:1, 0:NB], bank(p, 1, 1, NB), 1.0 / 64, None,
                    op0=ALU.mult), waits=[s1])
                read_bank(p, 1, t1)
                t2 = em.op("vector", lambda e: e.tensor_tensor(
                    p.strip2[0:1, 0:NB], p.strip0[0:1, 0:NB], p.strip0[0:1, 0:NB],
                    op=ALU.mult), waits=[t1])
                t3 = em.op("vector", lambda e: e.scalar_tensor_tensor(
                    p.strip1[0:1, 0:NB], bank(p, 2, 1, NB), 1.0 / 64,
                    p.strip2[0:1, 0:NB], op0=ALU.mult, op1=ALU.subtract),
                    waits=[t2, s2])
                read_bank(p, 2, t3)
                t4 = rsqrt_strip(p, p.strip1[0:1, 0:NB], p.strip1[0:1, 0:NB],
                                 1e-5, [t3], NB)
                b1 = em.op("tensor", lambda e: e.matmul(
                    bank(p, 1, 64, NB), ones(p, 1, 64), p.strip0[0:1, 0:NB],
                    start=True, stop=True), waits=use_bank(p, 1, [t1]))
                b2 = em.op("tensor", lambda e: e.matmul(
                    bank(p, 2, 64, NB), ones(p, 1, 64), p.strip1[0:1, 0:NB],
                    start=True, stop=True), waits=use_bank(p, 2, [t4]))
                a1 = em.op("vector", lambda e: e.tensor_tensor(
                    p.scr[0:64, 256:256 + NB], p.scr[0:64, 0:NB],
                    bank(p, 1, 64, NB), op=ALU.subtract), waits=[b1, e0])
                read_bank(p, 1, a1)
                a2 = em.op("vector", lambda e: e.tensor_tensor(
                    p.scr[0:64, 256:256 + NB], p.scr[0:64, 256:256 + NB],
                    bank(p, 2, 64, NB), op=ALU.mult), waits=[a1, b2])
                read_bank(p, 2, a2)
                a3 = em.op("vector", lambda e: e.tensor_scalar(
                    p.scr[0:64, 256:256 + NB], p.scr[0:64, 256:256 + NB],
                    p.pv["compln_w"][0:64, 0:1], p.pv["compln_b"][0:64, 0:1],
                    op0=ALU.mult, op1=ALU.add), waits=[a2])
                if which == "k":
                    a4 = em.op("scalar", lambda e, h=h: e.activation(
                        kcomp[0:64, h, 0:NB], p.scr[0:64, 256:256 + NB],
                        AF.Gelu), waits=[a3])
                else:
                    a4 = em.op("scalar", lambda e: e.activation(
                        p.scr[0:64, 384:384 + NB], p.scr[0:64, 256:256 + NB],
                        AF.Gelu), waits=[a3])
                    tt = em.op("tensor", lambda e: e.transpose(
                        bank(p, 0, NB, 64), p.scr[0:64, 384:384 + NB],
                        ident(p)[0:64, 0:64]), waits=use_bank(p, 0, [a4]))
                    a4 = em.op("vector", lambda e, h=h: e.tensor_copy(
                        vcomp[0:NB, h, :], bank(p, 0, NB, 64)), waits=[tt])
                    read_bank(p, 0, a4)
                out_toks.append(a4)
            cdep = [out_toks[-1], out_toks[-2]] + ones_toks + getattr(p, "tap_guard", [])
            otc, _ = attn_head(p, h, Qh, Kh, Vth, comp_all[:, h // 2, :],
                               cdep, comp=dict(
                                   kc=kcomp[:, h, :], vtm=vcomp[:, h, :]))
            out_toks.append(otc)
            nsf += [ot, otc] + getattr(p, "tap_guard", [])
            p.tap_guard = []
        slot_free[par2] = nsf

    jn = em.join()
    jn = tap(p, "local", local_all[:].rearrange("p c t -> p (c t)"), jn)
    jn = tap(p, "comp", comp_all[:].rearrange("p c t -> p (c t)"), jn)
    lnl = ln_fm(p, lambda c: local_all[:, c, :], None,
                lambda c: local_all[:, c, :], jn,
                gamma=p.pv["lnloc_w"], beta=p.pv["lnloc_b"], bf_in=True)
    ar.release()
    ar.mark()
    x_fm = ar.alloc("x_fm", [128, NC8, T], F32)
    p.x_fm = x_fm
    gw1 = ar.alloc("gw1", [128, 16, 128], BF16)
    gh = p.stat[:, 0:8]
    ghf = p.stat[:, 8:16]
    gtoks = []
    for oc in range(NC8):
        wt = stream_w_bf16(p, gw1[:],
                           dram["gate_w1"][:, 128 * oc:128 * oc + 128]
                           .rearrange("(f pp) j -> pp f j", pp=128),
                           deps=(gtoks[-1:] if gtoks else lnl[-1:]))
        mt = None
        for fc in range(16):
            rhs = (local_all[:, fc, T - 1:T] if fc < NC8
                   else comp_all[:, fc - NC8, T - 1:T])
            mt = em.op("tensor", lambda e, fc=fc, rhs=rhs, wt=wt: e.matmul(
                bank(p, 3, 128, 1), gw1[:, fc, :], rhs,
                start=(fc == 0), stop=(fc == 15), skip_group_check=True),
                waits=use_bank(p, 3, [wt] + lnl) if fc == 0 else [wt])
        gt = em.op("scalar", lambda e, oc=oc: e.activation(
            gh[:, oc:oc + 1], bank(p, 3, 128, 1), AF.Identity,
            bias=p.pv["gate_b1"][:, oc:oc + 1]), waits=[mt])
        read_bank(p, 3, gt)
        gtoks.append(gt)
    s1 = em.op("tensor", lambda e: e.matmul(
        bank(p, 0, 1, 8), ones(p, 128, 1), gh, start=True, stop=True),
        waits=use_bank(p, 0, gtoks))
    sq = em.op("scalar", lambda e: e.activation(
        p.scr[:, 0:8], gh, AF.Square), waits=gtoks)
    s2 = em.op("tensor", lambda e: e.matmul(
        bank(p, 1, 1, 8), ones(p, 128, 1), p.scr[:, 0:8], start=True,
        stop=True), waits=use_bank(p, 1, [sq]))
    r1 = em.op("vector", lambda e: e.reduce_sum(
        p.strip0[0:1, 0:1], bank(p, 0, 1, 8), axis=AX.X), waits=[s1])
    read_bank(p, 0, r1)
    r2 = em.op("vector", lambda e: e.reduce_sum(
        p.strip1[0:1, 0:1], bank(p, 1, 1, 8), axis=AX.X), waits=[s2])
    read_bank(p, 1, r2)
    r3 = em.op("vector", lambda e: e.tensor_scalar(
        p.strip0[0:1, 0:1], p.strip0[0:1, 0:1], 1.0 / 1024, None, op0=ALU.mult),
        waits=[r1])
    r4 = em.op("vector", lambda e: e.tensor_tensor(
        p.strip2[0:1, 0:1], p.strip0[0:1, 0:1], p.strip0[0:1, 0:1], op=ALU.mult),
        waits=[r3])
    r5 = em.op("vector", lambda e: e.scalar_tensor_tensor(
        p.strip1[0:1, 0:1], p.strip1[0:1, 0:1], 1.0 / 1024, p.strip2[0:1, 0:1],
        op0=ALU.mult, op1=ALU.subtract), waits=[r2, r4])
    r6 = rsqrt_strip(p, p.strip1[0:1, 0:1], p.strip1[0:1, 0:1], 1e-5, [r5], 1)
    b1 = em.op("tensor", lambda e: e.matmul(
        bank(p, 0, 128, 1), ones(p, 1, 128), p.strip0[0:1, 0:1], start=True,
        stop=True), waits=use_bank(p, 0, [r3, r6]))
    b2 = em.op("tensor", lambda e: e.matmul(
        bank(p, 1, 128, 1), ones(p, 1, 128), p.strip1[0:1, 0:1], start=True,
        stop=True), waits=use_bank(p, 1, [r6]))
    e1 = em.op("vector", lambda e: e.tensor_copy(p.stat[:, 16:17],
                                                 bank(p, 0, 128, 1)), waits=[b1])
    read_bank(p, 0, e1)
    e2 = em.op("vector", lambda e: e.tensor_copy(p.stat[:, 17:18],
                                                 bank(p, 1, 128, 1)), waits=[b2])
    read_bank(p, 1, e2)
    a1 = em.op("vector", lambda e: e.tensor_scalar(
        ghf, gh, p.stat[:, 16:17], p.stat[:, 17:18], op0=ALU.subtract,
        op1=ALU.mult), waits=[e1, e2] + gtoks)
    a2 = em.op("vector", lambda e: e.tensor_tensor(
        ghf, ghf, p.pv["gateln_w"][:, 0:8], op=ALU.mult), waits=[a1])
    a3 = em.op("vector", lambda e: e.tensor_tensor(
        ghf, ghf, p.pv["gateln_b"][:, 0:8], op=ALU.add), waits=[a2])
    a4 = em.op("scalar", lambda e: e.activation(ghf, ghf, AF.Relu), waits=[a3])
    mt = None
    for oc in range(NC8):
        mt = em.op("tensor", lambda e, oc=oc: e.matmul(
            bank(p, 2, 2, 1), p.pv["gate_w2"][:, 2 * oc:2 * oc + 2],
            ghf[:, oc:oc + 1], start=(oc == 0), stop=(oc == 7),
            skip_group_check=True),
            waits=use_bank(p, 2, [a4] + p.ctok) if oc == 0 else [a4])
    l1 = em.op("scalar", lambda e: e.activation(
        p.l2[0:2, 0:1], bank(p, 2, 2, 1), AF.Identity,
        bias=p.pv["gate_b2"][0:2, 0:1]), waits=[mt])
    read_bank(p, 2, l1)
    dmm = em.op("tensor", lambda e: e.matmul(
        bank(p, 3, 1, 1), p.cf[0:2, PM_O:PM_O + 1], p.l2[0:2, 0:1],
        start=True, stop=True), waits=use_bank(p, 3, [l1]))
    g0 = em.op("scalar", lambda e: e.activation(
        p.strip0[0:1, 8:9], bank(p, 3, 1, 1), AF.Sigmoid), waits=[dmm])
    read_bank(p, 3, g0)
    g1 = em.op("vector", lambda e: e.tensor_scalar(
        p.strip0[0:1, 9:10], p.strip0[0:1, 8:9], -1.0, 1.0, op0=ALU.mult,
        op1=ALU.add), waits=[g0])
    bg = em.op("tensor", lambda e: e.matmul(
        bank(p, 0, 128, 2), ones(p, 1, 128), p.strip0[0:1, 8:10], start=True,
        stop=True), waits=use_bank(p, 0, [g1]))
    gv = em.op("vector", lambda e: e.tensor_copy(p.stat[:, 18:20],
                                                 bank(p, 0, 128, 2)), waits=[bg])
    read_bank(p, 0, gv)
    comb_toks = []
    for c in range(NC8):
        t1 = em.op("vector", lambda e, c=c: e.tensor_scalar(
            p.scr[:, 0:2048], local_all[:, c, :], p.stat[:, 18:19], None,
            op0=ALU.mult), waits=[gv] + lnl)
        t2 = em.op("vector", lambda e, c=c: e.scalar_tensor_tensor(
            comp_all[:, c, :], comp_all[:, c, :], p.stat[:, 19:20],
            p.scr[:, 0:2048], op0=ALU.mult, op1=ALU.add), waits=[t1])
        comb_toks.append(t2)
    comb_toks = tap(p, "combined", comp_all[:].rearrange("p c t -> p (c t)"),
                    comb_toks)
    xr = em.dma("sync", lambda e, xh=p.x_hbm: e.dma_start(
        out=x_fm[:],
        in_=xh[:, :].rearrange("(c p) t -> p c t", p=128)),
        waits=lnl)
    res_toks = []
    for oc in range(NC8):
        wt = stream_w_bf16(p, gw1[:, 0:8, :],
                           dram["cproj_w"][:, 128 * oc:128 * oc + 128]
                           .rearrange("(c pp) j -> pp c j", pp=128),
                           deps=(res_toks[-4:] if res_toks else gtoks[-1:]))
        for tc in range(4):
            sltc = slice(512 * tc, 512 * tc + 512)
            mt = None
            for c in range(NC8):
                mt = em.op("tensor", lambda e, c=c, oc=oc, sltc=sltc: e.matmul(
                    bank(p, 4 + oc % 2),
                    gw1[:, c, :],
                    comp_all[:, c, sltc],
                    start=(c == 0), stop=(c == NC8 - 1), skip_group_check=True),
                    waits=use_bank(p, 4 + oc % 2, [wt] + comb_toks) if c == 0 else [wt])
            ee1 = em.op("scalar", lambda e, oc=oc: e.activation(
                p.scr2[:, 0:512] if oc % 2 == 0 else p.scr2[:, 512:1024],
                bank(p, 4 + oc % 2), AF.Identity,
                bias=p.pv["cproj_b"][:, oc:oc + 1]), waits=[mt])
            read_bank(p, 4 + oc % 2, ee1)
            ee2 = em.op("vector", lambda e, oc=oc, sltc=sltc: e.tensor_tensor(
                x_fm[:, oc, sltc],
                p.scr2[:, 0:512] if oc % 2 == 0 else p.scr2[:, 512:1024],
                x_fm[:, oc, sltc], op=ALU.add), waits=[ee1, xr])
            res_toks.append(ee2)
    lnt = ln_fm(p, lambda c: x_fm[:, c, :], lambda c: x_fm[:, c, :],
                lambda c: p.x_bf[:, c, :], res_toks)
    if is_last:
        yt = em.dma("sync", lambda e: e.dma_start(
            out=dram["y"][:, :].rearrange("(c p) t -> p c t", p=128),
            in_=x_fm[:]), waits=lnt)
        out = [yt]
    else:
        st = em.dma("sync", lambda e: e.dma_start(
            out=dram["spill_x"][:, :].rearrange("(c p) t -> p c t", p=128),
            in_=x_fm[:]), waits=lnt)
        p.x_hbm = dram["spill_x"]
        out = [st] + lnt
    ar.release()
    ar.release()
    return out


def emit_l0_ffn(p, entry, is_last=False):
    em = p.em
    ar = p.ar
    dram = p.dram
    st = p.stat
    MX1 = 0
    SC1 = 16
    RS1 = 32
    TMP = 48
    M1B = 52
    M2B = 55
    AC1 = 58
    MX2 = 32

    ar.mark()
    hsil = ar.alloc("hsil", [128, 32, T], BF16)
    p._dbg_hsil = hsil
    s2bc = ar.alloc("s2bc", [128, T], BF16)
    ar.mark()
    h_full = ar.alloc("h_full", [128, 2050], F32)
    wsl = [ar.alloc(f"fw{i}", [128, 8, 128], BF16) for i in range(2)]
    xtmp = ar.alias("xtmp", [128, T], F32, h_full)
    scri = ar.alias("scri", [128, 1024], I32, p.scr2)

    def w_stats(src_tiles, acc_col, mean_col, thr_col, cnt, deps):
        toks = []
        z = em.op("vector", lambda e, acc_col=acc_col: e.memset(
            st[:, acc_col:acc_col + 1], 0.0), waits=deps)
        bufs = [xtmp, p.scr]
        prev = [z]
        ld_prev = {}
        for i, (ap, cols) in enumerate(src_tiles):
            buf = bufs[i % 2]
            ld = em.dma("sync", lambda e, ap=ap, buf=buf, cols=cols: e.dma_start(
                out=buf[:, 0:cols], in_=ap), waits=ld_prev.get(i % 2, []) + deps)
            rd = em.op("vector", lambda e, buf=buf, cols=cols: e.reduce_sum(
                st[:, TMP:TMP + 1], buf[:, 0:cols], axis=AX.X,
                apply_absolute_value=True), waits=[ld] + prev)
            ac = em.op("vector", lambda e, acc_col=acc_col: e.tensor_tensor(
                st[:, acc_col:acc_col + 1], st[:, acc_col:acc_col + 1],
                st[:, TMP:TMP + 1], op=ALU.add), waits=[rd])
            prev = [ac]
            ld_prev[i % 2] = [rd]
        pm = em.op("tensor", lambda e, acc_col=acc_col: e.matmul(
            bank(p, 3, 1, 1), st[:, acc_col:acc_col + 1], ones(p, 128, 1),
            start=True, stop=True), waits=use_bank(p, 3, prev + p.ctok))
        m1 = em.op("vector", lambda e, cnt=cnt: e.tensor_scalar(
            st[0:1, TMP:TMP + 1], bank(p, 3, 1, 1), 1.0 / cnt, None,
            op0=ALU.mult), waits=[pm])
        read_bank(p, 3, m1)
        m2 = em.op("vector", lambda e: e.tensor_scalar(
            st[0:1, TMP:TMP + 1], st[0:1, TMP:TMP + 1], 1e-5, None,
            op0=ALU.max), waits=[m1])
        bc = em.op("tensor", lambda e: e.matmul(
            bank(p, 3, 128, 1), ones(p, 1, 128), st[0:1, TMP:TMP + 1],
            start=True, stop=True), waits=use_bank(p, 3, [m2]))
        mb = em.op("vector", lambda e, mean_col=mean_col: e.tensor_copy(
            st[:, mean_col:mean_col + 1], bank(p, 3, 128, 1)), waits=[bc])
        read_bank(p, 3, mb)
        t1 = em.op("vector", lambda e, mean_col=mean_col, thr_col=thr_col:
                   e.tensor_scalar(st[:, thr_col:thr_col + 1],
                                   st[:, mean_col:mean_col + 1], 0.5, None,
                                   op0=ALU.mult), waits=[mb])
        t2 = em.op("vector", lambda e, mean_col=mean_col, thr_col=thr_col:
                   e.tensor_scalar(st[:, thr_col + 1:thr_col + 2],
                                   st[:, mean_col:mean_col + 1], -0.5, None,
                                   op0=ALU.mult), waits=[mb])
        return [t1, t2]

    def w_tern(src_tiles, dst_tiles, thr_col, deps, dma_guard=()):
        toks = []
        prev_q = deps
        prev_dma = list(dma_guard)
        for (ap, cols), dap in zip(src_tiles, dst_tiles):
            ld = em.dma("sync", lambda e, ap=ap, cols=cols: e.dma_start(
                out=xtmp[:, 0:cols], in_=ap), waits=prev_q + deps)
            c1 = em.op("vector", lambda e, cols=cols, thr_col=thr_col:
                       e.tensor_scalar(p.scr[:, 0:cols], xtmp[:, 0:cols],
                                       st[:, thr_col + 1:thr_col + 2], None,
                                       op0=ALU.is_lt), waits=[ld])
            q = em.op("vector", lambda e, cols=cols, thr_col=thr_col:
                      e.scalar_tensor_tensor(p.scrb[:, 0:cols], xtmp[:, 0:cols],
                                             st[:, thr_col:thr_col + 1],
                                             p.scr[:, 0:cols], op0=ALU.is_gt,
                                             op1=ALU.subtract),
                      waits=[c1] + prev_dma)
            wd = em.dma("gpsimd", lambda e, dap=dap, cols=cols: e.dma_start(
                out=dap, in_=p.scrb[:, 0:cols]), waits=[q])
            prev_q = [q]
            prev_dma = [wd]
            toks.append(wd)
        return toks

    w1_tiles = [(dram["ffn_w1t"][128 * c:128 * c + 128, 2048 * h:2048 * h + 2048],
                 2048)
                for c in range(8) for h in range(2)]
    w1_dst = [dram["t1_scr"][128 * c:128 * c + 128, 2048 * h:2048 * h + 2048]
              for c in range(8) for h in range(2)]
    w2_tiles = [(dram["ffn_w2t"][128 * k:128 * k + 128, 0:1024], 1024)
                for k in range(32)]
    w2_dst = [dram["t2_scr"][128 * k:128 * k + 128, 0:1024] for k in range(32)]

    s1t = w_stats(w1_tiles, AC1, M1B, M1B + 1, 4096 * 1024, entry_w(p, entry))
    tn1 = w_tern(w1_tiles, w1_dst, M1B + 1, s1t)
    s2t = w_stats(w2_tiles, AC1 + 1, M2B, M2B + 1, 4096 * 1024,
                  s1t + tn1[-1:])
    tn2 = w_tern(w2_tiles, w2_dst, M2B + 1, s2t, dma_guard=tn1[-1:])

    bufs = [xtmp, p.scr]
    ld_prev = {}
    mx_toks = []
    prev_acc = tn2[-1:]
    for c in range(NC8):
        buf = bufs[c % 2]
        ld = em.dma("sync", lambda e, c=c, buf=buf, xh=p.x_hbm: e.dma_start(
            out=buf[:, :], in_=xh[128 * c:128 * c + 128, :]),
            waits=ld_prev.get(c % 2, []) + entry + (tn2[-1:] if c < 2 else []))
        for i in range(NT):
            tr = em.op("tensor", lambda e, buf=buf, i=i: e.transpose(
                bank(p, 2, 128, 128), buf[:, 128 * i:128 * i + 128], ident(p)),
                waits=use_bank(p, 2, [ld] + p.ctok))
            if c == 0:
                rd = em.op("vector", lambda e, i=i: e.reduce_max(
                    st[:, MX1 + i:MX1 + i + 1], bank(p, 2, 128, 128), axis=AX.X,
                    apply_absolute_value=True), waits=[tr])
            else:
                rd = em.op("vector", lambda e: e.reduce_max(
                    st[:, TMP:TMP + 1], bank(p, 2, 128, 128), axis=AX.X,
                    apply_absolute_value=True), waits=[tr] + prev_acc)
                rd = em.op("vector", lambda e, i=i: e.tensor_tensor(
                    st[:, MX1 + i:MX1 + i + 1], st[:, MX1 + i:MX1 + i + 1],
                    st[:, TMP:TMP + 1], op=ALU.max), waits=[rd])
                prev_acc = [rd]
            read_bank(p, 2, rd)
            mx_toks.append(rd)
        ld_prev[c % 2] = mx_toks[-NT:]
    mc = em.op("vector", lambda e: e.tensor_scalar(
        st[:, MX1:MX1 + 16], st[:, MX1:MX1 + 16], 1e-5, None, op0=ALU.max),
        waits=mx_toks)
    sl1 = em.op("scalar", lambda e: e.activation(
        st[:, SC1:SC1 + 16], st[:, MX1:MX1 + 16], AF.Ln), waits=[mc])
    sl2 = em.op("scalar", lambda e: e.activation(
        st[:, SC1:SC1 + 16], st[:, SC1:SC1 + 16], AF.Exp, scale=-1.0,
        bias=p.cf[:, LN127_O:LN127_O + 1]), waits=[sl1] + p.ctok)
    rs1 = em.op("vector", lambda e: e.tensor_scalar(
        st[:, RS1:RS1 + 16], st[:, MX1:MX1 + 16], st[:, M1B:M1B + 1],
        1.0 / 127.0, op0=ALU.mult, op1=ALU.mult), waits=[mc] + s1t)
    sc_bc = []
    rs_toks = []
    for i in range(NT):
        d1 = em.op("vector", lambda e, i=i: e.tensor_scalar(
            p.msks[:, :], ident(p), st[:, SC1 + i:SC1 + i + 1], None,
            op0=ALU.mult), waits=[sl2] + (sc_bc[-1:] if sc_bc else []))
        m1 = em.op("tensor", lambda e, i=i: e.matmul(
            bank(p, i // 4, 128, 128, coloff=128 * (i % 4)), ones(p, 128, 128),
            p.msks[:, :], start=(i % 4 == 0), stop=(i % 4 == 3),
            skip_group_check=True),
            waits=use_bank(p, i // 4, [d1]) if i % 4 == 0 else [d1])
        sc_bc.append(m1)
        d2 = em.op("vector", lambda e, i=i: e.tensor_scalar(
            p.msks[:, :], ident(p), st[:, RS1 + i:RS1 + i + 1], None,
            op0=ALU.mult), waits=[rs1, m1])
        m2 = em.op("tensor", lambda e, i=i: e.matmul(
            bank(p, 4 + i // 4, 128, 128, coloff=128 * (i % 4)),
            ones(p, 128, 128), p.msks[:, :], start=(i % 4 == 0),
            stop=(i % 4 == 3), skip_group_check=True),
            waits=use_bank(p, 4 + i // 4, [d2]) if i % 4 == 0 else [d2])
        sc_bc.append(m2)
        rs_toks.append(m2)
    s2bt = []
    for b in range(4):
        t = em.op("vector", lambda e, b=b: e.tensor_copy(
            s2bc[:, 512 * b:512 * b + 512], bank(p, 4 + b, 128, 512)),
            waits=rs_toks)
        read_bank(p, 4 + b, t)
        s2bt.append(t)
    xq_toks = []
    ld_prev = []
    for c in range(NC8):
        ld = em.dma("sync", lambda e, c=c, xh=p.x_hbm: e.dma_start(
            out=xtmp[:, :], in_=xh[128 * c:128 * c + 128, :]),
            waits=ld_prev + entry)
        for hf in range(2):
            hsl = slice(1024 * hf, 1024 * hf + 1024)
            v = em.op("vector", lambda e, hsl=hsl: e.tensor_tensor(
                p.scr[:, 0:1024], xtmp[:, hsl], p.PA[:, hsl], op=ALU.mult),
                waits=[ld] + sc_bc + xq_toks[-1:])
            u = em.op("scalar", lambda e: e.activation(
                p.scr[:, 1024:2048], p.scr[:, 0:1024], AF.Identity,
                bias=p.cf[:, F512_O:F512_O + 1]), waits=[v])
            qi = em.op("vector", lambda e: e.tensor_copy(
                scri[:, 0:1024], p.scr[:, 1024:2048]), waits=[u])
            qf = em.op("scalar", lambda e: e.activation(
                p.scr[:, 0:1024], scri[:, 0:1024], AF.Identity), waits=[qi])
            cc = em.op("vector", lambda e: e.scalar_tensor_tensor(
                p.scrb[:, 0:1024], p.scr[:, 1024:2048], -0.5, p.scr[:, 0:1024],
                op0=ALU.add, op1=ALU.is_gt), waits=[qf])
            q = em.op("vector", lambda e, c=c, hsl=hsl: e.scalar_tensor_tensor(
                p.x_bf[:, c, hsl], p.scr[:, 0:1024], -512.0, p.scrb[:, 0:1024],
                op0=ALU.add, op1=ALU.add), waits=[cc])
            xq_toks.append(q)
        ld_prev = xq_toks[-2:]
    for b in range(4):
        read_bank(p, b, xq_toks[-1])

    ed0 = em.op("vector", lambda e: e.memset(h_full[:, 0:1], 0.0),
                waits=xq_toks[-1:])
    ed1 = em.op("vector", lambda e: e.memset(h_full[:, 2049:2050], 0.0),
                waits=xq_toks[-1:])
    edges = [ed0, ed1]
    wtok = [None, None]
    wsl_free = [[], []]
    prev_silu = None
    prev_conv = []
    mx2_first = True
    mx2_prev = []
    for j in range(32):
        wtok[j % 2] = em.dma("gpsimd", lambda e, j=j: e.dma_start(
            out=wsl[j % 2][:], in_=dram["t1_scr"][:, 128 * j:128 * j + 128]
            .rearrange("(c p) j2 -> p c j2", p=128)),
            waits=wsl_free[j % 2] + (tn1[-1:] if j < 2 else []))
        ev_toks = []
        mm_last = []
        for tc in range(4):
            sltc = slice(512 * tc, 512 * tc + 512)
            bk = tc % 2
            mt = None
            for c in range(NC8):
                mt = em.op("tensor", lambda e, j=j, c=c, sltc=sltc, bk=bk: e.matmul(
                    bank(p, bk, 128, 512), wsl[j % 2][:, c, :],
                    p.x_bf[:, c, sltc], start=(c == 0), stop=(c == NC8 - 1),
                    skip_group_check=True),
                    waits=use_bank(p, bk, [wtok[j % 2]] + xq_toks[-1:])
                    if c == 0 else [wtok[j % 2]])
            hv = em.op("vector", lambda e, tc=tc, sltc=sltc, bk=bk: e.tensor_tensor(
                h_full[:, 1 + 512 * tc:1 + 512 * tc + 512], bank(p, bk, 128, 512),
                s2bc[:, sltc], op=ALU.mult),
                waits=[mt] + s2bt + edges + prev_conv)
            read_bank(p, bk, hv)
            mm_last.append(mt)
            ev_toks.append(hv)
        wsl_free[j % 2] = [mm_last[-1]]
        w0 = p.pv["ffn_convw"][:, 3 * j:3 * j + 1]
        w1a = p.pv["ffn_convw"][:, 3 * j + 1:3 * j + 2]
        w2a = p.pv["ffn_convw"][:, 3 * j + 2:3 * j + 3]
        c1 = em.op("vector", lambda e, w0=w0: e.tensor_scalar(
            p.scr[:, 0:2048], h_full[:, 0:2048], w0, None, op0=ALU.mult),
            waits=ev_toks + ([prev_silu] if prev_silu else []) + p.ctok)
        c2 = em.op("vector", lambda e, w1a=w1a: e.scalar_tensor_tensor(
            p.scr[:, 0:2048], h_full[:, 1:2049], w1a, p.scr[:, 0:2048],
            op0=ALU.mult, op1=ALU.add), waits=[c1])
        c3 = em.op("vector", lambda e, w2a=w2a: e.scalar_tensor_tensor(
            p.scr[:, 0:2048], h_full[:, 2:2050], w2a, p.scr[:, 0:2048],
            op0=ALU.mult, op1=ALU.add), waits=[c2])
        sl_t = em.op("scalar", lambda e, j=j: e.activation(
            hsil[:, j, :], p.scr[:, 0:2048], AF.Silu,
            bias=p.pv["ffn_convb"][:, j:j + 1]), waits=[c3])
        sl_f = em.op("scalar", lambda e, j=j: e.activation(
            h_full[:, 1:2049], p.scr[:, 0:2048], AF.Silu,
            bias=p.pv["ffn_convb"][:, j:j + 1]), waits=[c3])
        prev_conv = [c3, sl_f]
        prev_silu = sl_t
        for i in range(NT):
            tr = em.op("tensor", lambda e, j=j, i=i: e.transpose(
                bank(p, 2, 128, 128), h_full[:, 1 + 128 * i:1 + 128 * i + 128],
                ident(p)), waits=use_bank(p, 2, [sl_f]))
            if mx2_first:
                rd = em.op("vector", lambda e, i=i: e.reduce_max(
                    st[:, MX2 + i:MX2 + i + 1], bank(p, 2, 128, 128),
                    axis=AX.X, apply_absolute_value=True), waits=[tr])
            else:
                rd = em.op("vector", lambda e: e.reduce_max(
                    st[:, TMP + 1:TMP + 2], bank(p, 2, 128, 128), axis=AX.X,
                    apply_absolute_value=True), waits=[tr] + mx2_prev)
                rd = em.op("vector", lambda e, i=i: e.tensor_tensor(
                    st[:, MX2 + i:MX2 + i + 1], st[:, MX2 + i:MX2 + i + 1],
                    st[:, TMP + 1:TMP + 2], op=ALU.max), waits=[rd])
                mx2_prev = [rd]
            read_bank(p, 2, rd)
        mx2_first = False
        mx2_toks = [rd]
    mc2 = em.op("vector", lambda e: e.tensor_scalar(
        st[:, MX2:MX2 + 16], st[:, MX2:MX2 + 16], 1e-5, None, op0=ALU.max),
        waits=mx2_toks + mx2_prev)
    q1 = em.op("scalar", lambda e: e.activation(
        st[:, SC1:SC1 + 16], st[:, MX2:MX2 + 16], AF.Ln), waits=[mc2])
    q2 = em.op("scalar", lambda e: e.activation(
        st[:, SC1:SC1 + 16], st[:, SC1:SC1 + 16], AF.Exp, scale=-1.0,
        bias=p.cf[:, LN127_O:LN127_O + 1]), waits=[q1])
    rs2 = em.op("vector", lambda e: e.tensor_scalar(
        st[:, MX1:MX1 + 16], st[:, MX2:MX2 + 16], st[:, M2B:M2B + 1],
        1.0 / 127.0, op0=ALU.mult, op1=ALU.mult), waits=[mc2] + s2t)
    sc_bc2 = []
    rs_toks2 = []
    for i in range(NT):
        d1 = em.op("vector", lambda e, i=i: e.tensor_scalar(
            p.msks[:, :], ident(p), st[:, SC1 + i:SC1 + i + 1], None,
            op0=ALU.mult), waits=[q2] + (sc_bc2[-1:] if sc_bc2 else []))
        m1 = em.op("tensor", lambda e, i=i: e.matmul(
            bank(p, i // 4, 128, 128, coloff=128 * (i % 4)), ones(p, 128, 128),
            p.msks[:, :], start=(i % 4 == 0), stop=(i % 4 == 3),
            skip_group_check=True),
            waits=use_bank(p, i // 4, [d1]) if i % 4 == 0 else [d1])
        sc_bc2.append(m1)
        d2 = em.op("vector", lambda e, i=i: e.tensor_scalar(
            p.msks[:, :], ident(p), st[:, MX1 + i:MX1 + i + 1], None,
            op0=ALU.mult), waits=[rs2, m1])
        m2 = em.op("tensor", lambda e, i=i: e.matmul(
            bank(p, 4 + i // 4, 128, 128, coloff=128 * (i % 4)),
            ones(p, 128, 128), p.msks[:, :], start=(i % 4 == 0),
            stop=(i % 4 == 3), skip_group_check=True),
            waits=use_bank(p, 4 + i // 4, [d2]) if i % 4 == 0 else [d2])
        sc_bc2.append(m2)
        rs_toks2.append(m2)
    s2bt2 = []
    for b in range(4):
        t = em.op("vector", lambda e, b=b: e.tensor_copy(
            s2bc[:, 512 * b:512 * b + 512], bank(p, 4 + b, 128, 512)),
            waits=rs_toks2 + s2bt)
        read_bank(p, 4 + b, t)
        s2bt2.append(t)
    hq_toks = []
    for kc in range(32):
        for hf in range(2):
            hsl = slice(1024 * hf, 1024 * hf + 1024)
            v = em.op("vector", lambda e, kc=kc, hsl=hsl: e.tensor_tensor(
                p.scr[:, 0:1024], hsil[:, kc, hsl], p.PA[:, hsl], op=ALU.mult),
                waits=sc_bc2 + hq_toks[-1:])
            u = em.op("scalar", lambda e: e.activation(
                p.scr[:, 1024:2048], p.scr[:, 0:1024], AF.Identity,
                bias=p.cf[:, F512_O:F512_O + 1]), waits=[v])
            qi = em.op("vector", lambda e: e.tensor_copy(
                scri[:, 0:1024], p.scr[:, 1024:2048]), waits=[u])
            qf = em.op("scalar", lambda e: e.activation(
                p.scr[:, 0:1024], scri[:, 0:1024], AF.Identity), waits=[qi])
            cc = em.op("vector", lambda e: e.scalar_tensor_tensor(
                p.scrb[:, 0:1024], p.scr[:, 1024:2048], -0.5, p.scr[:, 0:1024],
                op0=ALU.add, op1=ALU.is_gt), waits=[qf])
            q = em.op("vector", lambda e, kc=kc, hsl=hsl: e.scalar_tensor_tensor(
                hsil[:, kc, hsl], p.scr[:, 0:1024], -512.0, p.scrb[:, 0:1024],
                op0=ALU.add, op1=ALU.add), waits=[cc])
            hq_toks.append(q)
    for b in range(4):
        read_bank(p, b, hq_toks[-1])

    ar.release()
    hw = [ar.alloc(f"hw{i}", [128, 16, 128], BF16) for i in range(3)]
    hw_free = [[], [], []]
    wout_toks = {tc: [] for tc in range(4)}
    res_ld = []
    scrh_free = {0: hq_toks[-1:], 1: hq_toks[-1:]}
    scr2_free = {0: hq_toks[-1:], 1: hq_toks[-1:]}
    for oc in range(NC8):
        ht = []
        for hh in range(2):
            s = (2 * oc + hh) % 3
            t = em.dma("gpsimd", lambda e, oc=oc, hh=hh, s=s: e.dma_start(
                out=hw[s][:], in_=dram["t2_scr"]
                [2048 * hh:2048 * hh + 2048, 128 * oc:128 * oc + 128]
                .rearrange("(c p) o -> p c o", p=128)),
                waits=hw_free[s] + (tn2[-1:] + hq_toks[-1:] if oc == 0 else []))
            ht.append(t)
        mm_by_tc = {}
        for hh in range(2):
            s = (2 * oc + hh) % 3
            last_mm = None
            for tc in range(4):
                sltc = slice(512 * tc, 512 * tc + 512)
                bk = (4 * oc + tc) % 8
                for c16 in range(16):
                    kc = 16 * hh + c16
                    w0 = (hh == 0 and c16 == 0)
                    mm = em.op("tensor", lambda e, s=s, c16=c16, kc=kc, bk=bk,
                               sltc=sltc, w0=w0: e.matmul(
                        bank(p, bk, 128, 512), hw[s][:, c16, :],
                        hsil[:, kc, sltc], start=w0,
                        stop=(kc == 31 and c16 == 15), skip_group_check=True),
                        waits=(use_bank(p, bk, [ht[hh]] + hq_toks[-1:])
                               if w0 else [ht[hh]]))
                    last_mm = mm
                mm_by_tc[tc] = mm
            hw_free[s] = [last_mm]
        for tc in range(4):
            sltc = slice(512 * tc, 512 * tc + 512)
            bk = (4 * oc + tc) % 8
            sh = 512 * (tc % 2)
            rx = em.dma("sync", lambda e, oc=oc, sltc=sltc, tc=tc,
                        xh=p.x_hbm: e.dma_start(
                out=p.scr2[:, 512 * (tc % 2):512 * (tc % 2) + 512],
                in_=xh[128 * oc:128 * oc + 128, sltc]),
                waits=res_ld[-1:] + scr2_free.get(tc % 2, []))
            res_ld.append(rx)
            o1 = em.op("vector", lambda e, bk=bk, sltc=sltc, sh=sh: e.tensor_tensor(
                p.scr[:, sh:sh + 512], bank(p, bk, 128, 512), s2bc[:, sltc],
                op=ALU.mult),
                waits=[mm_by_tc[tc]] + s2bt2 + scrh_free.get(tc % 2, []))
            o2 = em.op("vector", lambda e, sh=sh, tc=tc: e.tensor_tensor(
                p.scr[:, sh:sh + 512], p.scr[:, sh:sh + 512],
                p.scr2[:, 512 * (tc % 2):512 * (tc % 2) + 512], op=ALU.add),
                waits=[o1, rx])
            read_bank(p, bk, o2)
            scr2_free[tc % 2] = [o2]
            wd = em.dma("scalar", lambda e, oc=oc, sltc=sltc, sh=sh: e.dma_start(
                out=dram["spill_x2"][128 * oc:128 * oc + 128, sltc],
                in_=p.scr[:, sh:sh + 512]), waits=[o2])
            scrh_free[tc % 2] = [wd]
            wout_toks[tc].append(wd)

    ar.release()
    return postln_stream(p, wout_toks, is_last, dram["spill_x"])


def postln_stream(p, wout_toks, is_last, new_xhbm):
    em = p.em
    ar = p.ar
    dram = p.dram
    ar.mark()
    stage16 = ar.alloc("stage16", [128, NC8, 512], F32)
    out_toks = []
    st_free = []
    for tc in range(4):
        sltc = slice(512 * tc, 512 * tc + 512)
        ld = em.dma("sync", lambda e, sltc=sltc: e.dma_start(
            out=stage16[:], in_=dram["spill_x2"][:, sltc]
            .rearrange("(c p) t -> p c t", p=128)),
            waits=wout_toks[tc] + st_free)
        lnt = ln_fm(p, lambda c: stage16[:, c, :],
                    lambda c: stage16[:, c, :],
                    lambda c: p.x_bf[:, c, :], [ld],
                    blocks=[(slice(0, 512), slice(0, 512), sltc)])
        dst = dram["y"] if is_last else new_xhbm
        wv = em.dma("sync", lambda e, dst=dst, sltc=sltc: e.dma_start(
            out=dst[:, sltc].rearrange("(c p) t -> p c t", p=128),
            in_=stage16[:]), waits=lnt)
        st_free = [wv] + lnt
        out_toks += [wv] + lnt
    ar.release()
    if not is_last:
        p.x_hbm = new_xhbm
    return out_toks


def entry_w(p, entry):
    return p.ctok


def emit_l1_attn(p, entry, is_last=False):
    em = p.em
    ar = p.ar
    dram = p.dram
    ar.mark()
    latq = ar.alloc("latq", [128, NC8, T], BF16)
    latk = ar.alloc("latk", [128, NC8, T], BF16)
    vtm_all = ar.alloc("vtm_all", [128, NH, NT, 64], BF16)
    Qaug = ar.alloc("l1Q", [128, T], BF16)
    Kaug = ar.alloc("l1K", [128, T], BF16)
    Ebuf = ar.alloc("l1E", [128, 512], BF16)
    ypair = ar.alloc("ypair", [128, T], BF16)
    ropec = ar.alloc("ropec", [128, T], F32)
    upk = ar.alloc("upk", [128, 320], BF16)
    smallf = ar.alloc("smallf", [128, 64], F32)
    wds = [ar.alloc(f"l1wd{i}", [128, 8, 128], BF16) for i in range(2)]

    rc = em.dma("sync", lambda e: e.dma_start(
        out=ropec[:], in_=dram["constf"][:, COS_O:COS_O + T]), waits=entry)
    ko = em.op("vector", lambda e: e.memset(Kaug[64:66, :], 1.0), waits=entry)

    upk_toks = []
    for off, nmc, nme in ((0, "wq_upc", "wq_upe"), (128, "wk_upc", "wk_upe")):
        l1 = em.dma("sync", lambda e, nmc=nmc: e.dma_start(
            out=smallf[0:64, 0:32], in_=dram[nmc][:, :]), waits=upk_toks[-1:])
        l2 = em.dma("sync", lambda e, nmc=nmc: e.dma_start(
            out=smallf[64:128, 0:32], in_=dram[nmc][:, :]))
        l3 = em.dma("sync", lambda e, nme=nme: e.dma_start(
            out=smallf[0:64, 32:64], in_=dram[nme][:, :]))
        l4 = em.dma("sync", lambda e, nme=nme: e.dma_start(
            out=smallf[64:128, 32:64], in_=dram[nme][:, :]))
        lds = [l1, l2, l3, l4]
        t1 = em.op("vector", lambda e, off=off: e.tensor_copy(
            upk[:, off:off + 64], smallf[:, 0:64]), waits=lds)
        t2 = em.op("vector", lambda e, off=off: e.memset(
            upk[:, off + 64:off + 96], 0.0), waits=entry)
        t3 = em.op("vector", lambda e, off=off: e.tensor_scalar(
            upk[:, off + 96:off + 128:2], smallf[:, 33:64:2], -1.0, None,
            op0=ALU.mult), waits=lds)
        t4 = em.op("vector", lambda e, off=off: e.tensor_copy(
            upk[:, off + 97:off + 128:2], smallf[:, 32:64:2]), waits=lds)
        upk_toks += [t1, t2, t3, t4]
    l5 = em.dma("sync", lambda e: e.dma_start(
        out=smallf[0:64, 0:64], in_=dram["wv_up"][:, :]), waits=upk_toks)
    l6 = em.dma("sync", lambda e: e.dma_start(
        out=smallf[64:128, 0:64], in_=dram["wv_up"][:, :]))
    t5 = em.op("vector", lambda e: e.tensor_copy(
        upk[:, 256:320], smallf[:, 0:64]), waits=[l5, l6])
    upk_toks.append(t5)

    def produce_lat(dst, wname, deps):
        toks = []
        wfree = [[], []]
        for oc in range(NC8):
            wt = em.dma("gpsimd", lambda e, oc=oc, wname=wname: e.dma_start(
                out=wds[oc % 2][:], in_=dram[wname][:, 128 * oc:128 * oc + 128]
                .rearrange("(c p) j -> p c j", p=128)),
                waits=wfree[oc % 2] + (deps if oc < 2 else []))
            for tc in range(4):
                sltc = slice(512 * tc, 512 * tc + 512)
                bk = tc % 2
                mt = None
                for c in range(NC8):
                    mt = em.op("tensor", lambda e, oc=oc, c=c, sltc=sltc, bk=bk:
                               e.matmul(bank(p, bk, 128, 512),
                                        wds[oc % 2][:, c, :], p.x_bf[:, c, sltc],
                                        start=(c == 0), stop=(c == NC8 - 1),
                                        skip_group_check=True),
                               waits=use_bank(p, bk, [wt] + deps)
                               if c == 0 else [wt])
                ev = em.op("vector", lambda e, oc=oc, sltc=sltc, bk=bk, dst=dst:
                           e.tensor_copy(dst[:, oc, sltc], bank(p, bk, 128, 512)),
                           waits=[mt])
                read_bank(p, bk, ev)
                toks.append(ev)
            wfree[oc % 2] = toks[-4:]
        return toks

    lv = produce_lat(latq, "wv_down", list(entry) + p.ctok)
    vc_toks = []
    for h in range(NH):
        base = 64 * (h % 2)
        for kt in range(NT):
            mm = em.op("tensor", lambda e, h=h, kt=kt, base=base: e.matmul(
                bank(p, 2, 128, 64), latq[base:base + 64, h // 2,
                                          128 * kt:128 * kt + 128],
                upk[base:base + 64, 256:320], start=True, stop=True),
                waits=use_bank(p, 2, lv + upk_toks))
            ev = em.op("vector", lambda e, h=h, kt=kt: e.tensor_copy(
                vtm_all[:, h, kt, :], bank(p, 2, 128, 64)), waits=[mm])
            read_bank(p, 2, ev)
            vc_toks.append(ev)
    lq = produce_lat(latq, "wq_down", vc_toks[-1:])
    lk = produce_lat(latk, "wk_down", vc_toks[-1:])
    prod_deps = lq + lk

    ypair_free = []
    youts = []
    prev_attn = []
    for h in range(NH):
        base = 64 * (h % 2)
        bld = []
        for dst, lat, off in ((Qaug, latq, 0), (Kaug, latk, 128)):
            for tc in range(4):
                sltc = slice(512 * tc, 512 * tc + 512)
                m1 = em.op("tensor", lambda e, h=h, base=base, off=off,
                           sltc=sltc, lat=lat: e.matmul(
                    bank(p, 0, 128, 512), upk[base:base + 64, off:off + 128],
                    lat[base:base + 64, h // 2, sltc], start=True, stop=True),
                    waits=use_bank(p, 0, prod_deps + upk_toks + prev_attn))
                te = em.op("vector", lambda e, sltc=sltc: e.tensor_tensor(
                    Ebuf[:, :], bank(p, 0, 128, 512), ropec[:, sltc],
                    op=ALU.mult), waits=[m1, rc])
                read_bank(p, 0, te)
                m2 = em.op("tensor", lambda e: e.matmul(
                    bank(p, 1, 64, 512), p.cb[:, FOLD_O:FOLD_O + 64],
                    Ebuf[:, :], start=True, stop=True),
                    waits=use_bank(p, 1, [te]))
                ev = em.op("vector", lambda e, dst=dst, sltc=sltc: e.tensor_copy(
                    dst[0:64, sltc], bank(p, 1, 64, 512)), waits=[m2])
                read_bank(p, 1, ev)
                bld.append(ev)
        ot, _ = attn_head(p, h, Qaug, Kaug, vtm_all[:, h], ypair,
                          bld + [ko] + ypair_free + vc_toks[-1:])
        prev_attn = [ot]
        if h % 2 == 1:
            yw = em.dma("gpsimd", lambda e, h=h: e.dma_start(
                out=dram["y_scr"][64 * (h - 1):64 * (h - 1) + 128, :],
                in_=ypair[:, :]), waits=[ot] + youts[-1:])
            youts.append(yw)
            ypair_free = [yw]
    ar.release()
    ar.mark()
    wc_all = ar.alloc("wc_all", [128, NC8, NC8, 128], BF16)
    ystage = [ar.alloc(f"yst{i}", [128, NC8, 512], BF16) for i in range(2)]
    wcl = em.dma("gpsimd", lambda e: e.dma_start(
        out=wc_all[:], in_=dram["wc"][:, :]
        .rearrange("(c p) (d j) -> p c d j", p=128, j=128)),
        waits=youts[-1:] + prev_attn)
    wout_toks = {tc: [] for tc in range(4)}
    yst_free = [[], []]
    res_ld = []
    scr2_free = {}
    scrh_free = {}
    for tc in range(4):
        sltc = slice(512 * tc, 512 * tc + 512)
        yl = em.dma("gpsimd", lambda e, tc=tc, sltc=sltc: e.dma_start(
            out=ystage[tc % 2][:], in_=dram["y_scr"][:, sltc]
            .rearrange("(c p) t -> p c t", p=128)),
            waits=yst_free[tc % 2] + youts[-1:])
        for oc in range(NC8):
            bk = oc
            mt = None
            for d in range(NC8):
                mt = em.op("tensor", lambda e, oc=oc, d=d, tc=tc, bk=bk: e.matmul(
                    bank(p, bk, 128, 512), wc_all[:, d, oc, :],
                    ystage[tc % 2][:, d, :], start=(d == 0), stop=(d == NC8 - 1),
                    skip_group_check=True),
                    waits=use_bank(p, bk, [yl, wcl]) if d == 0 else [yl])
            sh = 512 * (oc % 2)
            rx = em.dma("sync", lambda e, oc=oc, sltc=sltc, xh=p.x_hbm: e.dma_start(
                out=p.scr2[:, 512 * (oc % 2):512 * (oc % 2) + 512],
                in_=xh[128 * oc:128 * oc + 128, sltc]),
                waits=res_ld[-1:] + scr2_free.get(oc % 2, []))
            res_ld.append(rx)
            o1 = em.op("vector", lambda e, bk=bk, sh=sh, oc=oc: e.tensor_tensor(
                p.scr[:, sh:sh + 512], bank(p, bk, 128, 512),
                p.scr2[:, 512 * (oc % 2):512 * (oc % 2) + 512], op=ALU.add),
                waits=[mt, rx] + scrh_free.get(oc % 2, []))
            read_bank(p, bk, o1)
            scr2_free[oc % 2] = [o1]
            wd = em.dma("scalar", lambda e, oc=oc, sltc=sltc, sh=sh: e.dma_start(
                out=dram["spill_x2"][128 * oc:128 * oc + 128, sltc],
                in_=p.scr[:, sh:sh + 512]), waits=[o1])
            scrh_free[oc % 2] = [wd]
            wout_toks[tc].append(wd)
        yst_free[tc % 2] = wout_toks[tc][-NC8:]
    ar.release()
    return postln_stream(p, wout_toks, is_last, dram["spill_x"])


def emit_l1_ffn(p, entry, is_last=True):
    em = p.em
    ar = p.ar
    dram = p.dram
    BIG = 1.0e6
    ar.mark()
    yacc = ar.alloc("yacc", [128, NC8, T], F32)
    Wte = ar.alloc("Wte", [128, NT, 32], F32)

    ar.mark()
    xr = ar.alloc("xr", [128, T], F32)
    sfm = ar.alloc("sfm", [32, T], F32)
    sfcT = ar.alloc("sfcT", [128, NT, 32], F32)
    scT = ar.alloc("scT", [128, NT, 32], F32)
    tmpv = ar.alloc("tmpv", [128, NT, 32], F32)
    iot = ar.alloc("iot", [128, NT, 32], F32)
    eqb = ar.alloc("eqb", [128, NT, 32], F32)
    fms = [ar.alloc(f"fm{k}", [128, NT, 32], F32) for k in range(4)]
    g8 = ar.alloc("g8", [128, NT, 8], F32)
    g4 = [ar.alloc(f"g4_{i}", [128, NT, 4], F32) for i in range(5)]
    r16 = ar.alloc("r16", [128, NT, 8], F32)
    rwt = ar.alloc("rwt", [128, NC8, 32], F32)

    rwl = em.dma("sync", lambda e: e.dma_start(
        out=rwt[:], in_=dram["rw_t"][:, :].rearrange("(c p) e -> p c e", p=128)),
        waits=entry)
    lg_mm = {}
    ld_prev = []
    for c in range(NC8):
        ld = em.dma("sync", lambda e, c=c, xh=p.x_hbm: e.dma_start(
            out=xr[:, :], in_=xh[128 * c:128 * c + 128, :]),
            waits=ld_prev + entry)
        for tc in range(4):
            sltc = slice(512 * tc, 512 * tc + 512)
            lg_mm[tc] = em.op("tensor", lambda e, c=c, tc=tc, sltc=sltc: e.matmul(
                bank(p, tc, 32, 512), rwt[:, c, :], xr[:, sltc],
                start=(c == 0), stop=(c == NC8 - 1), skip_group_check=True),
                waits=use_bank(p, tc, [ld, rwl]) if c == 0 else [ld, rwl])
        ld_prev = [lg_mm[3]]
    sg_toks = []
    for tc in range(4):
        t = em.op("scalar", lambda e, tc=tc: e.activation(
            sfm[0:32, 512 * tc:512 * tc + 512], bank(p, tc, 32, 512),
            AF.Sigmoid), waits=[lg_mm[tc]])
        read_bank(p, tc, t)
        sg_toks.append(t)
    bb1 = em.op("tensor", lambda e: e.matmul(
        bank(p, 5, 128, 32), ones(p, 1, 128),
        p.pv["r_bias"][0:1, 0:32], start=True,
        stop=True), waits=use_bank(p, 5, p.ctok))
    tr_toks = []
    for i in range(NT):
        t1 = em.op("tensor", lambda e, i=i: e.transpose(
            bank(p, 4, 128, 32), sfm[0:32, 128 * i:128 * i + 128],
            ident(p)[0:32, 0:32]), waits=use_bank(p, 4, sg_toks + p.ctok))
        e1 = em.op("vector", lambda e, i=i: e.tensor_copy(
            scT[:, i, :], bank(p, 4, 128, 32)), waits=[t1])
        read_bank(p, 4, e1)
        e2 = em.op("vector", lambda e, i=i: e.tensor_tensor(
            sfcT[:, i, :], scT[:, i, :], bank(p, 5, 128, 32), op=ALU.add),
            waits=[e1, bb1])
        tr_toks.append(e2)
    read_bank(p, 5, tr_toks[-1])
    gm, cnt4, m24, gsum, rank4 = g4
    gt_toks = []
    for g in range(4):
        sl3 = (slice(None), slice(None), slice(8 * g, 8 * g + 8))
        m1 = em.op("vector", lambda e, g=g, sl3=sl3: e.reduce_max(
            gm[:, :, g:g + 1], sfcT[sl3], axis=AX.X), waits=tr_toks)
        eq = em.op("vector", lambda e, g=g, sl3=sl3: e.tensor_tensor(
            g8[:, :, :], sfcT[sl3],
            gm[:, :, g:g + 1].to_broadcast([128, NT, 8]), op=ALU.is_equal),
            waits=[m1] + gt_toks[-1:])
        ct = em.op("vector", lambda e, g=g: e.reduce_sum(
            cnt4[:, :, g:g + 1], g8[:, :, :], axis=AX.X), waits=[eq])
        mk = em.op("vector", lambda e, g=g, sl3=sl3: e.scalar_tensor_tensor(
            r16[:, :, :], g8[:, :, :], -BIG, sfcT[sl3], op0=ALU.mult,
            op1=ALU.add), waits=[eq])
        m2 = em.op("vector", lambda e, g=g: e.reduce_max(
            m24[:, :, g:g + 1], r16[:, :, :], axis=AX.X), waits=[mk])
        gt_toks += [ct, m2]
    s2 = em.op("vector", lambda e: e.tensor_scalar(
        cnt4[:, :, :], cnt4[:, :, :], 1.5, None, op0=ALU.is_gt), waits=gt_toks)
    d1 = em.op("vector", lambda e: e.tensor_tensor(
        rank4[:, :, :], gm[:, :, :], m24[:, :, :], op=ALU.subtract), waits=[s2])
    d2 = em.op("vector", lambda e: e.tensor_tensor(
        rank4[:, :, :], rank4[:, :, :], cnt4[:, :, :], op=ALU.mult), waits=[d1])
    d3 = em.op("vector", lambda e: e.tensor_tensor(
        gsum[:, :, :], gm[:, :, :], m24[:, :, :], op=ALU.add), waits=[d2])
    d4 = em.op("vector", lambda e: e.tensor_tensor(
        gsum[:, :, :], gsum[:, :, :], rank4[:, :, :], op=ALU.add), waits=[d3])
    z0 = em.op("vector", lambda e: e.memset(rank4[:, :, :], 0.0), waits=[d4])
    rk_toks = [z0]
    for g in range(4):
        for j in range(4):
            if j == g:
                continue
            op = ALU.is_ge if j < g else ALU.is_gt
            c1 = em.op("vector", lambda e, g=g, j=j, op=op: e.tensor_tensor(
                g8[:, :, 0:1], gsum[:, :, j:j + 1], gsum[:, :, g:g + 1], op=op),
                waits=rk_toks[-1:])
            c2 = em.op("vector", lambda e, g=g: e.tensor_tensor(
                rank4[:, :, g:g + 1], rank4[:, :, g:g + 1], g8[:, :, 0:1],
                op=ALU.add), waits=[c1])
            rk_toks.append(c2)
    sel = em.op("vector", lambda e: e.tensor_scalar(
        rank4[:, :, :], rank4[:, :, :], 1.5, None, op0=ALU.is_lt),
        waits=rk_toks)
    tv_toks = []
    for g in range(4):
        sl3 = (slice(None), slice(None), slice(8 * g, 8 * g + 8))
        t = em.op("vector", lambda e, g=g, sl3=sl3: e.tensor_tensor(
            tmpv[sl3], sfcT[sl3],
            rank4[:, :, g:g + 1].to_broadcast([128, NT, 8]), op=ALU.mult),
            waits=[sel])
        tv_toks.append(t)
    io_toks = []
    for ei in range(32):
        io_toks.append(em.op("vector", lambda e, ei=ei: e.memset(
            iot[:, :, ei:ei + 1], float(ei)), waits=entry))
    wtk = ar.alloc("wtk", [128, NT, 8], F32)
    prev = tv_toks + io_toks
    for k in range(4):
        mk = em.op("vector", lambda e: e.reduce_max(
            wtk[:, :, 4:5], tmpv[:, :, :], axis=AX.X), waits=prev)
        eq = em.op("vector", lambda e: e.tensor_tensor(
            eqb[:, :, :], tmpv[:, :, :],
            wtk[:, :, 4:5].to_broadcast([128, NT, 32]), op=ALU.is_equal),
            waits=[mk])
        mi = em.op("vector", lambda e, k=k: e.scalar_tensor_tensor(
            fms[k][:, :, :], eqb[:, :, :], -BIG, iot[:, :, :], op0=ALU.mult,
            op1=ALU.add), waits=[eq])
        rm = em.op("vector", lambda e, k=k: e.tensor_reduce(
            wtk[:, :, 5:6], fms[k][:, :, :], axis=AX.X, op=ALU.min), waits=[mi])
        ra = em.op("vector", lambda e: e.tensor_scalar(
            wtk[:, :, 5:6], wtk[:, :, 5:6], BIG, None, op0=ALU.add), waits=[rm])
        fm = em.op("vector", lambda e, k=k: e.tensor_tensor(
            fms[k][:, :, :], iot[:, :, :],
            wtk[:, :, 5:6].to_broadcast([128, NT, 32]), op=ALU.is_equal),
            waits=[ra])
        wv = em.op("vector", lambda e, k=k: e.tensor_tensor(
            eqb[:, :, :], fms[k][:, :, :], scT[:, :, :], op=ALU.mult),
            waits=[fm])
        ws = em.op("vector", lambda e, k=k: e.reduce_sum(
            wtk[:, :, k:k + 1], eqb[:, :, :], axis=AX.X), waits=[wv])
        rmv = em.op("vector", lambda e, k=k: e.scalar_tensor_tensor(
            tmpv[:, :, :], fms[k][:, :, :], -BIG, tmpv[:, :, :], op0=ALU.mult,
            op1=ALU.add), waits=[ws])
        prev = [rmv]
    n1 = em.op("vector", lambda e: e.tensor_tensor(
        wtk[:, :, 6:7], wtk[:, :, 0:1], wtk[:, :, 1:2], op=ALU.add), waits=prev)
    n2 = em.op("vector", lambda e: e.tensor_tensor(
        wtk[:, :, 7:8], wtk[:, :, 2:3], wtk[:, :, 3:4], op=ALU.add), waits=[n1])
    n3 = em.op("vector", lambda e: e.tensor_tensor(
        wtk[:, :, 6:7], wtk[:, :, 6:7], wtk[:, :, 7:8], op=ALU.add), waits=[n2])
    n4 = em.op("vector", lambda e: e.tensor_scalar(
        wtk[:, :, 6:7], wtk[:, :, 6:7], 1e-20, None, op0=ALU.add), waits=[n3])
    n5 = em.op("scalar", lambda e: e.activation(
        wtk[:, :, 7:8], wtk[:, :, 6:7], AF.Ln), waits=[n4])
    n6 = em.op("scalar", lambda e: e.activation(
        wtk[:, :, 7:8], wtk[:, :, 7:8], AF.Exp, scale=-1.0,
        bias=p.cf[:, LN25_O:LN25_O + 1]), waits=[n5])
    wt_toks = []
    for k in range(4):
        t1 = em.op("vector", lambda e, k=k: e.tensor_tensor(
            wtk[:, :, 4:5], wtk[:, :, k:k + 1], wtk[:, :, 7:8], op=ALU.mult),
            waits=[n6] + wt_toks[-1:])
        t2 = em.op("vector", lambda e, k=k: e.tensor_tensor(
            eqb[:, :, :], fms[k][:, :, :],
            wtk[:, :, 4:5].to_broadcast([128, NT, 32]), op=ALU.mult),
            waits=[t1])
        if k == 0:
            t3 = em.op("vector", lambda e: e.tensor_copy(
                Wte[:, :, :], eqb[:, :, :]), waits=[t2])
        else:
            t3 = em.op("vector", lambda e: e.tensor_tensor(
                Wte[:, :, :], Wte[:, :, :], eqb[:, :, :], op=ALU.add),
                waits=[t2])
        wt_toks.append(t3)
    route_done = wt_toks[-1:]
    ar.release()

    ar.mark()
    gsl = [ar.alloc(f"gsl{i}", [128, NC8, 512], BF16) for i in range(2)]
    usl = [ar.alloc(f"usl{i}", [128, NC8, 512], BF16) for i in range(2)]
    dsl = [ar.alloc(f"dsl{i}", [128, 4, 1024], BF16) for i in range(2)]
    hb = ar.alloc("hb", [128, 4, T], BF16)
    wbc = ar.alloc("wbc", [128, T], F32)
    NE = 34
    slot_free = [route_done, route_done]
    hb_free = []
    wbc_free = []
    yacc_toks = []
    for e_ in range(NE):
        shared = e_ >= 32
        shi = e_ - 32
        s = e_ % 2
        if not shared:
            gsrc = dram["moe_gate"][e_].rearrange("(c p) i -> p c i", p=128)
            usrc = dram["moe_up"][e_].rearrange("(c p) i -> p c i", p=128)
            dsrc = dram["moe_down"][e_].rearrange("(ic p) o -> p ic o", p=128)
        else:
            gsrc = dram["sh_gate"][:, 512 * shi:512 * shi + 512] \
                .rearrange("(c p) i -> p c i", p=128)
            usrc = dram["sh_up"][:, 512 * shi:512 * shi + 512] \
                .rearrange("(c p) i -> p c i", p=128)
            dsrc = dram["sh_down"][512 * shi:512 * shi + 512, :] \
                .rearrange("(ic p) o -> p ic o", p=128)
        gt = em.dma("gpsimd", lambda e, s=s, gsrc=gsrc: e.dma_start(
            out=gsl[s][:], in_=gsrc), waits=slot_free[s])
        ut = em.dma("gpsimd", lambda e, s=s, usrc=usrc: e.dma_start(
            out=usl[s][:], in_=usrc), waits=slot_free[s])
        dt = em.dma("gpsimd", lambda e, s=s, dsrc=dsrc: e.dma_start(
            out=dsl[s][:], in_=dsrc), waits=slot_free[s])
        wb_toks = []
        if not shared:
            bts = []
            for i in range(NT):
                bb = 4 + i // 4
                dg = em.op("vector", lambda e, i=i, e_=e_: e.tensor_scalar(
                    p.msks[:, :], ident(p), Wte[:, i, e_:e_ + 1],
                    None, op0=ALU.mult), waits=route_done + bts[-1:])
                bm = em.op("tensor", lambda e, i=i, bb=bb: e.matmul(
                    bank(p, bb, 128, 128, coloff=128 * (i % 4)),
                    ones(p, 128, 128), p.msks[:, :], start=(i % 4 == 0),
                    stop=(i % 4 == 3), skip_group_check=True),
                    waits=use_bank(p, bb, [dg]) if i % 4 == 0 else [dg])
                bts.append(bm)
            for b in range(4):
                t = em.op("vector", lambda e, b=b: e.tensor_copy(
                    wbc[:, 512 * b:512 * b + 512], bank(p, 4 + b, 128, 512)),
                    waits=bts + wbc_free)
                read_bank(p, 4 + b, t)
                wb_toks.append(t)
        hb_toks = []
        t1_prev = getattr(p, "_moe_t1_prev", {})
        for tc in range(4):
            sltc = slice(512 * tc, 512 * tc + 512)
            for ic in range(4):
                ba = ic % 2
                bu = 2 + ic % 2
                mg = None
                mu = None
                for c in range(NC8):
                    mg = em.op("tensor", lambda e, s=s, ic=ic, c=c, sltc=sltc,
                               ba=ba: e.matmul(
                        bank(p, ba, 128, 512),
                        gsl[s][:, c, 128 * ic:128 * ic + 128],
                        p.x_bf[:, c, sltc], start=(c == 0), stop=(c == NC8 - 1),
                        skip_group_check=True),
                        waits=use_bank(p, ba, [gt]) if c == 0 else [gt])
                for c in range(NC8):
                    mu = em.op("tensor", lambda e, s=s, ic=ic, c=c, sltc=sltc,
                               bu=bu: e.matmul(
                        bank(p, bu, 128, 512),
                        usl[s][:, c, 128 * ic:128 * ic + 128],
                        p.x_bf[:, c, sltc], start=(c == 0), stop=(c == NC8 - 1),
                        skip_group_check=True),
                        waits=use_bank(p, bu, [ut]) if c == 0 else [ut])
                sh0 = 512 * (ic % 2)
                sv = em.op("scalar", lambda e, ba=ba, sh0=sh0: e.activation(
                    p.scr[:, sh0:sh0 + 512], bank(p, ba, 128, 512), AF.Silu),
                    waits=[mg] + t1_prev.get(ic % 2, []))
                read_bank(p, ba, sv)
                t1 = em.op("vector", lambda e, bu=bu, sh0=sh0: e.tensor_tensor(
                    p.scr[:, 1024 + sh0:1024 + sh0 + 512],
                    p.scr[:, sh0:sh0 + 512], bank(p, bu, 128, 512),
                    op=ALU.mult), waits=[sv, mu])
                read_bank(p, bu, t1)
                t1_prev[ic % 2] = [t1]
                if shared:
                    t2 = em.op("vector", lambda e, ic=ic, sltc=sltc, sh0=sh0:
                               e.tensor_copy(
                        hb[:, ic, sltc], p.scr[:, 1024 + sh0:1024 + sh0 + 512]),
                        waits=[t1] + hb_free)
                else:
                    t2 = em.op("vector", lambda e, ic=ic, sltc=sltc, sh0=sh0:
                               e.tensor_tensor(
                        hb[:, ic, sltc], p.scr[:, 1024 + sh0:1024 + sh0 + 512],
                        wbc[:, sltc], op=ALU.mult),
                        waits=[t1] + wb_toks + hb_free)
                hb_toks.append(t2)
        dn_toks = []
        for tc in range(4):
            sltc = slice(512 * tc, 512 * tc + 512)
            for oc in range(NC8):
                bk = 4 + (4 * tc + oc) % 4
                md = None
                for ic in range(4):
                    md = em.op("tensor", lambda e, s=s, ic=ic, oc=oc, sltc=sltc,
                               bk=bk: e.matmul(
                        bank(p, bk, 128, 512),
                        dsl[s][:, ic, 128 * oc:128 * oc + 128],
                        hb[:, ic, sltc], start=(ic == 0), stop=(ic == 3),
                        skip_group_check=True),
                        waits=use_bank(p, bk, [dt] + hb_toks[4 * tc:4 * tc + 4])
                        if ic == 0 else [dt])
                if e_ == 0:
                    ya = em.op("vector", lambda e, oc=oc, sltc=sltc, bk=bk:
                               e.tensor_copy(
                        yacc[:, oc, sltc], bank(p, bk, 128, 512)), waits=[md])
                else:
                    ya = em.op("vector", lambda e, oc=oc, sltc=sltc, bk=bk:
                               e.tensor_tensor(
                        yacc[:, oc, sltc], yacc[:, oc, sltc],
                        bank(p, bk, 128, 512), op=ALU.add),
                        waits=[md] + yacc_toks[-1:])
                read_bank(p, bk, ya)
                dn_toks.append(ya)
        yacc_toks = dn_toks
        slot_free[s] = dn_toks[-4:]
        hb_free = dn_toks[-4:]
        wbc_free = hb_toks[-1:]
        p._moe_t1_prev = t1_prev
    wout_toks = {tc: [] for tc in range(4)}
    res_ld = []
    scr2_free = {}
    scrh_free = {}
    for tc in range(4):
        sltc = slice(512 * tc, 512 * tc + 512)
        for oc in range(NC8):
            sh0 = 512 * (oc % 2)
            rx = em.dma("sync", lambda e, oc=oc, sltc=sltc, xh=p.x_hbm:
                        e.dma_start(
                out=p.scr2[:, 512 * (oc % 2):512 * (oc % 2) + 512],
                in_=xh[128 * oc:128 * oc + 128, sltc]),
                waits=res_ld[-1:] + scr2_free.get(oc % 2, []))
            res_ld.append(rx)
            o1 = em.op("vector", lambda e, oc=oc, sltc=sltc, sh0=sh0:
                       e.tensor_tensor(
                p.scr[:, sh0:sh0 + 512], yacc[:, oc, sltc],
                p.scr2[:, 512 * (oc % 2):512 * (oc % 2) + 512], op=ALU.add),
                waits=[rx] + yacc_toks[-1:] + scrh_free.get(oc % 2, []))
            scr2_free[oc % 2] = [o1]
            wd = em.dma("scalar", lambda e, oc=oc, sltc=sltc, sh0=sh0:
                        e.dma_start(
                out=dram["spill_x2"][128 * oc:128 * oc + 128, sltc],
                in_=p.scr[:, sh0:sh0 + 512]), waits=[o1])
            scrh_free[oc % 2] = [wd]
            wout_toks[tc].append(wd)
    ar.release()
    return postln_stream(p, wout_toks, is_last, dram["spill_x"])


def build(stage="l0attn", dbg=None, tap=None):
    p = mkprog(stage, dbg)
    p.tap = tap
    p.tap_done = False
    p.tap_guard = []
    em = p.em
    if stage == "l0attn":
        emit_l0_attn(p, is_last=True)
    elif stage == "l0ffn":
        t = emit_l0_attn(p, is_last=False)
        emit_l0_ffn(p, t, is_last=True)
    elif stage == "l1attn":
        t = emit_l0_attn(p, is_last=False)
        t = emit_l0_ffn(p, t, is_last=False)
        emit_l1_attn(p, t, is_last=True)
    elif stage in ("l1ffn", "full"):
        t = emit_l0_attn(p, is_last=False)
        t = emit_l0_ffn(p, t, is_last=False)
        t = emit_l1_attn(p, t, is_last=False)
        emit_l1_ffn(p, t, is_last=True)
    else:
        raise NotImplementedError(stage)

    def fin(e):
        for s, c in em.all_tok():
            e.wait_ge(em.sems[s], c)
        e.nop()
    em.build(p.block)
    p.block.sync(lambda e: fin(e))
    p.stack.close()
    globals()["_last_p"] = p
    return p.nc


def kernel(**inputs):
    nc = build(stage="full")
    in_maps = [host_inputs(core, inputs) for core in range(NCORES)]
    res = run_bass_kernel_spmd(nc, in_maps, core_ids=list(range(NCORES)))
    out = np.stack([r["y"].T for r in res.results], axis=0)
    return np.ascontiguousarray(out)
